# version 01
# baseline (speedup 1.0000x reference)
import numpy as np
import ml_dtypes
from contextlib import ExitStack

import concourse.bass as bass
import concourse.mybir as mybir
from concourse.bass_utils import run_bass_kernel_spmd
from concourse.bass import IndirectOffsetOnAxis

F32 = mybir.dt.float32
BF16 = mybir.dt.bfloat16
I32 = mybir.dt.int32
AF = mybir.ActivationFunctionType
ALU = mybir.AluOpType
AX = mybir.AxisListType
BF = ml_dtypes.bfloat16

NCORES = 8
B = 2
L = 16384
D = 1024
SEG = L // 4
DIN = 2048
DSTATE = 16
DTRANK = 64

def f32c(x):
    return np.ascontiguousarray(np.asarray(x), dtype=np.float32)

def bf16c(x):
    return np.ascontiguousarray(np.asarray(x).astype(BF))


class Em:
    ENGINES = ("tensor", "vector", "scalar", "gpsimd", "sync")

    def __init__(self, nc, es):
        self.nc = nc
        self.es = es
        self.q = {e: [] for e in self.ENGINES}
        self.cnt = {}
        self.sems = {}
        self.writer = {}
        self.readers = {}

    def _sem(self, key):
        if key not in self.sems:
            self.sems[key] = self.es.enter_context(self.nc.semaphore(f"s_{key}"))
            self.cnt[key] = 0
        return self.sems[key]

    def op(self, engine, fn, reads=(), writes=(), deps=(), dma=False, tag="", semkey=None):
        key = semkey if semkey is not None else engine
        self._sem(key)
        inc = 16 if dma else 1
        waits = [d for d in deps if d is not None]
        for r in reads:
            if r in self.writer:
                waits.append(self.writer[r])
        for w in writes:
            if w in self.writer:
                waits.append(self.writer[w])
            for t in self.readers.get(w, ()):
                waits.append(t)
        self.cnt[key] += inc
        tok = (key, self.cnt[key])
        for r in reads:
            self.readers.setdefault(r, []).append(tok)
        for w in writes:
            self.writer[w] = tok
            self.readers[w] = []
        wm = {}
        for (k, v) in waits:
            wm[k] = max(wm.get(k, 0), v)
        self.q[engine].append((sorted(wm.items()), fn, key, inc, tag))
        return tok

    def dma(self, engine, out, in_, reads=(), writes=(), deps=(), tag="", sem=None, **kw):
        if sem is None:
            sem = "d:" + (writes[0] if writes else (reads[0] if reads else engine))
        return self.op(engine, lambda e: e.dma_start(out=out, in_=in_, **kw),
                       reads=reads, writes=writes, deps=deps, dma=True, tag=tag,
                       semkey=sem)

    def coalesce(self, names, semkey):
        tok = (semkey, self.cnt[semkey])
        for n in names:
            self.writer[n] = tok

    def build(self, block):
        for e in self.ENGINES:
            if not self.q[e]:
                continue

            def make(e):
                def body(eng):
                    done = {}
                    for waits, fn, key, inc, tag in self.q[e]:
                        for (dk, val) in waits:
                            if done.get(dk, 0) >= val:
                                continue
                            eng.wait_ge(self._sem(dk), val)
                            done[dk] = val
                        fn(eng).then_inc(self.sems[key], inc)
                return body
            getattr(block, e)(make(e))

    def finish(self, block):
        toks = [(k, self.cnt[k]) for k in self.cnt]
        self.op("sync", lambda e: e.nop(), deps=toks, tag="final")
        self.build(block)


def rr(ap, p=128):
    return ap.rearrange("(a p) t -> a p t", p=p)


def build_l0a(nch=None, level=9):
    nc = bass.Bass("TRN2", target_bir_lowering=False)
    TC = 512
    NCH = nch if nch is not None else SEG // TC
    H = 3

    x_in = nc.dram_tensor("x", [H + SEG, D], F32, kind="ExternalInput").ap()
    w_in = nc.dram_tensor("w_in", [D, 2 * DIN], F32, kind="ExternalInput").ap()
    conv_w = nc.dram_tensor("conv_w", [128, 16, 4], F32, kind="ExternalInput").ap()
    conv_b = nc.dram_tensor("conv_b", [128, 16, 1], F32, kind="ExternalInput").ap()
    w_x = nc.dram_tensor("w_x", [DIN, 96], F32, kind="ExternalInput").ap()
    w_dt = nc.dram_tensor("w_dt", [DTRANK, DIN], F32, kind="ExternalInput").ap()
    b_dt = nc.dram_tensor("b_dt", [128, 16, 1], F32, kind="ExternalInput").ap()
    xhalo = nc.dram_tensor("xhalo", [128, 8, 3], F32, kind="ExternalInput").ap()
    ident_in = nc.dram_tensor("ident", [128, 128], BF16, kind="ExternalInput").ap()

    xs_out = nc.dram_tensor("xs", [DIN, SEG], BF16, kind="ExternalOutput").ap()
    dt_out = nc.dram_tensor("dt", [DIN, SEG], BF16, kind="ExternalOutput").ap()
    u_out = nc.dram_tensor("u", [DIN, SEG], BF16, kind="ExternalOutput").ap()
    ra_out = nc.dram_tensor("res_act", [DIN, SEG], BF16, kind="ExternalOutput").ap()
    bc_out = nc.dram_tensor("bc", [32, SEG], F32, kind="ExternalOutput").ap()

    es = ExitStack()
    T = lambda *a: es.enter_context(nc.sbuf_tensor(*a))
    P = lambda *a: es.enter_context(nc.psum_tensor(*a))

    w_sb = T("w_sb", [128, 8, 2 * DIN], BF16)
    wx_sb = T("wx_sb", [128, 16, 96], BF16)
    wdt_sb = T("wdt_sb", [DTRANK, DIN], BF16)
    cw_sb = T("cw_sb", [128, 16, 4], F32)
    cb_sb = T("cb_sb", [128, 16, 1], F32)
    bdt_sb = T("bdt_sb", [128, 16, 1], F32)
    id_sb = T("id_sb", [128, 128], BF16)

    xtok = T("xtok", [128, 4, D], F32)
    xtokb = T("xtokb", [128, 4, D], BF16)
    xT = [T(f"xT{i}", [128, 8, TC], BF16) for i in range(2)]
    xhT = T("xhT", [128, 8, H], BF16)
    xc_ext = T("xc_ext", [128, 16, H + TC], BF16)
    xs_sb = T("xs_sb", [128, 16, TC], BF16)
    dt_sb = T("dt_sb", [128, 16, TC], BF16)
    res_bf = T("res_bf", [128, 2, TC], BF16)
    xdbl_sb = T("xdbl_sb", [96, TC], F32)
    dtr_bf = T("dtr_bf", [DTRANK, TC], BF16)
    sp_t1 = T("sp_t1", [128, TC], F32)
    sp_t2 = T("sp_t2", [128, TC], F32)
    u_sb = T("u_sb", [128, 2, TC], BF16)

    ps_tr = [P(f"ps_tr{i}", [128, 128], BF16) for i in range(2)]
    ps_mm = [P(f"ps_mm{i}", [128, TC], F32) for i in range(2)]
    ps_dbl = P("ps_dbl", [96, TC], F32)
    ps_dt = [P(f"ps_dt{i}", [128, TC], F32) for i in range(2)]

    em = Em(nc, es)
    block = es.enter_context(nc.Block())
    with es:
        em.dma("gpsimd", w_sb[:], w_in.rearrange("(k p) n -> p k n", p=128), writes=["w_sb"])
        em.dma("gpsimd", wx_sb[:], w_x.rearrange("(k p) n -> p k n", p=128), writes=["wx_sb"])
        em.dma("gpsimd", wdt_sb[:], w_dt, writes=["wdt_sb"])
        em.dma("sync", cw_sb[:], conv_w, writes=["cw_sb"])
        em.dma("sync", cb_sb[:], conv_b, writes=["cb_sb"])
        em.dma("sync", bdt_sb[:], b_dt, writes=["bdt_sb"])
        em.dma("sync", id_sb[:], ident_in, writes=["id_sb"])
        if level >= 1.2:
            for kc in range(8):
                em.dma("gpsimd", xhT[:, kc, :], xhalo[:, kc, :], writes=[f"xhT{kc}"])

        for oc in range(16 if level >= 2 else 0):
            for kc in range(8):
                em.op("tensor",
                      lambda e, oc=oc, kc=kc: e.matmul(
                          ps_dt[0][:, 0:H], w_sb[:, kc, oc * 128:(oc + 1) * 128], xhT[:, kc, :],
                          start=(kc == 0), stop=(kc == 7)),
                      reads=["w_sb", f"xhT{kc}"], writes=(["ps_dt0"] if kc in (0, 7) else []),
                      tag=f"hmm{oc}")
            em.op("scalar",
                  lambda e, oc=oc: e.copy(out=xc_ext[:, oc, 0:H], in_=ps_dt[0][:, 0:H]),
                  reads=["ps_dt0"], writes=[f"ext_halo{oc}"], tag=f"hev{oc}")

        for c in range(NCH if level >= 1.4 else 0):
            buf = c % 2
            xTn = f"xT{buf}"
            evs = []
            for tt in range(4):
                row0 = H + c * TC + tt * 128
                ld = em.dma("sync", xtok[:, tt, :], x_in[row0:row0 + 128, :],
                            writes=[f"xtok{tt}"], tag=f"xld{c}_{tt}")
                cv = em.op("vector", lambda e, tt=tt: e.tensor_copy(out=xtokb[:, tt, :], in_=xtok[:, tt, :]),
                           reads=[f"xtok{tt}"], writes=[f"xtokb{tt}"], tag="xcv")
                for dc in range(8 if level >= 1.6 else 0):
                    sl = (tt * 8 + dc) % 2
                    tr = em.op("tensor",
                               lambda e, dc=dc, tt=tt, sl=sl: e.transpose(
                                   ps_tr[sl][:, :], xtokb[:, tt, dc * 128:(dc + 1) * 128], id_sb[:]),
                               reads=[f"xtokb{tt}", "id_sb"], writes=[f"ps_tr{sl}"], tag="xtr")
                    ev = em.op("scalar",
                               lambda e, dc=dc, tt=tt, b=buf, sl=sl: e.copy(
                                   out=xT[b][:, dc, tt * 128:(tt + 1) * 128], in_=ps_tr[sl][:, :]),
                               reads=[f"ps_tr{sl}"], writes=[xTn + f"_{dc}_{tt}"], tag="xev")
                    evs.append(ev)

            xT_all = [xTn + f"_{dc}_{tt}" for dc in range(8) for tt in range(4)]
            for oc in range(32 if level >= 3 else 0):
                pb = oc % 2
                if oc < 16 and level < 4:
                    continue
                if oc >= 16 and level < 5:
                    continue
                for kc in range(8):
                    em.op("tensor",
                          lambda e, oc=oc, kc=kc, b=buf, pb=pb: e.matmul(
                              ps_mm[pb][:, :], w_sb[:, kc, oc * 128:(oc + 1) * 128], xT[b][:, kc, :],
                              start=(kc == 0), stop=(kc == 7)),
                          reads=["w_sb"] + [xTn + f"_{kc}_{tt}" for tt in range(4)],
                          writes=([f"ps_mm{pb}"] if kc in (0, 7) else []), tag=f"mm{c}_{oc}")
                if oc < 16:
                    em.op("vector",
                          lambda e, oc=oc, pb=pb: e.tensor_copy(out=xc_ext[:, oc, H:], in_=ps_mm[pb][:, :]),
                          reads=[f"ps_mm{pb}", f"ext_tail{oc}"], writes=[f"ext{oc}"], tag=f"xcev{oc}")
                    t1 = em.op("vector",
                               lambda e, oc=oc: e.tensor_scalar(
                                   out=sp_t1[:], in0=xc_ext[:, oc, 0:TC],
                                   scalar1=cw_sb[:, oc, 0:1],
                                   scalar2=cb_sb[:, oc, 0:1],
                                   op0=ALU.mult, op1=ALU.add),
                               reads=[f"ext{oc}", f"ext_halo{oc}", "cw_sb", "cb_sb"],
                               writes=["sp_t1"], tag="cv0")
                    for j in range(1, 4):
                        t1 = em.op("vector",
                                   lambda e, oc=oc, j=j: e.scalar_tensor_tensor(
                                       out=sp_t1[:], in0=xc_ext[:, oc, j:j + TC],
                                       scalar=cw_sb[:, oc, j:j + 1],
                                       in1=sp_t1[:], op0=ALU.mult, op1=ALU.add),
                                   reads=[f"ext{oc}", "sp_t1"], writes=["sp_t1"], tag=f"cv{j}")
                    em.op("scalar",
                          lambda e, oc=oc: e.activation(out=xs_sb[:, oc, :], in_=sp_t1[:], func=AF.Silu),
                          reads=["sp_t1"], writes=[f"xs{oc}"], tag="sil")
                    em.op("vector",
                          lambda e, oc=oc: e.tensor_copy(out=xc_ext[:, oc, 0:H], in_=xc_ext[:, oc, TC:TC + H]),
                          reads=[f"ext{oc}"], writes=[f"ext_halo{oc}", f"ext_tail{oc}"], tag="tail")
                    em.dma("scalar", rr(xs_out)[oc, :, c * TC:(c + 1) * TC], xs_sb[:, oc, :],
                           reads=[f"xs{oc}"], tag="xs_st")
                    em.op("tensor",
                          lambda e, oc=oc: e.matmul(
                              ps_dbl[:, :], wx_sb[:, oc, :], xs_sb[:, oc, :],
                              start=(oc == 0), stop=(oc == 15)),
                          reads=["wx_sb", f"xs{oc}"], writes=(["ps_dbl"] if oc in (0, 15) else []), tag="dbl")
                else:
                    oc2 = oc - 16
                    rb = oc2 % 2
                    em.op("scalar",
                          lambda e, oc2=oc2, pb=pb, rb=rb: e.activation(
                              out=res_bf[:, rb, :], in_=ps_mm[pb][:, :], func=AF.Silu),
                          reads=[f"ps_mm{pb}"], writes=[f"res_bf{rb}"], tag="ressil")
                    em.dma("scalar", rr(ra_out)[oc2, :, c * TC:(c + 1) * TC], res_bf[:, rb, :],
                           reads=[f"res_bf{rb}"], tag="ra_st")

            em.op("scalar", lambda e: e.copy(out=xdbl_sb[:], in_=ps_dbl[:, :]),
                  reads=["ps_dbl"], writes=["xdbl_sb"], tag="dble")
            em.dma("sync", bc_out[:, c * TC:(c + 1) * TC], xdbl_sb[64:96, :],
                   reads=["xdbl_sb"], tag="bc_st")
            em.op("vector", lambda e: e.tensor_copy(out=dtr_bf[:], in_=xdbl_sb[0:64, :]),
                  reads=["xdbl_sb"], writes=["dtr_bf"], tag="dtrcv")
            for oc in range(16 if level >= 7 else 0):
                db = oc % 2
                em.op("tensor",
                      lambda e, oc=oc, db=db: e.matmul(
                          ps_dt[db][:, :], wdt_sb[:, oc * 128:(oc + 1) * 128], dtr_bf[:],
                          start=True, stop=True),
                      reads=["wdt_sb", "dtr_bf"], writes=[f"ps_dt{db}"], tag="dtmm")
                em.op("scalar",
                      lambda e, oc=oc, db=db: e.activation(out=sp_t1[:], in_=ps_dt[db][:, :], func=AF.Abs,
                                                           bias=bdt_sb[:, oc, 0:1]),
                      reads=[f"ps_dt{db}", "bdt_sb"], writes=["sp_t1"], tag="sp_abs")
                em.op("scalar",
                      lambda e: e.activation(out=sp_t1[:], in_=sp_t1[:], func=AF.Exp, scale=-1.0),
                      reads=["sp_t1"], writes=["sp_t1"], tag="sp_exp")
                em.op("scalar",
                      lambda e: e.activation(out=sp_t1[:], in_=sp_t1[:], func=AF.Ln, bias=1.0),
                      reads=["sp_t1"], writes=["sp_t1"], tag="sp_ln")
                em.op("scalar",
                      lambda e, oc=oc, db=db: e.activation(out=sp_t2[:], in_=ps_dt[db][:, :], func=AF.Relu,
                                                           bias=bdt_sb[:, oc, 0:1]),
                      reads=[f"ps_dt{db}", "bdt_sb"], writes=["sp_t2"], tag="sp_relu")
                em.op("vector",
                      lambda e, oc=oc: e.tensor_add(out=dt_sb[:, oc, :], in0=sp_t1[:], in1=sp_t2[:]),
                      reads=["sp_t1", "sp_t2"], writes=[f"dt{oc}"], tag="sp_add")
                em.dma("sync", rr(dt_out)[oc, :, c * TC:(c + 1) * TC], dt_sb[:, oc, :],
                       reads=[f"dt{oc}"], tag="dt_st")
                ub = oc % 2
                em.op("vector",
                      lambda e, oc=oc, ub=ub: e.tensor_mul(out=u_sb[:, ub, :], in0=dt_sb[:, oc, :], in1=xs_sb[:, oc, :]),
                      reads=[f"dt{oc}", f"xs{oc}"], writes=[f"u_sb{ub}"], tag="u")
                em.dma("gpsimd", rr(u_out)[oc, :, c * TC:(c + 1) * TC], u_sb[:, ub, :],
                       reads=[f"u_sb{ub}"], tag="u_st")
        em.finish(block)
    return nc


_L0A_CACHE = {}

def run_l0a(inputs):
    x = inputs["L0_attn_x"]
    ident = np.eye(128, dtype=np.float32).astype(BF)
    in_maps = []
    for c in range(NCORES):
        b, s = divmod(c, 4)
        t0 = s * SEG
        xs_seg = np.zeros((3 + SEG, D), np.float32)
        xs_seg[3:] = x[b, t0:t0 + SEG]
        if s > 0:
            xs_seg[0:3] = x[b, t0 - 3:t0]
        in_maps.append(dict(
            x=f32c(xs_seg),
            w_in=f32c(inputs["L0_attn_W_in"]),
            conv_w=f32c(inputs["L0_attn_conv_w"]).reshape(16, 128, 4).transpose(1, 0, 2),
            conv_b=f32c(inputs["L0_attn_conv_b"]).reshape(16, 128, 1).transpose(1, 0, 2),
            w_x=f32c(inputs["L0_attn_W_x"]),
            w_dt=f32c(inputs["L0_attn_W_dt"]),
            b_dt=f32c(inputs["L0_attn_b_dt"]).reshape(16, 128, 1).transpose(1, 0, 2),
            xhalo=f32c(xs_seg[0:3].T.reshape(8, 128, 3).transpose(1, 0, 2)),
            ident=ident,
        ))
    if "l0a" not in _L0A_CACHE:
        _L0A_CACHE["l0a"] = build_l0a()
    res = run_bass_kernel_spmd(_L0A_CACHE["l0a"], in_maps, core_ids=list(range(NCORES)))
    out = {}
    for name, dt_, width in [("xs", BF, DIN), ("dt", BF, DIN), ("u", BF, DIN),
                             ("res_act", BF, DIN), ("bc", np.float32, 32)]:
        full = np.empty((B, width, L), dt_)
        for c in range(NCORES):
            b, s = divmod(c, 4)
            full[b, :, s * SEG:(s + 1) * SEG] = res.results[c][name]
        out[name] = full
    return out


def kernel(**inputs):
    raise NotImplementedError


def build_l0scan():
    nc = bass.Bass("TRN2", target_bir_lowering=False)
    DSL = 512
    NT = DSL // 8
    TC = 2048
    NCH = L // TC
    NSUB = TC // 512

    dt_in = nc.dram_tensor("dt", [DSL, L], BF16, kind="ExternalInput").ap()
    u_in = nc.dram_tensor("u", [DSL, L], BF16, kind="ExternalInput").ap()
    bc_in = nc.dram_tensor("bc", [32, L], F32, kind="ExternalInput").ap()
    alog_in = nc.dram_tensor("alog", [128, NT], F32, kind="ExternalInput").ap()
    nsum_in = nc.dram_tensor("nsum", [128, 8], BF16, kind="ExternalInput").ap()

    y_out = nc.dram_tensor("y", [DSL, L], BF16, kind="ExternalOutput").ap()

    es = ExitStack()
    T = lambda *a: es.enter_context(nc.sbuf_tensor(*a))
    P = lambda *a: es.enter_context(nc.psum_tensor(*a))

    apk = T("apk", [128, NT], F32)
    nsum_sb = T("nsum_sb", [128, 8], BF16)
    hcarry = T("hcarry", [128, NT], F32)
    bexp = [T(f"bexp{i}", [128, TC], BF16) for i in range(2)]
    cexp = [T(f"cexp{i}", [128, TC], BF16) for i in range(2)]
    dtp = [T(f"dtp{i}", [128, TC], BF16) for i in range(2)]
    up = [T(f"up{i}", [128, TC], BF16) for i in range(2)]
    dA = [T(f"dA{i}", [128, TC], F32) for i in range(2)]
    dBx = [T(f"dBx{i}", [128, TC], BF16) for i in range(2)]
    hs = [T(f"hs{i}", [128, TC], BF16) for i in range(2)]
    zs = [T(f"zs{i}", [128, TC], BF16) for i in range(2)]
    ps_y = [P(f"ps_y{i}", [8, TC], F32) for i in range(2)]
    ysb = [T(f"ysb{i}", [8, TC], BF16) for i in range(2)]

    em = Em(nc, es)
    block = es.enter_context(nc.Block())
    with es:
        em.dma("sync", apk[:], alog_in, writes=["apk_raw"])
        em.dma("sync", nsum_sb[:], nsum_in, writes=["nsum_sb"])
        em.op("scalar", lambda e: e.activation(out=apk[:], in_=apk[:], func=AF.Exp),
              reads=["apk_raw"], writes=["apk_e"], tag="a_exp")
        em.op("vector", lambda e: e.tensor_scalar_mul(out=apk[:], in0=apk[:], scalar1=-1.0),
              reads=["apk_e"], writes=["apk"], tag="a_neg")
        em.op("vector", lambda e: e.memset(hcarry[:], 0.0), writes=["hcarry"], tag="h0")

        for c in range(NCH):
            cb = c % 2
            t0 = c * TC
            em.dma("gpsimd", bexp[cb][:],
                   bc_in[0:16, t0:t0 + TC].unsqueeze(0).broadcast_to([8, 16, TC]),
                   writes=[f"bexp{cb}"], tag="bld")
            em.dma("gpsimd", cexp[cb][:],
                   bc_in[16:32, t0:t0 + TC].unsqueeze(0).broadcast_to([8, 16, TC]),
                   writes=[f"cexp{cb}"], tag="cld")
            for g in range(NT):
                gb = g % 2
                em.dma("sync", dtp[gb][:],
                       dt_in[g * 8:(g + 1) * 8, t0:t0 + TC].unsqueeze(1).broadcast_to([8, 16, TC]),
                       writes=[f"dtp{gb}"], tag="dtld")
                em.dma("sync", up[gb][:],
                       u_in[g * 8:(g + 1) * 8, t0:t0 + TC].unsqueeze(1).broadcast_to([8, 16, TC]),
                       writes=[f"up{gb}"], tag="uld")
                em.op("scalar",
                      lambda e, g=g, gb=gb: e.activation(out=dA[gb][:], in_=dtp[gb][:], func=AF.Exp,
                                                         scale=apk[:, g:g + 1]),
                      reads=[f"dtp{gb}", "apk"], writes=[f"dA{gb}"], tag="exp")
                em.op("vector",
                      lambda e, gb=gb, cb=cb: e.tensor_mul(out=dBx[gb][:], in0=up[gb][:], in1=bexp[cb][:]),
                      reads=[f"up{gb}", f"bexp{cb}"], writes=[f"dBx{gb}"], tag="dbx")
                em.op("vector",
                      lambda e, g=g, gb=gb: e.tensor_tensor_scan(
                          hs[gb][:], dA[gb][:], dBx[gb][:], hcarry[:, g:g + 1], ALU.mult, ALU.add),
                      reads=[f"dA{gb}", f"dBx{gb}", "hcarry"], writes=[f"hs{gb}"], tag="scan")
                em.op("vector",
                      lambda e, g=g, gb=gb: e.tensor_copy(out=hcarry[:, g:g + 1], in_=hs[gb][:, TC - 1:TC]),
                      reads=[f"hs{gb}"], writes=["hcarry"], tag="hc")
                em.op("vector",
                      lambda e, gb=gb, cb=cb: e.tensor_mul(out=zs[gb][:], in0=hs[gb][:], in1=cexp[cb][:]),
                      reads=[f"hs{gb}", f"cexp{cb}"], writes=[f"zs{gb}"], tag="z")
                for sub in range(NSUB):
                    em.op("tensor",
                          lambda e, gb=gb, sub=sub: e.matmul(
                              ps_y[gb][:, sub * 512:(sub + 1) * 512], nsum_sb[:],
                              zs[gb][:, sub * 512:(sub + 1) * 512],
                              start=True, stop=True),
                          reads=[f"zs{gb}", "nsum_sb"],
                          writes=([f"ps_y{gb}"] if sub in (0, NSUB - 1) else []), tag="nsum")
                em.op("scalar",
                      lambda e, gb=gb: e.copy(out=ysb[gb][:], in_=ps_y[gb][:, :]),
                      reads=[f"ps_y{gb}"], writes=[f"ysb{gb}"], tag="yev")
                em.dma("sync", y_out[g * 8:(g + 1) * 8, t0:t0 + TC],
                       ysb[gb][:], reads=[f"ysb{gb}"], tag="yst")
        em.finish(block)
    return nc


_BUILD_CACHE = {}

def run_l0scan(l0a_out, inputs):
    alog = f32c(inputs["L0_attn_A_log"])
    in_maps = []
    for c in range(NCORES):
        b, dg = divmod(c, 4)
        dsl = slice(dg * 512, (dg + 1) * 512)
        al = alog[dsl].reshape(64, 8, 16).reshape(64, 128).T
        nsum = np.zeros((128, 8), np.float32)
        for di in range(8):
            nsum[di * 16:(di + 1) * 16, di] = 1.0
        in_maps.append(dict(
            dt=bf16c(l0a_out["dt"][b, dsl]),
            u=bf16c(l0a_out["u"][b, dsl]),
            bc=f32c(l0a_out["bc"][b]),
            alog=f32c(al),
            nsum=bf16c(nsum),
        ))
    if "l0scan" not in _BUILD_CACHE:
        _BUILD_CACHE["l0scan"] = build_l0scan()
    res = run_bass_kernel_spmd(_BUILD_CACHE["l0scan"], in_maps, core_ids=list(range(NCORES)))
    y = np.empty((B, DIN, L), BF)
    for c in range(NCORES):
        b, dg = divmod(c, 4)
        y[b, dg * 512:(dg + 1) * 512] = res.results[c]["y"]
    return y


def build_l0b():
    nc = bass.Bass("TRN2", target_bir_lowering=False)
    TC = 512
    NCH = SEG // TC

    y_in = nc.dram_tensor("y", [DIN, SEG], BF16, kind="ExternalInput").ap()
    xs_in = nc.dram_tensor("xs", [DIN, SEG], BF16, kind="ExternalInput").ap()
    ra_in = nc.dram_tensor("ra", [DIN, SEG], BF16, kind="ExternalInput").ap()
    x_in = nc.dram_tensor("x", [SEG, D], F32, kind="ExternalInput").ap()
    wout_in = nc.dram_tensor("wout", [DIN, D], F32, kind="ExternalInput").ap()
    dvec_in = nc.dram_tensor("dvec", [128, 16, 1], F32, kind="ExternalInput").ap()
    ident_in = nc.dram_tensor("ident", [128, 128], BF16, kind="ExternalInput").ap()
    ones_in = nc.dram_tensor("ones", [128, 1], F32, kind="ExternalInput").ap()
    onesr_in = nc.dram_tensor("onesr", [1, 128], F32, kind="ExternalInput").ap()

    x1_out = nc.dram_tensor("x1", [D, SEG], BF16, kind="ExternalOutput").ap()

    es = ExitStack()
    T = lambda *a: es.enter_context(nc.sbuf_tensor(*a))
    P = lambda *a: es.enter_context(nc.psum_tensor(*a))

    wout_sb = T("wout_sb", [128, 16, D], BF16)
    dvec_sb = T("dvec_sb", [128, 16, 1], F32)
    id_sb = T("id_sb", [128, 128], BF16)
    ones_sb = T("ones_sb", [128, 1], F32)
    onesr_sb = T("onesr_sb", [1, 128], F32)
    ytile = T("ytile", [128, 2, TC], BF16)
    xstile = T("xstile", [128, 2, TC], BF16)
    ratile = T("ratile", [128, 2, TC], BF16)
    y2_sb = T("y2_sb", [128, 16, TC], BF16)
    tmp_bf = T("tmp_bf", [128, TC], BF16)
    xtok = T("xtok", [128, 4, D], F32)
    xtokb = T("xtokb", [128, 4, D], BF16)
    xfm = T("xfm", [128, 8, TC], BF16)
    s_sb = T("s_sb", [128, 8, TC], F32)
    sq_sb = T("sq_sb", [128, TC], F32)
    row_m = T("row_m", [1, TC], F32)
    row_v = T("row_v", [1, TC], F32)
    row_t = T("row_t", [1, TC], F32)
    row_b = T("row_b", [1, TC], F32)
    row_a = T("row_a", [1, TC], F32)
    tmp_f = T("tmp_f", [128, TC], F32)
    x1_sb = T("x1_sb", [128, 8, TC], BF16)

    ps_tr = [P(f"ps_tr{i}", [128, 128], BF16) for i in range(2)]
    ps_o = [P(f"ps_o{i}", [128, TC], F32) for i in range(2)]
    ps_sum = P("ps_sum", [1, TC], F32)
    ps_sq = P("ps_sq", [1, TC], F32)
    ps_a = P("ps_a", [128, TC], F32)
    ps_b = P("ps_b", [128, TC], F32)

    em = Em(nc, es)
    block = es.enter_context(nc.Block())
    with es:
        em.dma("gpsimd", wout_sb[:], wout_in.rearrange("(k p) n -> p k n", p=128), writes=["wout_sb"])
        em.dma("sync", dvec_sb[:], dvec_in, writes=["dvec_sb"])
        em.dma("sync", id_sb[:], ident_in, writes=["id_sb"], sem="wl2")
        em.dma("sync", ones_sb[:], ones_in, writes=["ones_sb"], sem="wl2")
        em.dma("sync", onesr_sb[:], onesr_in, writes=["onesr_sb"])

        for c in range(NCH):
            t0 = c * TC
            for oc in range(16):
                tb = oc % 2
                em.dma("sync", ytile[:, tb, :], rr(y_in)[oc, :, t0:t0 + TC], writes=[f"yt{tb}"], tag="yld")
                em.dma("sync", xstile[:, tb, :], rr(xs_in)[oc, :, t0:t0 + TC], writes=[f"xst{tb}"], tag="xsld")
                em.dma("scalar", ratile[:, tb, :], rr(ra_in)[oc, :, t0:t0 + TC], writes=[f"rat{tb}"], tag="rald")
                em.op("vector",
                      lambda e, oc=oc, tb=tb: e.scalar_tensor_tensor(
                          out=tmp_bf[:], in0=xstile[:, tb, :], scalar=dvec_sb[:, oc, 0:1],
                          in1=ytile[:, tb, :], op0=ALU.mult, op1=ALU.add),
                      reads=[f"xst{tb}", f"yt{tb}", "dvec_sb"], writes=["tmp_bf"], tag="y2a")
                em.op("vector",
                      lambda e, oc=oc, tb=tb: e.tensor_mul(out=y2_sb[:, oc, :], in0=tmp_bf[:], in1=ratile[:, tb, :]),
                      reads=["tmp_bf", f"rat{tb}"], writes=[f"y2_{oc}"], tag="y2b")
            for tt in range(4):
                row0 = t0 + tt * 128
                em.dma("sync", xtok[:, tt, :], x_in[row0:row0 + 128, :], writes=[f"xtok{tt}"], tag="xld")
                em.op("vector", lambda e, tt=tt: e.tensor_copy(out=xtokb[:, tt, :], in_=xtok[:, tt, :]),
                      reads=[f"xtok{tt}"], writes=[f"xtokb{tt}"], tag="xcv")
                for dc in range(8):
                    sl = (tt * 8 + dc) % 2
                    em.op("tensor",
                          lambda e, dc=dc, tt=tt, sl=sl: e.transpose(
                              ps_tr[sl][:, :], xtokb[:, tt, dc * 128:(dc + 1) * 128], id_sb[:]),
                          reads=[f"xtokb{tt}", "id_sb"], writes=[f"ps_tr{sl}"], tag="xtr")
                    em.op("scalar",
                          lambda e, dc=dc, tt=tt, sl=sl: e.copy(
                              out=xfm[:, dc, tt * 128:(tt + 1) * 128], in_=ps_tr[sl][:, :]),
                          reads=[f"ps_tr{sl}"], writes=[f"xfm{dc}_{tt}"], tag="xev")
            for dc in range(8):
                pb = dc % 2
                for kc in range(16):
                    em.op("tensor",
                          lambda e, dc=dc, kc=kc, pb=pb: e.matmul(
                              ps_o[pb][:, :], wout_sb[:, kc, dc * 128:(dc + 1) * 128], y2_sb[:, kc, :],
                              start=(kc == 0), stop=(kc == 15)),
                          reads=["wout_sb", f"y2_{kc}"],
                          writes=([f"ps_o{pb}"] if kc in (0, 15) else []), tag="wo")
                em.op("vector",
                      lambda e, dc=dc, pb=pb: e.tensor_add(
                          out=s_sb[:, dc, :], in0=ps_o[pb][:, :],
                          in1=xfm[:, dc, :]),
                      reads=[f"ps_o{pb}"] + [f"xfm{dc}_{tt}" for tt in range(4)],
                      writes=[f"s{dc}"], tag="resadd")
            for dc in range(8):
                em.op("tensor",
                      lambda e, dc=dc: e.matmul(ps_sum[:, :], ones_sb[:], s_sb[:, dc, :],
                                                start=(dc == 0), stop=(dc == 7)),
                      reads=[f"s{dc}", "ones_sb"], writes=(["ps_sum"] if dc in (0, 7) else []), tag="sum")
            for dc in range(8):
                em.op("scalar",
                      lambda e, dc=dc: e.square(out=sq_sb[:], in_=s_sb[:, dc, :]),
                      reads=[f"s{dc}"], writes=["sq_sb"], tag="sq")
                em.op("tensor",
                      lambda e, dc=dc: e.matmul(ps_sq[:, :], ones_sb[:], sq_sb[:],
                                                start=(dc == 0), stop=(dc == 7)),
                      reads=["sq_sb", "ones_sb"], writes=(["ps_sq"] if dc in (0, 7) else []), tag="sq_mm")
            em.op("vector", lambda e: e.tensor_scalar_mul(out=row_m[:], in0=ps_sum[:, :], scalar1=1.0 / D),
                  reads=["ps_sum"], writes=["row_m"], tag="m")
            em.op("vector", lambda e: e.tensor_scalar_mul(out=row_v[:], in0=ps_sq[:, :], scalar1=1.0 / D),
                  reads=["ps_sq"], writes=["row_v"], tag="msq")
            em.op("vector", lambda e: e.tensor_mul(out=row_t[:], in0=row_m[:], in1=row_m[:]),
                  reads=["row_m"], writes=["row_t"], tag="m2")
            em.op("vector", lambda e: e.tensor_sub(out=row_v[:], in0=row_v[:], in1=row_t[:]),
                  reads=["row_v", "row_t"], writes=["row_v"], tag="var")
            em.op("vector", lambda e: e.tensor_scalar_add(out=row_v[:], in0=row_v[:], scalar1=1e-5),
                  reads=["row_v"], writes=["row_v"], tag="eps")
            em.op("scalar", lambda e: e.activation(out=row_b[:], in_=row_v[:], func=AF.Ln),
                  reads=["row_v"], writes=["row_b"], tag="ln")
            em.op("scalar", lambda e: e.activation(out=row_b[:], in_=row_b[:], func=AF.Exp, scale=-0.5),
                  reads=["row_b"], writes=["row_b"], tag="rstd")
            em.op("vector", lambda e: e.tensor_mul(out=row_a[:], in0=row_m[:], in1=row_b[:]),
                  reads=["row_m", "row_b"], writes=["row_a"], tag="a1")
            em.op("vector", lambda e: e.tensor_scalar_mul(out=row_a[:], in0=row_a[:], scalar1=-1.0),
                  reads=["row_a"], writes=["row_a"], tag="a2")
            em.op("tensor", lambda e: e.matmul(ps_b[:, :], onesr_sb[:], row_b[:], start=True, stop=True),
                  reads=["row_b", "onesr_sb"], writes=["ps_b"], tag="bcb")
            em.op("tensor", lambda e: e.matmul(ps_a[:, :], onesr_sb[:], row_a[:], start=True, stop=True),
                  reads=["row_a", "onesr_sb"], writes=["ps_a"], tag="bca")
            for dc in range(8):
                em.op("vector",
                      lambda e, dc=dc: e.tensor_mul(out=tmp_f[:], in0=s_sb[:, dc, :], in1=ps_b[:, :]),
                      reads=[f"s{dc}", "ps_b"], writes=["tmp_f"], tag="apb")
                em.op("vector",
                      lambda e, dc=dc: e.tensor_add(out=x1_sb[:, dc, :], in0=tmp_f[:], in1=ps_a[:, :]),
                      reads=["tmp_f", "ps_a"], writes=[f"x1_{dc}"], tag="apa")
                em.dma("sync", rr(x1_out)[dc, :, t0:t0 + TC], x1_sb[:, dc, :],
                       reads=[f"x1_{dc}"], tag="x1st")
        em.finish(block)
    return nc


def run_l0b(inputs, l0a_out, y):
    wout = f32c(inputs["L0_attn_W_out"])
    ident = np.eye(128, dtype=np.float32).astype(BF)
    x = inputs["L0_attn_x"]
    in_maps = []
    for c in range(NCORES):
        b, sg = divmod(c, 4)
        t0 = sg * SEG
        in_maps.append(dict(
            y=bf16c(y[b, :, t0:t0 + SEG]),
            xs=bf16c(l0a_out["xs"][b, :, t0:t0 + SEG]),
            ra=bf16c(l0a_out["res_act"][b, :, t0:t0 + SEG]),
            x=f32c(x[b, t0:t0 + SEG]),
            wout=wout,
            dvec=f32c(inputs["L0_attn_D"]).reshape(16, 128, 1).transpose(1, 0, 2),
            ident=ident,
            ones=np.ones((128, 1), np.float32),
            onesr=np.ones((1, 128), np.float32),
        ))
    if "l0b" not in _BUILD_CACHE:
        _BUILD_CACHE["l0b"] = build_l0b()
    res = run_bass_kernel_spmd(_BUILD_CACHE["l0b"], in_maps, core_ids=list(range(NCORES)))
    x1 = np.empty((B, D, L), BF)
    for c in range(NCORES):
        b, sg = divmod(c, 4)
        x1[b, :, sg * SEG:(sg + 1) * SEG] = res.results[c]["x1"]
    return x1


def build_l0f():
    nc = bass.Bass("TRN2", target_bir_lowering=False)
    TC = 512
    NCH = SEG // TC
    G = TC // 32

    x1_in = nc.dram_tensor("x1", [D, SEG], BF16, kind="ExternalInput").ap()
    wc_in = nc.dram_tensor("wc", [D, 128], F32, kind="ExternalInput").ap()
    bcm_in = nc.dram_tensor("bcm", [128, 1], F32, kind="ExternalInput").ap()
    w1d_in = nc.dram_tensor("w1d", [D, 4096], F32, kind="ExternalInput").ap()
    b1p_in = nc.dram_tensor("b1p", [128, 32], F32, kind="ExternalInput").ap()
    w2d_in = nc.dram_tensor("w2d", [4096, D], F32, kind="ExternalInput").ap()
    b2e_in = nc.dram_tensor("b2e", [128, D], F32, kind="ExternalInput").ap()
    ones_in = nc.dram_tensor("ones", [128, 1], F32, kind="ExternalInput").ap()
    onesb_in = nc.dram_tensor("onesb", [128, 1], BF16, kind="ExternalInput").ap()
    onesr_in = nc.dram_tensor("onesr", [1, 128], F32, kind="ExternalInput").ap()

    x2_out = nc.dram_tensor("x2", [D, SEG], BF16, kind="ExternalOutput").ap()
    perm_dram = nc.dram_tensor("perm_scratch", [128, SEG], BF16, kind="Internal").ap()

    es = ExitStack()
    T = lambda *a: es.enter_context(nc.sbuf_tensor(*a))
    P = lambda *a: es.enter_context(nc.psum_tensor(*a))

    wc_sb = T("wc_sb", [128, 8, 128], BF16)
    bcm_sb = T("bcm_sb", [128, 1], F32)
    w1d_sb = T("w1d_sb", [128, 8, 4096], BF16)
    b1p_sb = T("b1p_sb", [128, 32], F32)
    w2d_sb = T("w2d_sb", [128, 32, D], BF16)
    b2e_sb = T("b2e_sb", [128, D], BF16)
    ones_sb = T("ones_sb", [128, 1], F32)
    onesb_sb = T("onesb_sb", [128, 1], BF16)
    onesr_sb = T("onesr_sb", [1, 128], F32)

    x1t = T("x1t", [128, 8, TC], BF16)
    lgs = T("lgs", [128, TC], F32)
    mx = T("mx", [128, TC // 32], F32)
    perm = T("perm", [128, TC], BF16)
    pe_sb = [T(f"pe_sb{i}", [128, TC], BF16) for i in range(2)]
    h1r = [T(f"h1r{i}", [128, TC], BF16) for i in range(2)]
    h1m = T("h1m", [128, 32, TC], BF16)
    s_sb = T("s_sb", [128, 8, TC], BF16)
    sq_sb = T("sq_sb", [128, TC], F32)
    row_m = T("row_m", [1, TC], F32)
    row_v = T("row_v", [1, TC], F32)
    row_t = T("row_t", [1, TC], F32)
    row_b = T("row_b", [1, TC], F32)
    row_a = T("row_a", [1, TC], F32)
    tmp_f = T("tmp_f", [128, TC], F32)
    x2_sb = T("x2_sb", [128, TC], BF16)

    ps_lg = P("ps_lg", [128, TC], F32)
    ps_h1 = [P(f"ps_h1{i}", [128, TC], F32) for i in range(2)]
    ps_r = [P(f"ps_r{i}", [128, TC], F32) for i in range(2)]
    ps_sum = P("ps_sum", [1, TC], F32)
    ps_sq = P("ps_sq", [1, TC], F32)
    ps_a = P("ps_a", [128, TC], F32)
    ps_b = ps_lg

    em = Em(nc, es)
    block = es.enter_context(nc.Block())
    with es:
        em.dma("gpsimd", wc_sb[:], wc_in.rearrange("(k p) n -> p k n", p=128), writes=["wc_sb"])
        em.dma("sync", bcm_sb[:], bcm_in, writes=["bcm_sb"])
        em.dma("gpsimd", w1d_sb[:], w1d_in.rearrange("(k p) n -> p k n", p=128), writes=["w1d_sb"])
        em.dma("sync", b1p_sb[:], b1p_in, writes=["b1p_sb"])
        em.dma("gpsimd", w2d_sb[:], w2d_in.rearrange("(k p) n -> p k n", p=128), writes=["w2d_sb"])
        em.dma("gpsimd", b2e_sb[:], b2e_in, writes=["b2e_sb"])
        em.dma("sync", ones_sb[:], ones_in, writes=["ones_sb"], sem="wl2")
        em.dma("sync", onesb_sb[:], onesb_in, writes=["onesb_sb"], sem="wl2")
        em.dma("sync", onesr_sb[:], onesr_in, writes=["onesr_sb"])

        for c in range(NCH):
            t0 = c * TC
            for dc in range(8):
                em.dma("sync", x1t[:, dc, :], rr(x1_in)[dc, :, t0:t0 + TC], writes=[f"x1t{dc}"], tag="x1ld")
            for kc in range(8):
                em.op("tensor",
                      lambda e, kc=kc: e.matmul(ps_lg[:, :], wc_sb[:, kc, :], x1t[:, kc, :],
                                                start=(kc == 0), stop=(kc == 7)),
                      reads=["wc_sb", f"x1t{kc}"], writes=(["ps_lg"] if kc in (0, 7) else []), tag="lg")
            em.op("scalar", lambda e: e.activation(out=lgs[:], in_=ps_lg[:, :], func=AF.Identity,
                                                   bias=bcm_sb[:, 0:1]),
                  reads=["ps_lg", "bcm_sb"], writes=["lgs"], tag="lgev")
            em.op("vector", lambda e: e.tensor_reduce(out=mx[:, :, None], in_=lgs[:].rearrange("p (g w) -> p g w", w=32),
                                                      axis=AX.X, op=ALU.max),
                  reads=["lgs"], writes=["mx"], tag="max")
            em.op("vector", lambda e: e.tensor_tensor(
                      out=perm[:].rearrange("p (g w) -> p g w", w=32),
                      in0=lgs[:].rearrange("p (g w) -> p g w", w=32),
                      in1=mx[:, :, None].broadcast_to([128, TC // 32, 32]),
                      op=ALU.is_equal),
                  reads=["lgs", "mx"], writes=["perm"], tag="onehot")
            em.dma("sync", perm_dram[:, t0:t0 + TC], perm[:], reads=["perm"], writes=["perm_dram"], tag="permst")
            for ch in range(32):
                hb = ch % 2
                for kc in range(8):
                    em.op("tensor",
                          lambda e, ch=ch, kc=kc, hb=hb: e.matmul(
                              ps_h1[hb][:, :], w1d_sb[:, kc, ch * 128:(ch + 1) * 128], x1t[:, kc, :],
                              start=(kc == 0), stop=(kc == 7)),
                          reads=["w1d_sb", f"x1t{kc}"], writes=([f"ps_h1{hb}"] if kc in (0, 7) else []),
                          tag="h1")
                em.op("scalar",
                      lambda e, ch=ch, hb=hb: e.activation(out=h1r[hb][:], in_=ps_h1[hb][:, :], func=AF.Relu,
                                                           bias=b1p_sb[:, ch:ch + 1]),
                      reads=[f"ps_h1{hb}", "b1p_sb"], writes=[f"h1r{hb}"], tag="relu")
                em.dma("gpsimd", pe_sb[hb][:],
                       perm_dram[ch * 4:(ch + 1) * 4, t0:t0 + TC].unsqueeze(1).broadcast_to([4, 32, TC]),
                       reads=["perm_dram"], writes=[f"pe{hb}"], tag="peld")
                em.op("vector",
                      lambda e, ch=ch, hb=hb: e.tensor_mul(out=h1m[:, ch, :], in0=h1r[hb][:], in1=pe_sb[hb][:]),
                      reads=[f"h1r{hb}", f"pe{hb}"], writes=[f"h1m{ch}"], tag="mask")
            for dc in range(8):
                pb = dc % 2
                for kc in range(32):
                    em.op("tensor",
                          lambda e, dc=dc, kc=kc, pb=pb: e.matmul(
                              ps_r[pb][:, :], w2d_sb[:, kc, dc * 128:(dc + 1) * 128], h1m[:, kc, :],
                              start=(kc == 0), stop=False),
                          reads=["w2d_sb", f"h1m{kc}"], writes=([f"ps_r{pb}"] if kc == 0 else []),
                          tag="w2")
                em.op("tensor",
                      lambda e, dc=dc, pb=pb: e.matmul(
                          ps_r[pb][:, :], b2e_sb[:, dc * 128:(dc + 1) * 128], perm[:],
                          start=False, stop=True),
                      reads=["b2e_sb", "perm"], writes=[f"ps_r{pb}"], tag="b2")
                em.op("vector",
                      lambda e, dc=dc, pb=pb: e.tensor_add(out=s_sb[:, dc, :], in0=ps_r[pb][:, :],
                                                           in1=x1t[:, dc, :]),
                      reads=[f"ps_r{pb}", f"x1t{dc}"], writes=[f"s{dc}"], tag="resadd")
            for dc in range(8):
                em.op("tensor",
                      lambda e, dc=dc: e.matmul(ps_sum[:, :], onesb_sb[:], s_sb[:, dc, :],
                                                start=(dc == 0), stop=(dc == 7)),
                      reads=[f"s{dc}", "onesb_sb"], writes=(["ps_sum"] if dc in (0, 7) else []), tag="sum")
            for dc in range(8):
                em.op("scalar",
                      lambda e, dc=dc: e.square(out=sq_sb[:], in_=s_sb[:, dc, :]),
                      reads=[f"s{dc}"], writes=["sq_sb"], tag="sq")
                em.op("tensor",
                      lambda e, dc=dc: e.matmul(ps_sq[:, :], ones_sb[:], sq_sb[:],
                                                start=(dc == 0), stop=(dc == 7)),
                      reads=["sq_sb", "ones_sb"], writes=(["ps_sq"] if dc in (0, 7) else []), tag="sqmm")
            em.op("vector", lambda e: e.tensor_scalar_mul(out=row_m[:], in0=ps_sum[:, :], scalar1=1.0 / D),
                  reads=["ps_sum"], writes=["row_m"], tag="m")
            em.op("vector", lambda e: e.tensor_scalar_mul(out=row_v[:], in0=ps_sq[:, :], scalar1=1.0 / D),
                  reads=["ps_sq"], writes=["row_v"], tag="msq")
            em.op("vector", lambda e: e.tensor_mul(out=row_t[:], in0=row_m[:], in1=row_m[:]),
                  reads=["row_m"], writes=["row_t"], tag="m2")
            em.op("vector", lambda e: e.tensor_sub(out=row_v[:], in0=row_v[:], in1=row_t[:]),
                  reads=["row_v", "row_t"], writes=["row_v"], tag="var")
            em.op("vector", lambda e: e.tensor_scalar_add(out=row_v[:], in0=row_v[:], scalar1=1e-5),
                  reads=["row_v"], writes=["row_v"], tag="eps")
            em.op("scalar", lambda e: e.activation(out=row_b[:], in_=row_v[:], func=AF.Ln),
                  reads=["row_v"], writes=["row_b"], tag="ln")
            em.op("scalar", lambda e: e.activation(out=row_b[:], in_=row_b[:], func=AF.Exp, scale=-0.5),
                  reads=["row_b"], writes=["row_b"], tag="rstd")
            em.op("vector", lambda e: e.tensor_mul(out=row_a[:], in0=row_m[:], in1=row_b[:]),
                  reads=["row_m", "row_b"], writes=["row_a"], tag="a1")
            em.op("vector", lambda e: e.tensor_scalar_mul(out=row_a[:], in0=row_a[:], scalar1=-1.0),
                  reads=["row_a"], writes=["row_a"], tag="a2")
            em.op("tensor", lambda e: e.matmul(ps_b[:, :], onesr_sb[:], row_b[:], start=True, stop=True),
                  reads=["row_b", "onesr_sb"], writes=["ps_lg"], tag="bcb")
            em.op("tensor", lambda e: e.matmul(ps_a[:, :], onesr_sb[:], row_a[:], start=True, stop=True),
                  reads=["row_a", "onesr_sb"], writes=["ps_a"], tag="bca")
            for dc in range(8):
                em.op("vector",
                      lambda e, dc=dc: e.tensor_mul(out=tmp_f[:], in0=s_sb[:, dc, :], in1=ps_b[:, :]),
                      reads=[f"s{dc}", "ps_lg"], writes=["tmp_f"], tag="apb")
                em.op("vector",
                      lambda e, dc=dc: e.tensor_add(out=x2_sb[:], in0=tmp_f[:], in1=ps_a[:, :]),
                      reads=["tmp_f", "ps_a"], writes=["x2_sb"], tag="apa")
                em.dma("sync", rr(x2_out)[dc, :, t0:t0 + TC], x2_sb[:],
                       reads=["x2_sb"], tag="x2st")
        em.finish(block)
    return nc


def run_l0f(inputs, x1):
    Wc = f32c(inputs["L0_ffn_Wc"]).reshape(D, 128)
    bcm = f32c(inputs["L0_ffn_bc"]).reshape(128, 1)
    W1 = f32c(inputs["L0_ffn_W1"])
    w1d = W1.transpose(2, 0, 1, 3).reshape(D, 4096)
    b1p = f32c(inputs["L0_ffn_b1"]).reshape(128, 32).T.copy()
    b1flat = f32c(inputs["L0_ffn_b1"]).reshape(4096)
    b1p = b1flat.reshape(32, 128).T.copy()
    W2 = f32c(inputs["L0_ffn_W2"])
    w2d = W2.transpose(0, 1, 3, 2).reshape(4096, D)
    b2e = f32c(inputs["L0_ffn_b2"]).reshape(128, D)
    in_maps = []
    for c in range(NCORES):
        b, sg = divmod(c, 4)
        in_maps.append(dict(
            x1=bf16c(x1[b, :, sg * SEG:(sg + 1) * SEG]),
            wc=Wc, bcm=bcm, w1d=w1d, b1p=b1p, w2d=w2d, b2e=b2e,
            ones=np.ones((128, 1), np.float32),
            onesb=np.ones((128, 1), np.float32).astype(BF),
            onesr=np.ones((1, 128), np.float32),
        ))
    if "l0f" not in _BUILD_CACHE:
        _BUILD_CACHE["l0f"] = build_l0f()
    res = run_bass_kernel_spmd(_BUILD_CACHE["l0f"], in_maps, core_ids=list(range(NCORES)))
    x2 = np.empty((B, D, L), BF)
    for c in range(NCORES):
        b, sg = divmod(c, 4)
        x2[b, :, sg * SEG:(sg + 1) * SEG] = res.results[c]["x2"]
    return x2


def build_l1a():
    nc = bass.Bass("TRN2", target_bir_lowering=False)
    TC = 512
    NCH = SEG // TC + 1
    EXT = NCH * TC
    KF = 64

    x2_in = nc.dram_tensor("x2e", [D, EXT], BF16, kind="ExternalInput").ap()
    qT_in = nc.dram_tensor("qT", [D, D], F32, kind="ExternalInput").ap()
    kT_in = nc.dram_tensor("kT", [D, D], F32, kind="ExternalInput").ap()
    vT_in = nc.dram_tensor("vT", [D, D], F32, kind="ExternalInput").ap()
    bT_in = nc.dram_tensor("bT", [D, 4], F32, kind="ExternalInput").ap()
    qcw_in = nc.dram_tensor("qcw", [128, 8, 4], F32, kind="ExternalInput").ap()
    kcw_in = nc.dram_tensor("kcw", [128, 8, 4], F32, kind="ExternalInput").ap()
    vcw_in = nc.dram_tensor("vcw", [128, 8, 4], F32, kind="ExternalInput").ap()
    fs_in = nc.dram_tensor("fs", [128, 8, 5], F32, kind="ExternalInput").ap()
    fl_in = nc.dram_tensor("fl", [128, 8, KF], F32, kind="ExternalInput").ap()

    q_out = nc.dram_tensor("q", [D, SEG], BF16, kind="ExternalOutput").ap()
    k_out = nc.dram_tensor("k", [D, SEG], BF16, kind="ExternalOutput").ap()
    v_out = nc.dram_tensor("v", [D, SEG], BF16, kind="ExternalOutput").ap()
    beta_out = nc.dram_tensor("beta", [4, SEG], F32, kind="ExternalOutput").ap()
    ls_out = nc.dram_tensor("ls", [D, SEG], BF16, kind="ExternalOutput").ap()
    ll_out = nc.dram_tensor("ll", [D, SEG], BF16, kind="ExternalOutput").ap()
    vsc = nc.dram_tensor("vscratch", [D, EXT], BF16, kind="Internal").ap()

    es = ExitStack()
    T = lambda *a: es.enter_context(nc.sbuf_tensor(*a))
    P = lambda *a: es.enter_context(nc.psum_tensor(*a))

    wq = T("wq", [128, 8, D], BF16)
    wk = T("wk", [128, 8, D], BF16)
    wv = T("wv", [128, 8, D], BF16)
    wb = T("wb", [128, 8, 4], BF16)
    qcw = T("qcw_s", [128, 8, 4], F32)
    kcw = T("kcw_s", [128, 8, 4], F32)
    vcw = T("vcw_s", [128, 8, 4], F32)
    fs_sb = T("fs_sb", [128, 8, 5], F32)
    fl_sb = T("fl_sb", [128, 8, KF], F32)

    xt = T("xt", [128, 8, TC], BF16)
    exts = {n: T(f"ext_{n}", [128, 8, 3 + TC], BF16) for n in "qkv"}
    conv_t = T("conv_t", [128, TC], F32)
    out_bf = T("out_bf", [128, 2, TC], BF16)
    bsig = T("bsig", [4, TC], F32)
    vw = T("vw", [128, 8, KF - 1 + TC], BF16)
    fir_t = T("fir_t", [128, TC], F32)
    fir_o = T("fir_o", [128, 2, TC], BF16)

    ps_mm = [P(f"ps_mm{i}", [128, TC], F32) for i in range(4)]
    ps_b = P("ps_b", [4, TC], F32)

    em = Em(nc, es)
    block = es.enter_context(nc.Block())
    with es:
        em.dma("gpsimd", wq[:], qT_in.rearrange("(k p) n -> p k n", p=128), writes=["wq"])
        em.dma("gpsimd", wk[:], kT_in.rearrange("(k p) n -> p k n", p=128), writes=["wk"])
        em.dma("gpsimd", wv[:], vT_in.rearrange("(k p) n -> p k n", p=128), writes=["wv"])
        em.dma("gpsimd", wb[:], bT_in.rearrange("(k p) n -> p k n", p=128), writes=["wb"])
        em.dma("sync", qcw[:], qcw_in, writes=["qcw"])
        em.dma("sync", kcw[:], kcw_in, writes=["kcw"])
        em.dma("sync", vcw[:], vcw_in, writes=["vcw"])
        em.dma("sync", fs_sb[:], fs_in, writes=["fs_sb"])
        em.dma("sync", fl_sb[:], fl_in, writes=["fl_sb"])

        for n2 in "qkv":
            em.op("vector", lambda e, n2=n2: e.memset(exts[n2][:], 0.0),
                  writes=[f"exth_{n2}{oc}" for oc in range(8)] + [f"ext_{n2}{oc}" for oc in range(8)],
                  tag="extz")
        for c in range(NCH):
            t0 = c * TC
            for dc in range(8):
                em.dma("sync", xt[:, dc, :], rr(x2_in)[dc, :, t0:t0 + TC], writes=[f"xt{dc}"], tag="xld")
            for name, w, cw, dst in (("q", wq, qcw, q_out), ("k", wk, kcw, k_out), ("v", wv, vcw, v_out)):
                ext = exts[name]
                for oc in range(8):
                    pb = oc % 4
                    for kc in range(8):
                        em.op("tensor",
                              lambda e, oc=oc, kc=kc, pb=pb, w=w: e.matmul(
                                  ps_mm[pb][:, :], w[:, kc, oc * 128:(oc + 1) * 128], xt[:, kc, :],
                                  start=(kc == 0), stop=(kc == 7)),
                              reads=["w" + name, f"xt{kc}"],
                              writes=([f"ps_mm{pb}"] if kc in (0, 7) else []), tag="proj")
                    em.op("vector",
                          lambda e, oc=oc, pb=pb, ext=ext: e.tensor_copy(out=ext[:, oc, 3:], in_=ps_mm[pb][:, :]),
                          reads=[f"ps_mm{pb}", f"tl_{name}{oc}"], writes=[f"ext_{name}{oc}"], tag="pev")
                    em.op("vector",
                          lambda e, oc=oc, cw=cw, ext=ext: e.tensor_scalar(
                              out=conv_t[:], in0=ext[:, oc, 0:TC], scalar1=cw[:, oc, 0:1], scalar2=None,
                              op0=ALU.mult),
                          reads=[f"ext_{name}{oc}", f"exth_{name}{oc}", name + "cw"], writes=["conv_t"], tag="cv0")
                    for j in range(1, 4):
                        em.op("vector",
                              lambda e, oc=oc, j=j, cw=cw, ext=ext: e.scalar_tensor_tensor(
                                  out=conv_t[:], in0=ext[:, oc, j:j + TC], scalar=cw[:, oc, j:j + 1],
                                  in1=conv_t[:], op0=ALU.mult, op1=ALU.add),
                              reads=[f"ext_{name}{oc}", "conv_t"], writes=["conv_t"], tag=f"cv{j}")
                    ob = oc % 2
                    em.op("scalar",
                          lambda e, ob=ob: e.activation(out=out_bf[:, ob, :], in_=conv_t[:], func=AF.Silu),
                          reads=["conv_t"], writes=[f"out_bf{ob}"], tag="sil")
                    em.op("vector",
                          lambda e, oc=oc, ext=ext: e.tensor_copy(out=ext[:, oc, 0:3], in_=ext[:, oc, TC:TC + 3]),
                          reads=[f"ext_{name}{oc}"], writes=[f"exth_{name}{oc}", f"tl_{name}{oc}"], tag="tail")
                    if c > 0:
                        em.dma("scalar", rr(dst)[oc, :, (c - 1) * TC:c * TC], out_bf[:, ob, :],
                               reads=[f"out_bf{ob}"], tag="post")
                    if name == "v":
                        em.dma("sync", rr(vsc)[oc, :, t0:t0 + TC], out_bf[:, ob, :],
                               reads=[f"out_bf{ob}"], writes=[f"vsc{oc}_{c % 2}"], tag="vst")
            for kc in range(8):
                em.op("tensor",
                      lambda e, kc=kc: e.matmul(ps_b[:, :], wb[:, kc, :], xt[:, kc, :],
                                                start=(kc == 0), stop=(kc == 7)),
                      reads=["wb", f"xt{kc}"], writes=(["ps_b"] if kc in (0, 7) else []), tag="bmm")
            em.op("scalar", lambda e: e.activation(out=bsig[:], in_=ps_b[:, :], func=AF.Sigmoid),
                  reads=["ps_b"], writes=["bsig"], tag="bsig")
            if c > 0:
                em.dma("scalar", beta_out[:, (c - 1) * TC:c * TC], bsig[:], reads=["bsig"], tag="bst")
            if c > 0:
                w0 = c * TC - (KF - 1)
                for oc in range(8):
                    em.dma("sync", vw[:, oc, :], rr(vsc)[oc, :, w0:w0 + KF - 1 + TC],
                           reads=[f"vsc{oc}_{c % 2}", f"vsc{oc}_{(c + 1) % 2}"],
                           writes=[f"vw{oc}"], tag="vwld")
                    em.op("vector",
                          lambda e, oc=oc: e.tensor_scalar(
                              out=fir_t[:], in0=vw[:, oc, KF - 5:KF - 5 + TC], scalar1=fs_sb[:, oc, 0:1],
                              scalar2=None, op0=ALU.mult),
                          reads=[f"vw{oc}", "fs_sb"], writes=["fir_t"], tag="fs0")
                    for j in range(1, 5):
                        em.op("vector",
                              lambda e, oc=oc, j=j: e.scalar_tensor_tensor(
                                  out=fir_t[:], in0=vw[:, oc, KF - 5 + j:KF - 5 + j + TC],
                                  scalar=fs_sb[:, oc, j:j + 1], in1=fir_t[:], op0=ALU.mult, op1=ALU.add),
                              reads=[f"vw{oc}", "fir_t"], writes=["fir_t"], tag="fsj")
                    fb = oc % 2
                    em.op("vector",
                          lambda e, fb=fb: e.tensor_copy(out=fir_o[:, fb, :], in_=fir_t[:]),
                          reads=["fir_t"], writes=[f"fir_o{fb}"], tag="fscv")
                    em.dma("gpsimd", rr(ls_out)[oc, :, (c - 1) * TC:c * TC], fir_o[:, fb, :],
                           reads=[f"fir_o{fb}"], tag="lsst")
                    em.op("vector",
                          lambda e, oc=oc: e.tensor_scalar(
                              out=fir_t[:], in0=vw[:, oc, 0:TC], scalar1=fl_sb[:, oc, 0:1],
                              scalar2=None, op0=ALU.mult),
                          reads=[f"vw{oc}", "fl_sb"], writes=["fir_t"], tag="fl0")
                    for j in range(1, KF):
                        em.op("vector",
                              lambda e, oc=oc, j=j: e.scalar_tensor_tensor(
                                  out=fir_t[:], in0=vw[:, oc, j:j + TC],
                                  scalar=fl_sb[:, oc, j:j + 1], in1=fir_t[:], op0=ALU.mult, op1=ALU.add),
                              reads=[f"vw{oc}", "fir_t"], writes=["fir_t"], tag="flj")
                    em.op("vector",
                          lambda e, fb=fb: e.tensor_copy(out=fir_o[:, fb, :], in_=fir_t[:]),
                          reads=["fir_t"], writes=[f"fir_o{fb}"], tag="flcv")
                    em.dma("gpsimd", rr(ll_out)[oc, :, (c - 1) * TC:c * TC], fir_o[:, fb, :],
                           reads=[f"fir_o{fb}"], tag="llst")
        em.finish(block)
    return nc


def run_l1a(inputs, x2):
    def wT(name):
        return f32c(inputs[name]).T.copy()
    def packc(name):
        return f32c(inputs[name]).reshape(8, 128, 4).transpose(1, 0, 2).copy()
    fir_s = f32c(inputs["L1_attn_fir_short_w"]).reshape(1024, 5)
    fir_l = f32c(inputs["L1_attn_fir_long_w"]).reshape(1024, 64)
    in_maps = []
    for c in range(NCORES):
        b, sg = divmod(c, 4)
        t0 = sg * SEG
        x2e = np.zeros((D, 512 + SEG), BF)
        x2e[:, 512:] = x2[b, :, t0:t0 + SEG]
        if sg > 0:
            x2e[:, :512] = x2[b, :, t0 - 512:t0]
        in_maps.append(dict(
            x2e=x2e,
            qT=wT("L1_attn_q_proj_w"), kT=wT("L1_attn_k_proj_w"), vT=wT("L1_attn_v_proj_w"),
            bT=wT("L1_attn_b_proj_w"),
            qcw=packc("L1_attn_q_conv_w"), kcw=packc("L1_attn_k_conv_w"), vcw=packc("L1_attn_v_conv_w"),
            fs=fir_s.reshape(8, 128, 5).transpose(1, 0, 2).copy(),
            fl=fir_l.reshape(8, 128, 64).transpose(1, 0, 2).copy(),
        ))
    if "l1a" not in _BUILD_CACHE:
        _BUILD_CACHE["l1a"] = build_l1a()
    res = run_bass_kernel_spmd(_BUILD_CACHE["l1a"], in_maps, core_ids=list(range(NCORES)))
    out = {}
    for nm, width in (("q", D), ("k", D), ("v", D), ("ls", D), ("ll", D)):
        full = np.empty((B, width, L), BF)
        for c in range(NCORES):
            b, sg = divmod(c, 4)
            full[b, :, sg * SEG:(sg + 1) * SEG] = res.results[c][nm]
        out[nm] = full
    beta = np.empty((B, 4, L), np.float32)
    for c in range(NCORES):
        b, sg = divmod(c, 4)
        beta[b, :, sg * SEG:(sg + 1) * SEG] = res.results[c]["beta"]
    out["beta"] = beta
    return out


def build_l1b(nsc=None):
    nc = bass.Bass("TRN2", target_bir_lowering=False)
    DH = 256
    SC = 512
    NSC = nsc if nsc is not None else L // SC
    LL = NSC * SC
    NTT = 4

    q_in = nc.dram_tensor("q", [DH, L], BF16, kind="ExternalInput").ap()
    k_in = nc.dram_tensor("k", [DH, L], BF16, kind="ExternalInput").ap()
    v_in = nc.dram_tensor("v", [DH, L], BF16, kind="ExternalInput").ap()
    beta_in = nc.dram_tensor("beta", [128, L // 128], F32, kind="ExternalInput").ap()
    id_in = nc.dram_tensor("identb", [128, 128], BF16, kind="ExternalInput").ap()
    maskl_in = nc.dram_tensor("maskl", [128, 128], BF16, kind="ExternalInput").ap()
    masku_in = nc.dram_tensor("masku", [128, 128], BF16, kind="ExternalInput").ap()
    ones_in = nc.dram_tensor("ones", [128, 1], F32, kind="ExternalInput").ap()
    onesr_in = nc.dram_tensor("onesr", [1, 128], F32, kind="ExternalInput").ap()

    do_out = nc.dram_tensor("dout", [DH, L], BF16, kind="ExternalOutput").ap()
    do_tm = nc.dram_tensor("do_tm", [L, DH], BF16, kind="Internal").ap()

    es = ExitStack()
    T = lambda *a: es.enter_context(nc.sbuf_tensor(*a))
    P = lambda *a: es.enter_context(nc.psum_tensor(*a))

    id_sb = T("id_sb", [128, 128], BF16)
    maskl_sb = T("maskl_sb", [128, 128], BF16)
    masku_sb = T("masku_sb", [128, 128], BF16)
    ones_sb = T("ones_sb", [128, 1], F32)
    onesr_sb = T("onesr_sb", [1, 128], F32)
    beta_sb = T("beta_sb", [128, L // 128], F32)
    negb_sb = T("negb_sb", [128, L // 128], F32)

    qf = T("qf", [128, 2, SC], BF16)
    kf = T("kf", [128, 2, SC], BF16)
    vf = T("vf", [128, 2, SC], BF16)
    qn = T("qn", [128, 2, SC], BF16)
    kn = T("kn", [128, 2, SC], BF16)
    sqs = T("sqs", [128, SC], F32)
    row1 = T("row1", [1, SC], F32)
    ktm = T("ktm", [128, NTT, DH], BF16)
    vtm = T("vtm", [128, NTT, DH], BF16)
    kbtm = T("kbtm", [128, NTT, DH], BF16)
    A_sb = T("A_sb", [128, 128], BF16)
    Bt_sb = T("Bt_sb", [128, 128], BF16)
    A2_sb = T("A2_sb", [128, 128], BF16)
    B2_sb = T("B2_sb", [128, 128], BF16)
    A4_sb = T("A4_sb", [128, 128], BF16)
    B4_sb = T("B4_sb", [128, 128], BF16)
    A8_sb = T("A8_sb", [128, 128], BF16)
    B8_sb = T("B8_sb", [128, 128], BF16)
    A16_sb = T("A16_sb", [128, 128], BF16)
    N_sb = T("N_sb", [128, 128], BF16)
    N2_sb = T("N2_sb", [128, 128], BF16)
    utm = T("utm", [128, NTT, DH], BF16)
    wfm = T("wfm", [128, 2, NTT, 128], BF16)
    attnT = T("attnT", [128, NTT, 128], BF16)
    S_bf = T("S_bf", [128, 2, DH], BF16)
    uadj = T("uadj", [128, DH], BF16)
    do_sb = T("do_sb", [32, DH], BF16)
    ttm = T("ttm", [128, DH], BF16)
    dofm = T("dofm", [128, 2, 128], BF16)

    ps_S = [P(f"ps_S{i}", [128, DH], F32) for i in range(2)]
    ps_pw = [P(f"ps_pw{i}", [128, 128], F32) for i in range(2)]
    ps_tr = P("ps_tr", [128, 128], BF16)
    ps_u = P("ps_u", [128, DH], F32)
    ps_o = P("ps_o", [32, DH], F32)
    ps_bc = P("ps_bc", [128, SC], F32)

    em = Em(nc, es)
    block = es.enter_context(nc.Block())
    with es:
        em.dma("sync", id_sb[:], id_in, writes=["id_sb"], sem="wl2")
        em.dma("sync", maskl_sb[:], maskl_in, writes=["maskl_sb"])
        em.dma("sync", masku_sb[:], masku_in, writes=["masku_sb"])
        em.dma("sync", ones_sb[:], ones_in, writes=["ones_sb"], sem="wl2")
        em.dma("sync", onesr_sb[:], onesr_in, writes=["onesr_sb"], sem="wl2")
        em.dma("sync", beta_sb[:], beta_in, writes=["beta_sb"])
        em.op("vector", lambda e: e.tensor_scalar_mul(out=negb_sb[:], in0=beta_sb[:], scalar1=-1.0),
              reads=["beta_sb"], writes=["negb_sb"], tag="negb")

        for sc in range(NSC):
            t0 = sc * SC
            for cc in range(2):
                em.dma("sync", qf[:, cc, :], q_in[cc * 128:(cc + 1) * 128, t0:t0 + SC],
                       writes=[f"qf{cc}"], tag="qld")
                em.dma("sync", kf[:, cc, :], k_in[cc * 128:(cc + 1) * 128, t0:t0 + SC],
                       writes=[f"kf{cc}"], tag="kld")
                em.dma("sync", vf[:, cc, :], v_in[cc * 128:(cc + 1) * 128, t0:t0 + SC],
                       writes=[f"vf{cc}"], tag="vld")
            for nm, src, dst in (("q", qf, qn), ("k", kf, kn)):
                for cc in range(2):
                    em.op("scalar", lambda e, cc=cc, src=src: e.square(out=sqs[:], in_=src[:, cc, :]),
                          reads=[f"{nm}f{cc}"], writes=["sqs"], tag="l2sq")
                    em.op("tensor",
                          lambda e, cc=cc: e.matmul(ps_bc[0:1, :], ones_sb[:], sqs[:],
                                                    start=(cc == 0), stop=(cc == 1)),
                          reads=["sqs", "ones_sb"], writes=["ps_row", "ps_bcast"], tag="l2mm")
                em.op("vector", lambda e: e.tensor_copy(out=row1[:], in_=ps_bc[0:1, :]),
                      reads=["ps_row"], writes=["row1"], tag="l2cp")
                em.op("scalar", lambda e: e.activation(out=row1[:], in_=row1[:], func=AF.Ln),
                      reads=["row1"], writes=["row1"], tag="l2ln")
                em.op("scalar", lambda e: e.activation(out=row1[:], in_=row1[:], func=AF.Exp, scale=-0.5),
                      reads=["row1"], writes=["row1"], tag="l2ex")
                em.op("tensor", lambda e: e.matmul(ps_bc[:, :], onesr_sb[:], row1[:], start=True, stop=True),
                      reads=["row1", "onesr_sb"], writes=["ps_row", "ps_bcast"], tag="l2bc")
                for cc in range(2):
                    em.op("vector",
                          lambda e, cc=cc, src=src, dst=dst: e.tensor_mul(
                              out=dst[:, cc, :], in0=src[:, cc, :], in1=ps_bc[:, :]),
                          reads=[f"{nm}f{cc}", "ps_bcast"], writes=[f"{nm}n{cc}"], tag="l2ap")
            for tt in range(NTT):
                for cc in range(2):
                    em.op("tensor",
                          lambda e, tt=tt, cc=cc: e.transpose(
                              ps_tr[:, :], kn[:, cc, tt * 128:(tt + 1) * 128], id_sb[:]),
                          reads=[f"kn{cc}", "id_sb"], writes=["ps_tr"], tag="ktr")
                    em.op("scalar",
                          lambda e, tt=tt, cc=cc: e.copy(out=ktm[:, tt, cc * 128:(cc + 1) * 128],
                                                         in_=ps_tr[:, :]),
                          reads=["ps_tr"], writes=[f"ktm{tt}"], tag="ktev")
                    em.op("tensor",
                          lambda e, tt=tt, cc=cc: e.transpose(
                              ps_tr[:, :], vf[:, cc, tt * 128:(tt + 1) * 128], id_sb[:]),
                          reads=[f"vf{cc}", "id_sb"], writes=["ps_tr"], tag="vtr")
                    em.op("scalar",
                          lambda e, tt=tt, cc=cc: e.copy(out=vtm[:, tt, cc * 128:(cc + 1) * 128],
                                                         in_=ps_tr[:, :]),
                          reads=["ps_tr"], writes=[f"vtm{tt}"], tag="vtev")
                em.op("vector",
                      lambda e, tt=tt, sc=sc: e.tensor_scalar_mul(
                          out=kbtm[:, tt, :], in0=ktm[:, tt, :],
                          scalar1=beta_sb[:, sc * NTT + tt:sc * NTT + tt + 1]),
                      reads=[f"ktm{tt}", "beta_sb"], writes=[f"kbtm{tt}"], tag="kb")
                em.op("vector",
                      lambda e, tt=tt, sc=sc: e.tensor_scalar_mul(
                          out=vtm[:, tt, :], in0=vtm[:, tt, :],
                          scalar1=beta_sb[:, sc * NTT + tt:sc * NTT + tt + 1]),
                      reads=[f"vtm{tt}", "beta_sb"], writes=[f"vtm{tt}"], tag="vb")
            for tt in range(NTT):
                for cc in range(2):
                    em.op("tensor",
                          lambda e, tt=tt, cc=cc: e.matmul(
                              ps_pw[0][:, :], kn[:, cc, tt * 128:(tt + 1) * 128],
                              kn[:, cc, tt * 128:(tt + 1) * 128],
                              start=(cc == 0), stop=(cc == 1)),
                          reads=[f"kn{cc}"], writes=(["ps_pw0"] if cc in (0, 1) else []), tag="gram")
                em.op("vector",
                      lambda e, tt=tt, sc=sc: e.scalar_tensor_tensor(
                          out=A_sb[:], in0=ps_pw[0][:, :],
                          scalar=negb_sb[:, sc * NTT + tt:sc * NTT + tt + 1],
                          in1=maskl_sb[:], op0=ALU.mult, op1=ALU.mult),
                      reads=["ps_pw0", "negb_sb", "maskl_sb"], writes=["A_sb"], tag="Amask")
                em.op("tensor", lambda e: e.transpose(ps_tr[:, :], A_sb[:], id_sb[:]),
                      reads=["A_sb", "id_sb"], writes=["ps_tr"], tag="Atr")
                em.op("scalar", lambda e: e.copy(out=Bt_sb[:], in_=ps_tr[:, :]),
                      reads=["ps_tr"], writes=["Bt_sb"], tag="Btev")
                for (lt, rh, dst, nm) in ((Bt_sb, A_sb, A2_sb, "A2"), (A_sb, Bt_sb, B2_sb, "B2"),
                                          (B2_sb, A2_sb, A4_sb, "A4"), (A2_sb, B2_sb, B4_sb, "B4"),
                                          (B4_sb, A4_sb, A8_sb, "A8"), (A4_sb, B4_sb, B8_sb, "B8"),
                                          (B8_sb, A8_sb, A16_sb, "A16")):
                    pw = 0 if nm in ("A2", "A4", "A8", "A16") else 1
                    em.op("tensor",
                          lambda e, lt=lt, rh=rh, pw=pw: e.matmul(ps_pw[pw][:, :], lt[:], rh[:],
                                                                  start=True, stop=True),
                          reads=["A_sb", "Bt_sb", "A2_sb", "B2_sb", "A4_sb", "B4_sb", "A8_sb", "B8_sb"],
                          writes=[f"ps_pw{pw}"], tag=f"pw{nm}")
                    em.op("scalar", lambda e, dst=dst, pw=pw: e.copy(out=dst[:], in_=ps_pw[pw][:, :]),
                          reads=[f"ps_pw{pw}"], writes=[f"{nm}_sb"], tag=f"ev{nm}")
                em.op("vector", lambda e: e.tensor_add(out=N_sb[:], in0=Bt_sb[:], in1=id_sb[:]),
                      reads=["Bt_sb", "id_sb"], writes=["N_sb"], tag="N1")
                for i, Ak in enumerate((A2_sb, A4_sb, A8_sb, A16_sb)):
                    nmA = ("A2_sb", "A4_sb", "A8_sb", "A16_sb")[i]
                    em.op("tensor",
                          lambda e, Ak=Ak: e.matmul(ps_pw[0][:, :], Ak[:], N_sb[:], start=True, stop=False),
                          reads=[nmA, "N_sb"], writes=["ps_pw0"], tag="Np")
                    em.op("tensor",
                          lambda e: e.matmul(ps_pw[0][:, :], id_sb[:], N_sb[:], start=False, stop=True),
                          reads=["id_sb", "N_sb"], writes=["ps_pw0"], tag="Ni")
                    em.op("scalar", lambda e: e.copy(out=N_sb[:], in_=ps_pw[0][:, :]),
                          reads=["ps_pw0"], writes=["N_sb"], tag="Nev")
                em.op("tensor",
                      lambda e, tt=tt: e.matmul(ps_u[:, :], N_sb[:], vtm[:, tt, :], start=True, stop=True),
                      reads=["N_sb", f"vtm{tt}"], writes=["ps_u"], tag="umm")
                em.op("scalar", lambda e, tt=tt: e.copy(out=utm[:, tt, :], in_=ps_u[:, :]),
                      reads=["ps_u"], writes=[f"utm{tt}"], tag="uev")
                for cc in range(2):
                    em.op("tensor",
                          lambda e, tt=tt, cc=cc: e.matmul(
                              ps_u[:, 0:128], kbtm[:, tt, cc * 128:(cc + 1) * 128], N_sb[:],
                              start=True, stop=True),
                          reads=[f"kbtm{tt}", "N_sb"], writes=["ps_u"], tag="wmm")
                    em.op("scalar", lambda e, tt=tt, cc=cc: e.copy(out=wfm[:, cc, tt, :], in_=ps_u[:, 0:128]),
                          reads=["ps_u"], writes=[f"wfm{tt}"], tag="wev")
                for cc in range(2):
                    em.op("tensor",
                          lambda e, tt=tt, cc=cc: e.matmul(
                              ps_pw[0][:, :], kn[:, cc, tt * 128:(tt + 1) * 128],
                              qn[:, cc, tt * 128:(tt + 1) * 128],
                              start=(cc == 0), stop=(cc == 1)),
                          reads=[f"kn{cc}", f"qn{cc}"], writes=(["ps_pw0"] if cc in (0, 1) else []), tag="atmm")
                em.op("vector",
                      lambda e, tt=tt: e.tensor_mul(out=attnT[:, tt, :], in0=ps_pw[0][:, :], in1=masku_sb[:]),
                      reads=["ps_pw0", "masku_sb"], writes=[f"attnT{tt}"], tag="atmask")
            for j in range(16):
                jj = sc * 16 + j
                tt = j // 4
                p = (j % 4) * 32
                first = (jj == 0)
                if not first:
                    for cc in range(2):
                        em.op("vector",
                              lambda e, cc=cc: e.tensor_copy(out=S_bf[:, cc, :], in_=ps_S[cc][:, :]),
                              reads=[f"ps_S{cc}"], writes=[f"S_bf{cc}"], tag="Scp")
                    for cc in range(2):
                        em.op("tensor",
                              lambda e, tt=tt, p=p, cc=cc: e.matmul(
                                  ps_o[:, :], wfm[:, cc, tt, p:p + 32], S_bf[:, cc, :],
                                  start=(cc == 0), stop=(cc == 1)),
                              reads=[f"wfm{tt}", f"S_bf{cc}"], writes=(["ps_o"] if cc in (0, 1) else []),
                              tag="wS")
                    em.op("vector",
                          lambda e, tt=tt, p=p: e.tensor_sub(
                              out=uadj[p:p + 32, :], in0=utm[p:p + 32, tt, :], in1=ps_o[:, :]),
                          reads=[f"utm{tt}", "ps_o"], writes=["uadj"], tag="uadj")
                else:
                    em.op("vector",
                          lambda e, tt=tt, p=p: e.tensor_copy(out=uadj[p:p + 32, :], in_=utm[p:p + 32, tt, :]),
                          reads=[f"utm{tt}"], writes=["uadj"], tag="uadj0")
                if not first:
                    for cc in range(2):
                        em.op("tensor",
                              lambda e, tt=tt, p=p, cc=cc: e.matmul(
                                  ps_o[:, :], qn[:, cc, (tt * 128 + p):(tt * 128 + p + 32)], S_bf[:, cc, :],
                                  start=(cc == 0), stop=False),
                              reads=[f"qn{cc}", f"S_bf{cc}"], writes=(["ps_o"] if cc == 0 else []), tag="qS")
                    em.op("tensor",
                          lambda e, tt=tt, p=p: e.matmul(
                              ps_o[:, :], attnT[p:p + 32, tt, p:p + 32], uadj[p:p + 32, :],
                              start=False, stop=True, tile_position=(p, 0)),
                          reads=[f"attnT{tt}", "uadj"], writes=["ps_o"], tag="attnterm")
                else:
                    em.op("tensor",
                          lambda e, tt=tt, p=p: e.matmul(
                              ps_o[:, :], attnT[p:p + 32, tt, p:p + 32], uadj[p:p + 32, :],
                              start=True, stop=True, tile_position=(p, 0)),
                          reads=[f"attnT{tt}", "uadj"], writes=["ps_o"], tag="attnterm0")
                em.op("scalar", lambda e: e.copy(out=do_sb[:], in_=ps_o[:, :]),
                      reads=["ps_o"], writes=["do_sb"], tag="doev")
                em.dma("sync", do_tm[jj * 32:(jj + 1) * 32, :], do_sb[:], reads=["do_sb"], tag="dost")
                for cc in range(2):
                    em.op("tensor",
                          lambda e, tt=tt, p=p, cc=cc, first=first: e.matmul(
                              ps_S[cc][:, :], ktm[p:p + 32, tt, cc * 128:(cc + 1) * 128], uadj[p:p + 32, :],
                              start=first, stop=False, tile_position=(p, 0), skip_group_check=True),
                          reads=[f"ktm{tt}", "uadj"], writes=[f"ps_S{cc}"], tag="Supd")
        all_dost = ("d:do_sb", em.cnt.get("d:do_sb", 0))
        for tt in range(LL // 128):
            em.dma("sync", ttm[:], do_tm[tt * 128:(tt + 1) * 128, :], writes=["ttm"],
                   deps=[all_dost], tag="dold")
            for cc in range(2):
                em.op("tensor",
                      lambda e, cc=cc: e.transpose(ps_tr[:, :], ttm[:, cc * 128:(cc + 1) * 128], id_sb[:]),
                      reads=["ttm", "id_sb"], writes=["ps_tr"], tag="dtr")
                em.op("scalar", lambda e, cc=cc: e.copy(out=dofm[:, cc, :], in_=ps_tr[:, :]),
                      reads=["ps_tr"], writes=[f"dofm{cc}"], tag="dofm")
                em.dma("scalar", do_out[cc * 128:(cc + 1) * 128, tt * 128:(tt + 1) * 128], dofm[:, cc, :],
                       reads=[f"dofm{cc}"], tag="dofst")
        em.finish(block)
    return nc


def run_l1b(l1a_out):
    maskl = np.zeros((128, 128), np.float32)
    masku = np.zeros((128, 128), np.float32)
    for g in range(4):
        for i in range(32):
            for jx in range(32):
                if jx < i:
                    maskl[g * 32 + i, g * 32 + jx] = 1.0
                if i < jx:
                    masku[g * 32 + i, g * 32 + jx] = 1.0
    ident = np.eye(128, dtype=np.float32).astype(BF)
    in_maps = []
    for c in range(NCORES):
        b, h = divmod(c, 4)
        sl = slice(h * 256, (h + 1) * 256)
        in_maps.append(dict(
            q=bf16c(l1a_out["q"][b, sl]),
            k=bf16c(l1a_out["k"][b, sl]),
            v=bf16c(l1a_out["v"][b, sl]),
            beta=f32c(l1a_out["beta"][b, h].reshape(L // 128, 128).T),
            identb=ident,
            maskl=maskl.astype(BF), masku=masku.astype(BF),
            ones=np.ones((128, 1), np.float32),
            onesr=np.ones((1, 128), np.float32),
        ))
    if "l1b" not in _BUILD_CACHE:
        _BUILD_CACHE["l1b"] = build_l1b()
    res = run_bass_kernel_spmd(_BUILD_CACHE["l1b"], in_maps, core_ids=list(range(NCORES)))
    do = np.empty((B, D, L), BF)
    for c in range(NCORES):
        b, h = divmod(c, 4)
        do[b, h * 256:(h + 1) * 256] = res.results[c]["dout"]
    return do


def build_l1c():
    nc = bass.Bass("TRN2", target_bir_lowering=False)
    TC = 512
    NCH = SEG // TC

    x2_in = nc.dram_tensor("x2", [D, SEG], BF16, kind="ExternalInput").ap()
    ls_in = nc.dram_tensor("ls", [D, SEG], BF16, kind="ExternalInput").ap()
    ll_in = nc.dram_tensor("ll", [D, SEG], BF16, kind="ExternalInput").ap()
    do_in = nc.dram_tensor("do", [D, SEG], BF16, kind="ExternalInput").ap()
    v_in = nc.dram_tensor("v", [D, SEG], BF16, kind="ExternalInput").ap()
    w1aT_in = nc.dram_tensor("w1aT", [D, D], F32, kind="ExternalInput").ap()
    w1bT_in = nc.dram_tensor("w1bT", [16, D], F32, kind="ExternalInput").ap()
    fb1_in = nc.dram_tensor("fb1", [128, 8], F32, kind="ExternalInput").ap()
    w2T_in = nc.dram_tensor("w2T", [D, 4], F32, kind="ExternalInput").ap()
    fb2_in = nc.dram_tensor("fb2", [4, 1], F32, kind="ExternalInput").ap()
    lt_in = nc.dram_tensor("lt", [1, 1], F32, kind="ExternalInput").ap()
    rss_in = nc.dram_tensor("rss", [128, 1], F32, kind="ExternalInput").ap()
    rsl_in = nc.dram_tensor("rsl", [128, 1], F32, kind="ExternalInput").ap()
    onw_in = nc.dram_tensor("onw", [128, 2, 1], F32, kind="ExternalInput").ap()
    opT_in = nc.dram_tensor("opT", [D, D], F32, kind="ExternalInput").ap()
    id_in = nc.dram_tensor("identb", [128, 128], BF16, kind="ExternalInput").ap()
    ones_in = nc.dram_tensor("ones", [128, 1], F32, kind="ExternalInput").ap()
    onesb_in = nc.dram_tensor("onesb", [128, 1], BF16, kind="ExternalInput").ap()
    onesr_in = nc.dram_tensor("onesr", [1, 128], F32, kind="ExternalInput").ap()

    x3_out = nc.dram_tensor("x3", [D, SEG], BF16, kind="ExternalOutput").ap()

    es = ExitStack()
    T = lambda *a: es.enter_context(nc.sbuf_tensor(*a))
    P = lambda *a: es.enter_context(nc.psum_tensor(*a))

    w1a = T("w1a", [128, 8, D], BF16)
    w1b = T("w1b", [16, D], BF16)
    fb1 = T("fb1_s", [128, 8], F32)
    w2T = T("w2T_s", [128, 8, 4], BF16)
    fb2 = T("fb2_s", [4, 1], F32)
    opT = T("opT_s", [128, 8, D], BF16)
    id_sb = T("id_sb", [128, 128], BF16)
    ones_sb = T("ones_sb", [128, 1], F32)
    onesb_sb = T("onesb_sb", [128, 1], BF16)
    onesr_sb = T("onesr_sb", [1, 128], F32)
    rss_sb = T("rss_sb", [128, 1], F32)
    rsl_sb = T("rsl_sb", [128, 1], F32)
    onw_sb = T("onw_sb", [128, 2, 1], F32)
    lt_sb = T("lt_sb", [1, 1], F32)
    it_sb = T("it_sb", [1, 1], F32)
    itb_sb = T("itb_sb", [128, 1], F32)
    idf = T("idf", [128, 128], F32)

    comp = {n: T(f"c_{n}", [128, 8, TC], BF16) for n in ("x2", "ls", "ll", "do", "v")}
    sqf = T("sqf", [128, TC], F32)
    row1 = T("row1", [1, TC], F32)
    row2 = T("row2", [1, TC], F32)
    statb = T("statb", [16, 4, TC], BF16)
    shared = T("shared", [128, 8, TC], F32)
    tmpf = T("tmpf", [128, TC], F32)
    hdn = T("hdn", [128, TC], BF16)
    lgh = T("lgh", [4, TC], F32)
    fwT = T("fwT", [128, 4], F32)
    mxc = T("mxc", [128, 1], F32)
    smc = T("smc", [128, 1], F32)
    fvt = T("fvt", [4, 128], F32)
    rcomp = T("rcomp", [1, 4, 4, TC], F32)
    o_sb = T("o_sb", [128, TC], F32)
    on_sb = T("on_sb", [128, 8, TC], BF16)
    s_sb = T("s_sb", [128, 8, TC], F32)
    row_m = T("row_m", [1, TC], F32)
    row_v = T("row_v", [1, TC], F32)
    row_t = T("row_t", [1, TC], F32)
    row_b = T("row_b", [1, TC], F32)
    row_a = T("row_a", [1, TC], F32)
    x3_sb = T("x3_sb", [128, TC], BF16)

    ps_r1 = P("ps_r1", [1, TC], F32)
    ps_r2 = P("ps_r2", [1, TC], F32)
    ps_b1 = P("ps_b1", [128, TC], F32)
    ps_b2 = P("ps_b2", [128, TC], F32)
    ps_lg = P("ps_lg", [4, TC], F32)
    ps_tr = P("ps_tr", [128, 128], F32)
    ps_mm = [P(f"ps_mm{i}", [128, TC], F32) for i in range(2)]

    em = Em(nc, es)
    block = es.enter_context(nc.Block())
    with es:
        em.dma("gpsimd", w1a[:], w1aT_in.rearrange("(k p) n -> p k n", p=128), writes=["w1a"], sem="wl")
        em.dma("gpsimd", w1b[:], w1bT_in, writes=["w1b"], sem="wl")
        em.dma("sync", fb1[:], fb1_in, writes=["fb1"], sem="wl2")
        em.dma("gpsimd", w2T[:], w2T_in.rearrange("(k p) n -> p k n", p=128), writes=["w2T"], sem="wl")
        em.dma("sync", fb2[:], fb2_in, writes=["fb2"], sem="wl2")
        em.dma("gpsimd", opT[:], opT_in.rearrange("(k p) n -> p k n", p=128), writes=["opT"], sem="wl")
        em.dma("sync", id_sb[:], id_in, writes=["id_sb"], sem="wl2")
        em.dma("sync", ones_sb[:], ones_in, writes=["ones_sb"], sem="wl2")
        em.dma("sync", onesb_sb[:], onesb_in, writes=["onesb_sb"], sem="wl2")
        em.dma("sync", onesr_sb[:], onesr_in, writes=["onesr_sb"], sem="wl2")
        em.dma("sync", rss_sb[:], rss_in, writes=["rss_sb"], sem="wl2")
        em.dma("sync", rsl_sb[:], rsl_in, writes=["rsl_sb"], sem="wl2")
        em.dma("sync", onw_sb[:], onw_in, writes=["onw_sb"], sem="wl2")
        em.dma("sync", lt_sb[:], lt_in, writes=["lt_sb"], sem="wl2")
        em.coalesce(["w1a", "w1b", "w2T", "opT"], "wl")
        em.coalesce(["fb1", "fb2", "id_sb", "ones_sb", "onesb_sb", "onesr_sb",
                     "rss_sb", "rsl_sb", "onw_sb", "lt_sb"], "wl2")
        em.op("scalar", lambda e: e.activation(out=it_sb[:], in_=lt_sb[:], func=AF.Abs),
              reads=["lt_sb"], writes=["it1"], tag="t1")
        em.op("scalar", lambda e: e.activation(out=it_sb[:], in_=it_sb[:], func=AF.Exp, scale=-1.0),
              reads=["it1"], writes=["it2"], tag="t2")
        em.op("scalar", lambda e: e.activation(out=it_sb[:], in_=it_sb[:], func=AF.Ln, bias=1.0),
              reads=["it2"], writes=["it3"], tag="t3")
        em.op("vector", lambda e: e.tensor_scalar_max(out=lt_sb[:], in0=lt_sb[:], scalar1=0.0),
              reads=["lt_sb"], writes=["lt_r", "lt_sb"], tag="t4")
        em.op("vector", lambda e: e.tensor_add(out=it_sb[:], in0=it_sb[:], in1=lt_sb[:]),
              reads=["it3", "lt_r"], writes=["it4"], tag="t5")
        em.op("vector", lambda e: e.tensor_scalar_add(out=it_sb[:], in0=it_sb[:], scalar1=1e-4),
              reads=["it4"], writes=["it5"], tag="t6")
        em.op("vector", lambda e: e.reciprocal(out=it_sb[:], in_=it_sb[:]),
              reads=["it5"], writes=["it6"], tag="t7")
        em.op("vector", lambda e: e.tensor_copy(out=idf[:], in_=id_sb[:]),
              reads=["id_sb"], writes=["idf"], tag="idf")
        em.op("tensor", lambda e: e.matmul(ps_b1[:, 0:1], onesr_sb[:], it_sb[:], start=True, stop=True),
              reads=["it6", "onesr_sb"], writes=["ps_b1"], tag="t8")
        em.op("vector", lambda e: e.tensor_copy(out=itb_sb[:], in_=ps_b1[:, 0:1]),
              reads=["ps_b1"], writes=["itb_sb"], tag="t9")

        names = ("ls", "ll", "do", "v")
        for c in range(NCH):
            t0 = c * TC
            for nm in ("x2", "ls", "ll", "do", "v"):
                src = {"x2": x2_in, "ls": ls_in, "ll": ll_in, "do": do_in, "v": v_in}[nm]
                for dc in range(8):
                    em.dma("sync", comp[nm][:, dc, :], rr(src)[dc, :, t0:t0 + TC],
                           writes=[f"{nm}_{dc}"], sem="comps", tag=nm + "ld")
            em.coalesce([f"{nm}_{dc}" for nm in ("x2", "ls", "ll", "do", "v") for dc in range(8)], "comps")
            for ti, nm in enumerate(names):
                for h in range(4):
                    for cc in (2 * h, 2 * h + 1):
                        em.op("tensor",
                              lambda e, nm=nm, cc=cc, h=h: e.matmul(
                                  ps_r1[:, :], onesb_sb[:], comp[nm][:, cc, :],
                                  start=(cc == 2 * h), stop=(cc == 2 * h + 1)),
                              reads=[f"{nm}_{cc}", "onesb_sb"], writes=["ps_r1"], tag="st_s")
                    for cc in (2 * h, 2 * h + 1):
                        em.op("scalar",
                              lambda e, nm=nm, cc=cc: e.square(out=sqf[:], in_=comp[nm][:, cc, :]),
                              reads=[f"{nm}_{cc}"], writes=["sqf"], tag="st_sq")
                        em.op("tensor",
                              lambda e, cc=cc, h=h: e.matmul(ps_r2[:, :], ones_sb[:], sqf[:],
                                                             start=(cc == 2 * h), stop=(cc == 2 * h + 1)),
                              reads=["sqf", "ones_sb"], writes=["ps_r2"], tag="st_sqm")
                    em.op("vector", lambda e: e.tensor_scalar_mul(out=row1[:], in0=ps_r1[:, :], scalar1=1.0 / 256),
                          reads=["ps_r1"], writes=["row1"], tag="st_m")
                    em.dma("gpsimd", statb[4 * ti:4 * ti + 1, h, :], row1[:],
                           reads=["row1"], writes=[f"stat{ti}0_{h}"], sem="st_r1", tag="strow0")
                    em.op("vector", lambda e: e.tensor_scalar_mul(out=row2[:], in0=ps_r2[:, :], scalar1=1.0 / 256),
                          reads=["ps_r2"], writes=["row2"], tag="st_msq")
                    em.op("vector", lambda e: e.tensor_mul(out=row_t[:], in0=row1[:], in1=row1[:]),
                          reads=["row1"], writes=["row_t"], tag="st_m2")
                    em.op("vector", lambda e: e.tensor_sub(out=row2[:], in0=row2[:], in1=row_t[:]),
                          reads=["row2", "row_t"], writes=["row2v", "row2"], tag="st_v")
                    em.dma("gpsimd", statb[4 * ti + 1:4 * ti + 2, h, :], row2[:],
                           reads=["row2v", "row2"], writes=[f"stat{ti}1_{h}"], sem="st_r2", tag="strow1")
                    em.op("scalar", lambda e: e.sqrt(out=row2[:], in_=ps_r2[:, :]),
                          reads=["ps_r2"], writes=["row2l", "row2"], tag="st_l2")
                    em.dma("gpsimd", statb[4 * ti + 3:4 * ti + 4, h, :], row2[:],
                           reads=["row2l", "row2"], writes=[f"stat{ti}3_{h}"], sem="st_r2", tag="strow3")
                    for cc in (2 * h, 2 * h + 1):
                        em.op("scalar",
                              lambda e, nm=nm, cc=cc: e.activation(out=sqf[:], in_=comp[nm][:, cc, :], func=AF.Abs),
                              reads=[f"{nm}_{cc}"], writes=["sqf"], tag="st_ab")
                        em.op("tensor",
                              lambda e, cc=cc, h=h: e.matmul(ps_r2[:, :], ones_sb[:], sqf[:],
                                                             start=(cc == 2 * h), stop=(cc == 2 * h + 1)),
                              reads=["sqf", "ones_sb"], writes=["ps_r2"], tag="st_abm")
                    em.op("vector", lambda e: e.tensor_scalar_mul(out=row2[:], in0=ps_r2[:, :], scalar1=1.0 / 256),
                          reads=["ps_r2"], writes=["row2a", "row2"], tag="st_am")
                    em.dma("gpsimd", statb[4 * ti + 2:4 * ti + 3, h, :], row2[:],
                           reads=["row2a", "row2"], writes=[f"stat{ti}2_{h}"], sem="st_r2", tag="strow2")
            em.coalesce([f"stat{ti}0_{h}" for ti in range(4) for h in range(4)], "st_r1")
            em.coalesce([f"stat{ti}{si}_{h}" for ti in range(4) for si in (1, 2, 3) for h in range(4)], "st_r2")
            for hc in range(8):
                pb = hc % 2
                for kc in range(8):
                    em.op("tensor",
                          lambda e, hc=hc, kc=kc, pb=pb: e.matmul(
                              ps_mm[pb][:, :], w1a[:, kc, hc * 128:(hc + 1) * 128], comp["x2"][:, kc, :],
                              start=(kc == 0), stop=(kc == 7)),
                          reads=["w1a", f"x2_{kc}"], writes=([f"ps_mm{pb}"] if kc in (0, 7) else []), tag="sh")
                em.op("vector",
                      lambda e, hc=hc, pb=pb: e.tensor_copy(out=shared[:, hc, :], in_=ps_mm[pb][:, :]),
                      reads=[f"ps_mm{pb}"], writes=[f"shared{hc}"], tag="shev")
            allstats = [f"stat{ti}{si}_{h}" for ti in range(4) for si in range(4) for h in range(4)]
            for h in range(4):
                for hc in range(8):
                    em.op("tensor",
                          lambda e, h=h, hc=hc: e.matmul(
                              ps_b2[:, :], w1b[:, hc * 128:(hc + 1) * 128], statb[:, h, :],
                              start=True, stop=True),
                          reads=allstats + ["w1b"], writes=["ps_b2"], tag="gb")
                    em.op("vector",
                          lambda e, hc=hc: e.tensor_add(out=tmpf[:], in0=ps_b2[:, :], in1=shared[:, hc, :]),
                          reads=["ps_b2", f"shared{hc}"], writes=["tmpf"], tag="gadd")
                    em.op("scalar",
                          lambda e, hc=hc: e.activation(out=hdn[:], in_=tmpf[:], func=AF.Gelu,
                                                        bias=fb1[:, hc:hc + 1]),
                          reads=["tmpf", "fb1"], writes=["hdn"], tag="gelu")
                    em.op("tensor",
                          lambda e, hc=hc: e.matmul(ps_lg[:, :], w2T[:, hc, :], hdn[:],
                                                    start=(hc == 0), stop=(hc == 7)),
                          reads=["hdn", "w2T"], writes=(["ps_lg"] if hc in (0, 7) else []), tag="lgm")
                em.op("scalar", lambda e: e.activation(out=lgh[:], in_=ps_lg[:, :], func=AF.Identity,
                                                       bias=fb2[:, 0:1]),
                      reads=["ps_lg", "fb2"], writes=["lgh"], tag="lgev")
                for tt in range(4):
                    em.op("tensor",
                          lambda e, tt=tt: e.transpose(ps_tr[:, 0:4], lgh[0:4, tt * 128:(tt + 1) * 128], idf[0:4, 0:4]),
                          reads=["lgh", "id_sb"], writes=["ps_tr"], tag="lgtr")
                    em.op("vector", lambda e: e.tensor_copy(out=fwT[:], in_=ps_tr[:, 0:4]),
                          reads=["ps_tr"], writes=["fwT"], tag="fwcp")
                    em.op("vector", lambda e: e.tensor_reduce(out=mxc[:], in_=fwT[:], axis=AX.X, op=ALU.max),
                          reads=["fwT"], writes=["mxc"], tag="fwmax")
                    em.op("vector", lambda e: e.tensor_scalar(out=fwT[:], in0=fwT[:], scalar1=mxc[:, 0:1],
                                                              scalar2=None, op0=ALU.subtract),
                          reads=["fwT", "mxc"], writes=["fwT"], tag="fwsub")
                    em.op("scalar", lambda e: e.activation(out=fwT[:], in_=fwT[:], func=AF.Exp,
                                                           scale=itb_sb[:, 0:1]),
                          reads=["fwT", "itb_sb"], writes=["fwT"], tag="fwexp")
                    em.op("vector", lambda e: e.tensor_reduce(out=smc[:], in_=fwT[:], axis=AX.X, op=ALU.add),
                          reads=["fwT"], writes=["smc"], tag="fwsum")
                    em.op("vector", lambda e: e.reciprocal(out=smc[:], in_=smc[:]),
                          reads=["smc"], writes=["smc"], tag="fwrec")
                    em.op("vector", lambda e: e.tensor_scalar(out=fwT[:], in0=fwT[:], scalar1=smc[:, 0:1],
                                                              scalar2=None, op0=ALU.mult),
                          reads=["fwT", "smc"], writes=["fwT"], tag="fwmul")
                    em.op("vector", lambda e: e.tensor_add(out=fwT[:, 0:1], in0=fwT[:, 0:1], in1=rss_sb[:]),
                          reads=["fwT", "rss_sb"], writes=["fwT"], tag="fwrss")
                    em.op("vector", lambda e: e.tensor_add(out=fwT[:, 1:2], in0=fwT[:, 1:2], in1=rsl_sb[:]),
                          reads=["fwT", "rsl_sb"], writes=["fwT"], tag="fwrsl")
                    em.op("vector", lambda e: e.tensor_copy(out=tmpf[:, 0:4], in_=fwT[:]),
                          reads=["fwT"], writes=["tmp4"], tag="fw2t")
                    em.op("tensor",
                          lambda e: e.transpose(ps_tr[0:4, 0:128], tmpf[:, 0:4], idf[:]),
                          reads=["tmp4", "idf"], writes=["ps_tr"], tag="fwtr")
                    em.op("scalar", lambda e: e.copy(out=fvt[:], in_=ps_tr[0:4, 0:128]),
                          reads=["ps_tr"], writes=["fvt"], tag="fvev")
                    for comp_i in range(4):
                        em.dma("scalar", rcomp[0:1, comp_i, h, tt * 128:(tt + 1) * 128],
                               fvt[comp_i:comp_i + 1, :], reads=["fvt"],
                               writes=[f"rc{comp_i}_{h}"], sem="rc", tag="rcst")
            em.coalesce([f"rc{ci}_{h}" for ci in range(4) for h in range(4)], "rc")
            for h in range(4):
                for cc in (2 * h, 2 * h + 1):
                    first = True
                    for comp_i, nm in enumerate(names):
                        em.op("tensor",
                              lambda e, comp_i=comp_i, h=h: e.matmul(
                                  ps_b2[:, :], onesr_sb[:], rcomp[0:1, comp_i, h, :],
                                  start=True, stop=True),
                              reads=[f"rc{comp_i}_{h}", "onesr_sb"], writes=["ps_b2"], tag="fbb")
                        if first:
                            em.op("vector",
                                  lambda e, nm=nm, cc=cc: e.tensor_mul(out=o_sb[:], in0=comp[nm][:, cc, :],
                                                                       in1=ps_b2[:, :]),
                                  reads=[f"{nm}_{cc}", "ps_b2"], writes=["o_sb"], tag="omul")
                            first = False
                        else:
                            em.op("vector",
                                  lambda e, nm=nm, cc=cc: e.tensor_mul(out=tmpf[:], in0=comp[nm][:, cc, :],
                                                                       in1=ps_b2[:, :]),
                                  reads=[f"{nm}_{cc}", "ps_b2"], writes=["tmpf"], tag="omul2")
                            em.op("vector",
                                  lambda e: e.tensor_add(out=o_sb[:], in0=o_sb[:], in1=tmpf[:]),
                                  reads=["o_sb", "tmpf"], writes=["o_sb"], tag="oacc")
                    em.op("scalar", lambda e, cc=cc: e.square(out=sqf[:], in_=o_sb[:]),
                          reads=["o_sb"], writes=["sqf"], tag="osq")
                    em.op("tensor",
                          lambda e, cc=cc, h=h: e.matmul(ps_r2[:, :], ones_sb[:], sqf[:],
                                                         start=(cc == 2 * h), stop=(cc == 2 * h + 1)),
                          reads=["sqf", "ones_sb"], writes=["ps_r2"], tag="osqm")
                    em.op("vector",
                          lambda e, cc=cc: e.tensor_copy(out=s_sb[:, cc, :], in_=o_sb[:]),
                          reads=["o_sb"], writes=[f"oh{cc}", f"s{cc}"], tag="okeep")
                em.op("vector", lambda e: e.tensor_scalar_mul(out=row1[:], in0=ps_r2[:, :], scalar1=1.0 / 256),
                      reads=["ps_r2"], writes=["row1"], tag="oms")
                em.op("vector", lambda e: e.tensor_scalar_add(out=row1[:], in0=row1[:], scalar1=1e-5),
                      reads=["row1"], writes=["row1"], tag="oeps")
                em.op("scalar", lambda e: e.activation(out=row1[:], in_=row1[:], func=AF.Ln),
                      reads=["row1"], writes=["row1"], tag="oln")
                em.op("scalar", lambda e: e.activation(out=row1[:], in_=row1[:], func=AF.Exp, scale=-0.5),
                      reads=["row1"], writes=["row1"], tag="oex")
                em.op("tensor", lambda e: e.matmul(ps_b2[:, :], onesr_sb[:], row1[:], start=True, stop=True),
                      reads=["row1", "onesr_sb"], writes=["ps_b2"], tag="obc")
                for cc in (2 * h, 2 * h + 1):
                    em.op("vector",
                          lambda e, cc=cc: e.scalar_tensor_tensor(
                              out=on_sb[:, cc, :], in0=s_sb[:, cc, :], scalar=onw_sb[:, cc % 2, 0:1],
                              in1=ps_b2[:, :], op0=ALU.mult, op1=ALU.mult),
                          reads=[f"oh{cc}", "onw_sb", "ps_b2"], writes=[f"on{cc}"], tag="onorm")
            for dc in range(8):
                pb = dc % 2
                for kc in range(8):
                    em.op("tensor",
                          lambda e, dc=dc, kc=kc, pb=pb: e.matmul(
                              ps_mm[pb][:, :], opT[:, kc, dc * 128:(dc + 1) * 128], on_sb[:, kc, :],
                              start=(kc == 0), stop=(kc == 7)),
                          reads=["opT", f"on{kc}"], writes=([f"ps_mm{pb}"] if kc in (0, 7) else []), tag="op")
                em.op("vector",
                      lambda e, dc=dc, pb=pb: e.tensor_add(out=s_sb[:, dc, :], in0=ps_mm[pb][:, :],
                                                           in1=comp["x2"][:, dc, :]),
                      reads=[f"ps_mm{pb}", f"x2_{dc}"], writes=[f"s{dc}"], tag="resadd")
            for dc in range(8):
                em.op("tensor",
                      lambda e, dc=dc: e.matmul(ps_r1[:, :], ones_sb[:], s_sb[:, dc, :],
                                                start=(dc == 0), stop=(dc == 7)),
                      reads=[f"s{dc}", "ones_sb"], writes=(["ps_r1"] if dc in (0, 7) else []), tag="lsum")
            for dc in range(8):
                em.op("scalar",
                      lambda e, dc=dc: e.square(out=sqf[:], in_=s_sb[:, dc, :]),
                      reads=[f"s{dc}"], writes=["sqf"], tag="lsq")
                em.op("tensor",
                      lambda e, dc=dc: e.matmul(ps_r2[:, :], ones_sb[:], sqf[:],
                                                start=(dc == 0), stop=(dc == 7)),
                      reads=["sqf", "ones_sb"], writes=(["ps_r2"] if dc in (0, 7) else []), tag="lsqm")
            em.op("vector", lambda e: e.tensor_scalar_mul(out=row_m[:], in0=ps_r1[:, :], scalar1=1.0 / D),
                  reads=["ps_r1"], writes=["row_m"], tag="lm")
            em.op("vector", lambda e: e.tensor_scalar_mul(out=row_v[:], in0=ps_r2[:, :], scalar1=1.0 / D),
                  reads=["ps_r2"], writes=["row_v"], tag="lv")
            em.op("vector", lambda e: e.tensor_mul(out=row_t[:], in0=row_m[:], in1=row_m[:]),
                  reads=["row_m"], writes=["row_t"], tag="lm2")
            em.op("vector", lambda e: e.tensor_sub(out=row_v[:], in0=row_v[:], in1=row_t[:]),
                  reads=["row_v", "row_t"], writes=["row_v"], tag="lvar")
            em.op("vector", lambda e: e.tensor_scalar_add(out=row_v[:], in0=row_v[:], scalar1=1e-5),
                  reads=["row_v"], writes=["row_v"], tag="leps")
            em.op("scalar", lambda e: e.activation(out=row_b[:], in_=row_v[:], func=AF.Ln),
                  reads=["row_v"], writes=["row_b"], tag="lln")
            em.op("scalar", lambda e: e.activation(out=row_b[:], in_=row_b[:], func=AF.Exp, scale=-0.5),
                  reads=["row_b"], writes=["row_b"], tag="lrstd")
            em.op("vector", lambda e: e.tensor_mul(out=row_a[:], in0=row_m[:], in1=row_b[:]),
                  reads=["row_m", "row_b"], writes=["row_a"], tag="la1")
            em.op("vector", lambda e: e.tensor_scalar_mul(out=row_a[:], in0=row_a[:], scalar1=-1.0),
                  reads=["row_a"], writes=["row_a"], tag="la2")
            em.op("tensor", lambda e: e.matmul(ps_b1[:, :], onesr_sb[:], row_b[:], start=True, stop=True),
                  reads=["row_b", "onesr_sb"], writes=["ps_b1"], tag="lbcb")
            em.op("tensor", lambda e: e.matmul(ps_b2[:, :], onesr_sb[:], row_a[:], start=True, stop=True),
                  reads=["row_a", "onesr_sb"], writes=["ps_b2"], tag="lbca")
            for dc in range(8):
                em.op("vector",
                      lambda e, dc=dc: e.tensor_mul(out=tmpf[:], in0=s_sb[:, dc, :], in1=ps_b1[:, :]),
                      reads=[f"s{dc}", "ps_b1"], writes=["tmpf"], tag="lapb")
                em.op("vector",
                      lambda e, dc=dc: e.tensor_add(out=x3_sb[:], in0=tmpf[:], in1=ps_b2[:, :]),
                      reads=["tmpf", "ps_b2"], writes=["x3_sb"], tag="lapa")
                em.dma("sync", rr(x3_out)[dc, :, t0:t0 + TC], x3_sb[:],
                       reads=["x3_sb"], tag="x3st")
        em.finish(block)
    return nc


def run_l1c(inputs, x2, l1a_out, do):
    fw1 = f32c(inputs["L1_attn_fuse_w1"])
    in_common = dict(
        w1aT=fw1[:, :D].T.copy(),
        w1bT=fw1[:, D:].T.copy(),
        fb1=f32c(inputs["L1_attn_fuse_b1"]).reshape(8, 128).T.copy(),
        w2T=f32c(inputs["L1_attn_fuse_w2"]).T.copy(),
        fb2=f32c(inputs["L1_attn_fuse_b2"]).reshape(4, 1),
        lt=f32c(inputs["L1_attn_logit_temp"]).reshape(1, 1),
        rss=np.broadcast_to(f32c(inputs["L1_attn_res_scale_short"]).reshape(1, 1), (128, 1)).copy(),
        rsl=np.broadcast_to(f32c(inputs["L1_attn_res_scale_long"]).reshape(1, 1), (128, 1)).copy(),
        onw=f32c(inputs["L1_attn_o_norm_w"]).reshape(2, 128, 1).transpose(1, 0, 2).copy(),
        opT=f32c(inputs["L1_attn_o_proj_w"]).T.copy(),
        identb=np.eye(128, dtype=np.float32).astype(BF),
        ones=np.ones((128, 1), np.float32),
        onesb=np.ones((128, 1), np.float32).astype(BF),
        onesr=np.ones((1, 128), np.float32),
    )
    in_maps = []
    for c in range(NCORES):
        b, sg = divmod(c, 4)
        t0 = sg * SEG
        m = dict(in_common)
        m.update(
            x2=bf16c(x2[b, :, t0:t0 + SEG]),
            ls=bf16c(l1a_out["ls"][b, :, t0:t0 + SEG]),
            ll=bf16c(l1a_out["ll"][b, :, t0:t0 + SEG]),
            do=bf16c(do[b, :, t0:t0 + SEG]),
            v=bf16c(l1a_out["v"][b, :, t0:t0 + SEG]),
        )
        in_maps.append(m)
    if "l1c" not in _BUILD_CACHE:
        _BUILD_CACHE["l1c"] = build_l1c()
    res = run_bass_kernel_spmd(_BUILD_CACHE["l1c"], in_maps, core_ids=list(range(NCORES)))
    x3 = np.empty((B, D, L), BF)
    for c in range(NCORES):
        b, sg = divmod(c, 4)
        x3[b, :, sg * SEG:(sg + 1) * SEG] = res.results[c]["x3"]
    return x3


def build_l1f():
    nc = bass.Bass("TRN2", target_bir_lowering=False)
    TC = 512
    NCH = SEG // TC

    x3_in = nc.dram_tensor("x3", [D, SEG], BF16, kind="ExternalInput").ap()
    xp_in = nc.dram_tensor("xp", [128, 8, 1], BF16, kind="ExternalInput").ap()
    wrT_in = nc.dram_tensor("wrT", [D, D], F32, kind="ExternalInput").ap()
    wkT_in = nc.dram_tensor("wkT", [D, D], F32, kind="ExternalInput").ap()
    wvT_in = nc.dram_tensor("wvT", [D, D], F32, kind="ExternalInput").ap()
    mur_in = nc.dram_tensor("mur", [128, 8, 1], F32, kind="ExternalInput").ap()
    muk_in = nc.dram_tensor("muk", [128, 8, 1], F32, kind="ExternalInput").ap()
    ones_in = nc.dram_tensor("ones", [128, 1], F32, kind="ExternalInput").ap()
    onesr_in = nc.dram_tensor("onesr", [1, 128], F32, kind="ExternalInput").ap()

    x4_out = nc.dram_tensor("x4", [D, SEG], F32, kind="ExternalOutput").ap()

    es = ExitStack()
    T = lambda *a: es.enter_context(nc.sbuf_tensor(*a))
    P = lambda *a: es.enter_context(nc.psum_tensor(*a))

    wr = T("wr_s", [128, 8, D], BF16)
    wk = T("wk_s", [128, 8, D], BF16)
    wv = T("wv_s", [128, 8, D], BF16)
    mur = T("mur_s", [128, 8, 1], F32)
    muk = T("muk_s", [128, 8, 1], F32)
    ones_sb = T("ones_sb", [128, 1], F32)
    onesr_sb = T("onesr_sb", [1, 128], F32)
    ext = T("ext", [128, 8, 1 + TC], BF16)
    dcol = T("dcol", [128, TC], BF16)
    xr_sb = T("xr_sb", [128, 8, TC], BF16)
    xk_sb = T("xk_sb", [128, 8, TC], BF16)
    r_sb = T("r_sb", [128, 8, TC], BF16)
    k2_sb = T("k2_sb", [128, 8, TC], BF16)
    ka = T("ka", [128, TC], F32)
    tmpf = T("tmpf", [128, TC], F32)
    sqf = T("sqf", [128, TC], F32)
    s_sb = T("s_sb", [128, 8, TC], F32)
    row_m = T("row_m", [1, TC], F32)
    row_v = T("row_v", [1, TC], F32)
    row_t = T("row_t", [1, TC], F32)
    row_b = T("row_b", [1, TC], F32)
    row_a = T("row_a", [1, TC], F32)
    x4_sb = T("x4_sb", [128, TC], F32)
    onesb_sb = T("onesb_sb", [128, 1], BF16)

    ps_mm = [P(f"ps_mm{i}", [128, TC], F32) for i in range(2)]
    ps_r1 = P("ps_r1", [1, TC], F32)
    ps_r2 = P("ps_r2", [1, TC], F32)
    ps_a = P("ps_a", [128, TC], F32)
    ps_b = P("ps_b", [128, TC], F32)

    em = Em(nc, es)
    block = es.enter_context(nc.Block())
    with es:
        em.dma("gpsimd", wr[:], wrT_in.rearrange("(k p) n -> p k n", p=128), writes=["wr"], sem="wl")
        em.dma("gpsimd", wk[:], wkT_in.rearrange("(k p) n -> p k n", p=128), writes=["wk"], sem="wl")
        em.dma("gpsimd", wv[:], wvT_in.rearrange("(k p) n -> p k n", p=128), writes=["wv"], sem="wl")
        em.dma("sync", mur[:], mur_in, writes=["mur"], sem="wl2")
        em.dma("sync", muk[:], muk_in, writes=["muk"], sem="wl2")
        em.dma("sync", ones_sb[:], ones_in, writes=["ones_sb"], sem="wl2")
        em.dma("sync", onesr_sb[:], onesr_in, writes=["onesr_sb"], sem="wl2")
        em.dma("sync", ext[:, :, 0:1], xp_in, writes=[f"extc{dc}" for dc in range(8)], sem="wl2",
               allow_slow_non_contiguous=True)
        em.coalesce(["wr", "wk", "wv"], "wl")
        em.coalesce(["mur", "muk", "ones_sb", "onesr_sb"] + [f"extc{dc}" for dc in range(8)], "wl2")
        em.op("vector", lambda e: e.memset(onesb_sb[:], 1.0), writes=["onesb_sb"], tag="ob")

        for c in range(NCH):
            t0 = c * TC
            for dc in range(8):
                em.dma("sync", ext[:, dc, 1:], rr(x3_in)[dc, :, t0:t0 + TC],
                       reads=[f"extc{dc}"], writes=[f"ext{dc}"], sem="xl", tag="xld")
            em.coalesce([f"ext{dc}" for dc in range(8)], "xl")
            for dc in range(8):
                em.op("vector",
                      lambda e, dc=dc: e.tensor_sub(out=dcol[:], in0=ext[:, dc, 1:], in1=ext[:, dc, 0:TC]),
                      reads=[f"ext{dc}", f"extc{dc}"], writes=["dcol"], tag="d")
                em.op("vector",
                      lambda e, dc=dc: e.scalar_tensor_tensor(
                          out=xr_sb[:, dc, :], in0=dcol[:], scalar=mur[:, dc, 0:1],
                          in1=ext[:, dc, 0:TC], op0=ALU.mult, op1=ALU.add),
                      reads=["dcol", "mur", f"ext{dc}"], writes=[f"xr{dc}"], tag="xr")
                em.op("vector",
                      lambda e, dc=dc: e.scalar_tensor_tensor(
                          out=xk_sb[:, dc, :], in0=dcol[:], scalar=muk[:, dc, 0:1],
                          in1=ext[:, dc, 0:TC], op0=ALU.mult, op1=ALU.add),
                      reads=["dcol", "muk", f"ext{dc}"], writes=[f"xk{dc}"], tag="xk")
                em.op("vector",
                      lambda e, dc=dc: e.tensor_copy(out=ext[:, dc, 0:1], in_=ext[:, dc, TC:TC + 1]),
                      reads=[f"ext{dc}", f"xr{dc}", f"xk{dc}"], writes=[f"extc{dc}"], tag="tail")
            for dc in range(8):
                pb = dc % 2
                for kc in range(8):
                    em.op("tensor",
                          lambda e, dc=dc, kc=kc, pb=pb: e.matmul(
                              ps_mm[pb][:, :], wr[:, kc, dc * 128:(dc + 1) * 128], xr_sb[:, kc, :],
                              start=(kc == 0), stop=(kc == 7)),
                          reads=["wr", f"xr{kc}"], writes=([f"ps_mm{pb}"] if kc in (0, 7) else []), tag="rmm")
                em.op("scalar",
                      lambda e, dc=dc, pb=pb: e.activation(out=r_sb[:, dc, :], in_=ps_mm[pb][:, :],
                                                           func=AF.Sigmoid),
                      reads=[f"ps_mm{pb}"], writes=[f"r{dc}"], tag="rsig")
            for dc in range(8):
                pb = dc % 2
                for kc in range(8):
                    em.op("tensor",
                          lambda e, dc=dc, kc=kc, pb=pb: e.matmul(
                              ps_mm[pb][:, :], wk[:, kc, dc * 128:(dc + 1) * 128], xk_sb[:, kc, :],
                              start=(kc == 0), stop=(kc == 7)),
                          reads=["wk", f"xk{kc}"], writes=([f"ps_mm{pb}"] if kc in (0, 7) else []), tag="kmm")
                em.op("scalar",
                      lambda e, dc=dc, pb=pb: e.activation(out=ka[:], in_=ps_mm[pb][:, :], func=AF.Relu),
                      reads=[f"ps_mm{pb}"], writes=["ka"], tag="krelu")
                em.op("scalar",
                      lambda e, dc=dc: e.square(out=k2_sb[:, dc, :], in_=ka[:]),
                      reads=["ka"], writes=[f"k2{dc}"], tag="ksq")
            for dc in range(8):
                pb = dc % 2
                for kc in range(8):
                    em.op("tensor",
                          lambda e, dc=dc, kc=kc, pb=pb: e.matmul(
                              ps_mm[pb][:, :], wv[:, kc, dc * 128:(dc + 1) * 128], k2_sb[:, kc, :],
                              start=(kc == 0), stop=(kc == 7)),
                          reads=["wv", f"k2{kc}"], writes=([f"ps_mm{pb}"] if kc in (0, 7) else []), tag="vmm")
                em.op("vector",
                      lambda e, dc=dc, pb=pb: e.tensor_mul(out=tmpf[:], in0=r_sb[:, dc, :], in1=ps_mm[pb][:, :]),
                      reads=[f"r{dc}", f"ps_mm{pb}"], writes=["tmpf"], tag="rkv")
                em.op("vector",
                      lambda e, dc=dc: e.tensor_add(out=s_sb[:, dc, :], in0=tmpf[:], in1=ext[:, dc, 1:]),
                      reads=["tmpf", f"ext{dc}"], writes=[f"s{dc}"], tag="res")
            for dc in range(8):
                em.op("tensor",
                      lambda e, dc=dc: e.matmul(ps_r1[:, :], ones_sb[:], s_sb[:, dc, :],
                                                start=(dc == 0), stop=(dc == 7)),
                      reads=[f"s{dc}", "ones_sb"], writes=(["ps_r1"] if dc in (0, 7) else []), tag="lsum")
            for dc in range(8):
                em.op("scalar",
                      lambda e, dc=dc: e.square(out=sqf[:], in_=s_sb[:, dc, :]),
                      reads=[f"s{dc}"], writes=["sqf"], tag="lsq")
                em.op("tensor",
                      lambda e, dc=dc: e.matmul(ps_r2[:, :], ones_sb[:], sqf[:],
                                                start=(dc == 0), stop=(dc == 7)),
                      reads=["sqf", "ones_sb"], writes=(["ps_r2"] if dc in (0, 7) else []), tag="lsqm")
            em.op("vector", lambda e: e.tensor_scalar_mul(out=row_m[:], in0=ps_r1[:, :], scalar1=1.0 / D),
                  reads=["ps_r1"], writes=["row_m"], tag="lm")
            em.op("vector", lambda e: e.tensor_scalar_mul(out=row_v[:], in0=ps_r2[:, :], scalar1=1.0 / D),
                  reads=["ps_r2"], writes=["row_v"], tag="lv")
            em.op("vector", lambda e: e.tensor_mul(out=row_t[:], in0=row_m[:], in1=row_m[:]),
                  reads=["row_m"], writes=["row_t"], tag="lm2")
            em.op("vector", lambda e: e.tensor_sub(out=row_v[:], in0=row_v[:], in1=row_t[:]),
                  reads=["row_v", "row_t"], writes=["row_v"], tag="lvar")
            em.op("vector", lambda e: e.tensor_scalar_add(out=row_v[:], in0=row_v[:], scalar1=1e-5),
                  reads=["row_v"], writes=["row_v"], tag="leps")
            em.op("scalar", lambda e: e.activation(out=row_b[:], in_=row_v[:], func=AF.Ln),
                  reads=["row_v"], writes=["row_b"], tag="lln")
            em.op("scalar", lambda e: e.activation(out=row_b[:], in_=row_b[:], func=AF.Exp, scale=-0.5),
                  reads=["row_b"], writes=["row_b"], tag="lrstd")
            em.op("vector", lambda e: e.tensor_mul(out=row_a[:], in0=row_m[:], in1=row_b[:]),
                  reads=["row_m", "row_b"], writes=["row_a"], tag="la1")
            em.op("vector", lambda e: e.tensor_scalar_mul(out=row_a[:], in0=row_a[:], scalar1=-1.0),
                  reads=["row_a"], writes=["row_a"], tag="la2")
            em.op("tensor", lambda e: e.matmul(ps_b[:, :], onesr_sb[:], row_b[:], start=True, stop=True),
                  reads=["row_b", "onesr_sb"], writes=["ps_b"], tag="lbcb")
            em.op("tensor", lambda e: e.matmul(ps_a[:, :], onesr_sb[:], row_a[:], start=True, stop=True),
                  reads=["row_a", "onesr_sb"], writes=["ps_a"], tag="lbca")
            for dc in range(8):
                em.op("vector",
                      lambda e, dc=dc: e.tensor_mul(out=x4_sb[:], in0=s_sb[:, dc, :], in1=ps_b[:, :]),
                      reads=[f"s{dc}", "ps_b"], writes=["x4_sb"], tag="lapb")
                em.op("vector",
                      lambda e, dc=dc: e.tensor_add(out=x4_sb[:], in0=x4_sb[:], in1=ps_a[:, :]),
                      reads=["x4_sb", "ps_a"], writes=["x4_sb"], tag="lapa")
                em.dma("sync", rr(x4_out)[dc, :, t0:t0 + TC], x4_sb[:],
                       reads=["x4_sb"], sem="x4s", tag="x4st")
        em.finish(block)
    return nc


def run_l1f(inputs, x3):
    in_common = dict(
        wrT=f32c(inputs["L1_ffn_W_r"]).T.copy(),
        wkT=f32c(inputs["L1_ffn_W_k"]).T.copy(),
        wvT=f32c(inputs["L1_ffn_W_v"]).T.copy(),
        mur=f32c(inputs["L1_ffn_mu_r"]).reshape(8, 128, 1).transpose(1, 0, 2).copy(),
        muk=f32c(inputs["L1_ffn_mu_k"]).reshape(8, 128, 1).transpose(1, 0, 2).copy(),
        ones=np.ones((128, 1), np.float32),
        onesr=np.ones((1, 128), np.float32),
    )
    in_maps = []
    for c in range(NCORES):
        b, sg = divmod(c, 4)
        t0 = sg * SEG
        xp = np.zeros((128, 8, 1), BF)
        if sg > 0:
            xp = np.ascontiguousarray(x3[b, :, t0 - 1].reshape(8, 128, 1).transpose(1, 0, 2))
        m = dict(in_common)
        m.update(x3=bf16c(x3[b, :, t0:t0 + SEG]), xp=xp)
        in_maps.append(m)
    if "l1f" not in _BUILD_CACHE:
        _BUILD_CACHE["l1f"] = build_l1f()
    res = run_bass_kernel_spmd(_BUILD_CACHE["l1f"], in_maps, core_ids=list(range(NCORES)))
    x4 = np.empty((B, D, L), np.float32)
    for c in range(NCORES):
        b, sg = divmod(c, 4)
        x4[b, :, sg * SEG:(sg + 1) * SEG] = res.results[c]["x4"]
    return x4


def kernel(**inputs):
    l0a = run_l0a(inputs)
    y = run_l0scan(l0a, inputs)
    x1 = run_l0b(inputs, l0a, y)
    x2 = run_l0f(inputs, x1)
    l1a = run_l1a(inputs, x2)
    do = run_l1b(l1a)
    x3 = run_l1c(inputs, x2, l1a, do)
    x4 = run_l1f(inputs, x3)
    return np.ascontiguousarray(x4.transpose(0, 2, 1))


# version 02
# speedup vs baseline: 1.1610x; 1.1610x over previous
import numpy as np
import ml_dtypes
from contextlib import ExitStack

import concourse.bass as bass
import concourse.mybir as mybir
from concourse.bass_utils import run_bass_kernel_spmd
from concourse.bass import IndirectOffsetOnAxis

F32 = mybir.dt.float32
BF16 = mybir.dt.bfloat16
I32 = mybir.dt.int32
AF = mybir.ActivationFunctionType
ALU = mybir.AluOpType
AX = mybir.AxisListType
BF = ml_dtypes.bfloat16

NCORES = 8
B = 2
L = 16384
D = 1024
SEG = L // 4
DIN = 2048
DSTATE = 16
DTRANK = 64

def f32c(x):
    return np.ascontiguousarray(np.asarray(x), dtype=np.float32)

def bf16c(x):
    return np.ascontiguousarray(np.asarray(x).astype(BF))


class Em:
    ENGINES = ("tensor", "vector", "scalar", "gpsimd", "sync")

    def __init__(self, nc, es):
        self.nc = nc
        self.es = es
        self.q = {e: [] for e in self.ENGINES}
        self.cnt = {}
        self.sems = {}
        self.writer = {}
        self.readers = {}

    def _sem(self, key):
        if key not in self.sems:
            self.sems[key] = self.es.enter_context(self.nc.semaphore(f"s_{key}"))
            self.cnt[key] = 0
        return self.sems[key]

    def op(self, engine, fn, reads=(), writes=(), deps=(), dma=False, tag="", semkey=None):
        key = semkey if semkey is not None else engine
        self._sem(key)
        inc = 16 if dma else 1
        waits = [d for d in deps if d is not None]
        for r in reads:
            if r in self.writer:
                waits.append(self.writer[r])
        for w in writes:
            if w in self.writer:
                waits.append(self.writer[w])
            for t in self.readers.get(w, ()):
                waits.append(t)
        self.cnt[key] += inc
        tok = (key, self.cnt[key])
        for r in reads:
            self.readers.setdefault(r, []).append(tok)
        for w in writes:
            self.writer[w] = tok
            self.readers[w] = []
        wm = {}
        for (k, v) in waits:
            wm[k] = max(wm.get(k, 0), v)
        self.q[engine].append((sorted(wm.items()), fn, key, inc, tag))
        return tok

    def dma(self, engine, out, in_, reads=(), writes=(), deps=(), tag="", sem=None, **kw):
        if sem is None:
            sem = "d:" + (writes[0] if writes else (reads[0] if reads else engine))
        return self.op(engine, lambda e: e.dma_start(out=out, in_=in_, **kw),
                       reads=reads, writes=writes, deps=deps, dma=True, tag=tag,
                       semkey=sem)

    def coalesce(self, names, semkey):
        tok = (semkey, self.cnt[semkey])
        for n in names:
            self.writer[n] = tok

    def build(self, block):
        for e in self.ENGINES:
            if not self.q[e]:
                continue

            def make(e):
                def body(eng):
                    done = {}
                    for waits, fn, key, inc, tag in self.q[e]:
                        for (dk, val) in waits:
                            if done.get(dk, 0) >= val:
                                continue
                            eng.wait_ge(self._sem(dk), val)
                            done[dk] = val
                        fn(eng).then_inc(self.sems[key], inc)
                return body
            getattr(block, e)(make(e))

    def finish(self, block):
        toks = [(k, self.cnt[k]) for k in self.cnt]
        self.op("sync", lambda e: e.nop(), deps=toks, tag="final")
        self.build(block)


def rr(ap, p=128):
    return ap.rearrange("(a p) t -> a p t", p=p)


def build_l0a(nch=None, level=9):
    nc = bass.Bass("TRN2", target_bir_lowering=False)
    TC = 512
    NCH = nch if nch is not None else SEG // TC
    H = 3

    x_in = nc.dram_tensor("x", [H + SEG, D], F32, kind="ExternalInput").ap()
    w_in = nc.dram_tensor("w_in", [D, 2 * DIN], F32, kind="ExternalInput").ap()
    conv_w = nc.dram_tensor("conv_w", [128, 16, 4], F32, kind="ExternalInput").ap()
    conv_b = nc.dram_tensor("conv_b", [128, 16, 1], F32, kind="ExternalInput").ap()
    w_x = nc.dram_tensor("w_x", [DIN, 96], F32, kind="ExternalInput").ap()
    w_dt = nc.dram_tensor("w_dt", [DTRANK, DIN], F32, kind="ExternalInput").ap()
    b_dt = nc.dram_tensor("b_dt", [128, 16, 1], F32, kind="ExternalInput").ap()
    xhalo = nc.dram_tensor("xhalo", [128, 8, 3], F32, kind="ExternalInput").ap()
    ident_in = nc.dram_tensor("ident", [128, 128], BF16, kind="ExternalInput").ap()

    xs_out = nc.dram_tensor("xs", [DIN, SEG], BF16, kind="ExternalOutput").ap()
    dt_out = nc.dram_tensor("dt", [DIN, SEG], BF16, kind="ExternalOutput").ap()
    u_out = nc.dram_tensor("u", [DIN, SEG], BF16, kind="ExternalOutput").ap()
    ra_out = nc.dram_tensor("res_act", [DIN, SEG], BF16, kind="ExternalOutput").ap()
    bc_out = nc.dram_tensor("bc", [32, SEG], F32, kind="ExternalOutput").ap()

    es = ExitStack()
    T = lambda *a: es.enter_context(nc.sbuf_tensor(*a))
    P = lambda *a: es.enter_context(nc.psum_tensor(*a))

    w_sb = T("w_sb", [128, 8, 2 * DIN], BF16)
    wx_sb = T("wx_sb", [128, 16, 96], BF16)
    wdt_sb = T("wdt_sb", [DTRANK, DIN], BF16)
    cw_sb = T("cw_sb", [128, 16, 4], F32)
    cb_sb = T("cb_sb", [128, 16, 1], F32)
    bdt_sb = T("bdt_sb", [128, 16, 1], F32)
    id_sb = T("id_sb", [128, 128], BF16)

    xtok = T("xtok", [128, 4, D], F32)
    xtokb = T("xtokb", [128, 4, D], BF16)
    xT = [T(f"xT{i}", [128, 8, TC], BF16) for i in range(2)]
    xhT = T("xhT", [128, 8, H], BF16)
    xc_ext = T("xc_ext", [128, 16, H + TC], BF16)
    xs_sb = T("xs_sb", [128, 16, TC], BF16)
    dt_sb = T("dt_sb", [128, 16, TC], BF16)
    res_bf = T("res_bf", [128, 2, TC], BF16)
    xdbl_sb = T("xdbl_sb", [96, TC], F32)
    dtr_bf = T("dtr_bf", [DTRANK, TC], BF16)
    sp_t1 = T("sp_t1", [128, TC], F32)
    sp_t2 = T("sp_t2", [128, TC], F32)
    u_sb = T("u_sb", [128, 2, TC], BF16)

    ps_tr = [P(f"ps_tr{i}", [128, 128], BF16) for i in range(2)]
    ps_mm = [P(f"ps_mm{i}", [128, TC], F32) for i in range(2)]
    ps_dbl = P("ps_dbl", [96, TC], F32)
    ps_dt = [P(f"ps_dt{i}", [128, TC], F32) for i in range(2)]

    em = Em(nc, es)
    block = es.enter_context(nc.Block())
    with es:
        em.dma("gpsimd", w_sb[:], w_in.rearrange("(k p) n -> p k n", p=128), writes=["w_sb"])
        em.dma("gpsimd", wx_sb[:], w_x.rearrange("(k p) n -> p k n", p=128), writes=["wx_sb"])
        em.dma("gpsimd", wdt_sb[:], w_dt, writes=["wdt_sb"])
        em.dma("sync", cw_sb[:], conv_w, writes=["cw_sb"])
        em.dma("sync", cb_sb[:], conv_b, writes=["cb_sb"])
        em.dma("sync", bdt_sb[:], b_dt, writes=["bdt_sb"])
        em.dma("sync", id_sb[:], ident_in, writes=["id_sb"])
        if level >= 1.2:
            for kc in range(8):
                em.dma("gpsimd", xhT[:, kc, :], xhalo[:, kc, :], writes=[f"xhT{kc}"])

        for oc in range(16 if level >= 2 else 0):
            for kc in range(8):
                em.op("tensor",
                      lambda e, oc=oc, kc=kc: e.matmul(
                          ps_dt[0][:, 0:H], w_sb[:, kc, oc * 128:(oc + 1) * 128], xhT[:, kc, :],
                          start=(kc == 0), stop=(kc == 7)),
                      reads=["w_sb", f"xhT{kc}"], writes=(["ps_dt0"] if kc in (0, 7) else []),
                      tag=f"hmm{oc}")
            em.op("scalar",
                  lambda e, oc=oc: e.copy(out=xc_ext[:, oc, 0:H], in_=ps_dt[0][:, 0:H]),
                  reads=["ps_dt0"], writes=[f"ext_halo{oc}"], tag=f"hev{oc}")

        for c in range(NCH if level >= 1.4 else 0):
            buf = c % 2
            xTn = f"xT{buf}"
            evs = []
            for tt in range(4):
                row0 = H + c * TC + tt * 128
                ld = em.dma("sync", xtok[:, tt, :], x_in[row0:row0 + 128, :],
                            writes=[f"xtok{tt}"], tag=f"xld{c}_{tt}")
                cv = em.op("vector", lambda e, tt=tt: e.tensor_copy(out=xtokb[:, tt, :], in_=xtok[:, tt, :]),
                           reads=[f"xtok{tt}"], writes=[f"xtokb{tt}"], tag="xcv")
                for dc in range(8 if level >= 1.6 else 0):
                    sl = (tt * 8 + dc) % 2
                    tr = em.op("tensor",
                               lambda e, dc=dc, tt=tt, sl=sl: e.transpose(
                                   ps_tr[sl][:, :], xtokb[:, tt, dc * 128:(dc + 1) * 128], id_sb[:]),
                               reads=[f"xtokb{tt}", "id_sb"], writes=[f"ps_tr{sl}"], tag="xtr")
                    ev = em.op("scalar",
                               lambda e, dc=dc, tt=tt, b=buf, sl=sl: e.copy(
                                   out=xT[b][:, dc, tt * 128:(tt + 1) * 128], in_=ps_tr[sl][:, :]),
                               reads=[f"ps_tr{sl}"], writes=[xTn + f"_{dc}_{tt}"], tag="xev")
                    evs.append(ev)

            xT_all = [xTn + f"_{dc}_{tt}" for dc in range(8) for tt in range(4)]
            for oc in range(32 if level >= 3 else 0):
                pb = oc % 2
                if oc < 16 and level < 4:
                    continue
                if oc >= 16 and level < 5:
                    continue
                for kc in range(8):
                    em.op("tensor",
                          lambda e, oc=oc, kc=kc, b=buf, pb=pb: e.matmul(
                              ps_mm[pb][:, :], w_sb[:, kc, oc * 128:(oc + 1) * 128], xT[b][:, kc, :],
                              start=(kc == 0), stop=(kc == 7)),
                          reads=["w_sb"] + [xTn + f"_{kc}_{tt}" for tt in range(4)],
                          writes=([f"ps_mm{pb}"] if kc in (0, 7) else []), tag=f"mm{c}_{oc}")
                if oc < 16:
                    em.op("vector",
                          lambda e, oc=oc, pb=pb: e.tensor_copy(out=xc_ext[:, oc, H:], in_=ps_mm[pb][:, :]),
                          reads=[f"ps_mm{pb}", f"ext_tail{oc}"], writes=[f"ext{oc}"], tag=f"xcev{oc}")
                    t1 = em.op("vector",
                               lambda e, oc=oc: e.tensor_scalar(
                                   out=sp_t1[:], in0=xc_ext[:, oc, 0:TC],
                                   scalar1=cw_sb[:, oc, 0:1],
                                   scalar2=cb_sb[:, oc, 0:1],
                                   op0=ALU.mult, op1=ALU.add),
                               reads=[f"ext{oc}", f"ext_halo{oc}", "cw_sb", "cb_sb"],
                               writes=["sp_t1"], tag="cv0")
                    for j in range(1, 4):
                        t1 = em.op("vector",
                                   lambda e, oc=oc, j=j: e.scalar_tensor_tensor(
                                       out=sp_t1[:], in0=xc_ext[:, oc, j:j + TC],
                                       scalar=cw_sb[:, oc, j:j + 1],
                                       in1=sp_t1[:], op0=ALU.mult, op1=ALU.add),
                                   reads=[f"ext{oc}", "sp_t1"], writes=["sp_t1"], tag=f"cv{j}")
                    em.op("scalar",
                          lambda e, oc=oc: e.activation(out=xs_sb[:, oc, :], in_=sp_t1[:], func=AF.Silu),
                          reads=["sp_t1"], writes=[f"xs{oc}"], tag="sil")
                    em.op("vector",
                          lambda e, oc=oc: e.tensor_copy(out=xc_ext[:, oc, 0:H], in_=xc_ext[:, oc, TC:TC + H]),
                          reads=[f"ext{oc}"], writes=[f"ext_halo{oc}", f"ext_tail{oc}"], tag="tail")
                    em.dma("scalar", rr(xs_out)[oc, :, c * TC:(c + 1) * TC], xs_sb[:, oc, :],
                           reads=[f"xs{oc}"], tag="xs_st")
                    em.op("tensor",
                          lambda e, oc=oc: e.matmul(
                              ps_dbl[:, :], wx_sb[:, oc, :], xs_sb[:, oc, :],
                              start=(oc == 0), stop=(oc == 15)),
                          reads=["wx_sb", f"xs{oc}"], writes=(["ps_dbl"] if oc in (0, 15) else []), tag="dbl")
                else:
                    oc2 = oc - 16
                    rb = oc2 % 2
                    em.op("scalar",
                          lambda e, oc2=oc2, pb=pb, rb=rb: e.activation(
                              out=res_bf[:, rb, :], in_=ps_mm[pb][:, :], func=AF.Silu),
                          reads=[f"ps_mm{pb}"], writes=[f"res_bf{rb}"], tag="ressil")
                    em.dma("scalar", rr(ra_out)[oc2, :, c * TC:(c + 1) * TC], res_bf[:, rb, :],
                           reads=[f"res_bf{rb}"], tag="ra_st")

            em.op("scalar", lambda e: e.copy(out=xdbl_sb[:], in_=ps_dbl[:, :]),
                  reads=["ps_dbl"], writes=["xdbl_sb"], tag="dble")
            em.dma("sync", bc_out[:, c * TC:(c + 1) * TC], xdbl_sb[64:96, :],
                   reads=["xdbl_sb"], tag="bc_st")
            em.op("vector", lambda e: e.tensor_copy(out=dtr_bf[:], in_=xdbl_sb[0:64, :]),
                  reads=["xdbl_sb"], writes=["dtr_bf"], tag="dtrcv")
            for oc in range(16 if level >= 7 else 0):
                db = oc % 2
                em.op("tensor",
                      lambda e, oc=oc, db=db: e.matmul(
                          ps_dt[db][:, :], wdt_sb[:, oc * 128:(oc + 1) * 128], dtr_bf[:],
                          start=True, stop=True),
                      reads=["wdt_sb", "dtr_bf"], writes=[f"ps_dt{db}"], tag="dtmm")
                em.op("scalar",
                      lambda e, oc=oc, db=db: e.activation(out=sp_t1[:], in_=ps_dt[db][:, :], func=AF.Abs,
                                                           bias=bdt_sb[:, oc, 0:1]),
                      reads=[f"ps_dt{db}", "bdt_sb"], writes=["sp_t1"], tag="sp_abs")
                em.op("scalar",
                      lambda e: e.activation(out=sp_t1[:], in_=sp_t1[:], func=AF.Exp, scale=-1.0),
                      reads=["sp_t1"], writes=["sp_t1"], tag="sp_exp")
                em.op("scalar",
                      lambda e: e.activation(out=sp_t1[:], in_=sp_t1[:], func=AF.Ln, bias=1.0),
                      reads=["sp_t1"], writes=["sp_t1"], tag="sp_ln")
                em.op("scalar",
                      lambda e, oc=oc, db=db: e.activation(out=sp_t2[:], in_=ps_dt[db][:, :], func=AF.Relu,
                                                           bias=bdt_sb[:, oc, 0:1]),
                      reads=[f"ps_dt{db}", "bdt_sb"], writes=["sp_t2"], tag="sp_relu")
                em.op("vector",
                      lambda e, oc=oc: e.tensor_add(out=dt_sb[:, oc, :], in0=sp_t1[:], in1=sp_t2[:]),
                      reads=["sp_t1", "sp_t2"], writes=[f"dt{oc}"], tag="sp_add")
                em.dma("sync", rr(dt_out)[oc, :, c * TC:(c + 1) * TC], dt_sb[:, oc, :],
                       reads=[f"dt{oc}"], tag="dt_st")
                ub = oc % 2
                em.op("vector",
                      lambda e, oc=oc, ub=ub: e.tensor_mul(out=u_sb[:, ub, :], in0=dt_sb[:, oc, :], in1=xs_sb[:, oc, :]),
                      reads=[f"dt{oc}", f"xs{oc}"], writes=[f"u_sb{ub}"], tag="u")
                em.dma("gpsimd", rr(u_out)[oc, :, c * TC:(c + 1) * TC], u_sb[:, ub, :],
                       reads=[f"u_sb{ub}"], tag="u_st")
        em.finish(block)
    return nc


_L0A_CACHE = {}

def run_l0a(inputs):
    x = inputs["L0_attn_x"]
    ident = np.eye(128, dtype=np.float32).astype(BF)
    in_maps = []
    for c in range(NCORES):
        b, s = divmod(c, 4)
        t0 = s * SEG
        xs_seg = np.zeros((3 + SEG, D), np.float32)
        xs_seg[3:] = x[b, t0:t0 + SEG]
        if s > 0:
            xs_seg[0:3] = x[b, t0 - 3:t0]
        in_maps.append(dict(
            x=f32c(xs_seg),
            w_in=f32c(inputs["L0_attn_W_in"]),
            conv_w=f32c(inputs["L0_attn_conv_w"]).reshape(16, 128, 4).transpose(1, 0, 2),
            conv_b=f32c(inputs["L0_attn_conv_b"]).reshape(16, 128, 1).transpose(1, 0, 2),
            w_x=f32c(inputs["L0_attn_W_x"]),
            w_dt=f32c(inputs["L0_attn_W_dt"]),
            b_dt=f32c(inputs["L0_attn_b_dt"]).reshape(16, 128, 1).transpose(1, 0, 2),
            xhalo=f32c(xs_seg[0:3].T.reshape(8, 128, 3).transpose(1, 0, 2)),
            ident=ident,
        ))
    if "l0a" not in _L0A_CACHE:
        _L0A_CACHE["l0a"] = build_l0a()
    res = run_bass_kernel_spmd(_L0A_CACHE["l0a"], in_maps, core_ids=list(range(NCORES)))
    out = {}
    for name, dt_, width in [("xs", BF, DIN), ("dt", BF, DIN), ("u", BF, DIN),
                             ("res_act", BF, DIN), ("bc", np.float32, 32)]:
        full = np.empty((B, width, L), dt_)
        for c in range(NCORES):
            b, s = divmod(c, 4)
            full[b, :, s * SEG:(s + 1) * SEG] = res.results[c][name]
        out[name] = full
    return out


def kernel(**inputs):
    raise NotImplementedError


def build_l0scan():
    nc = bass.Bass("TRN2", target_bir_lowering=False)
    DSL = 512
    NT = DSL // 8
    TC = 2048
    NCH = L // TC
    NSUB = TC // 512

    dt_in = nc.dram_tensor("dt", [DSL, L], BF16, kind="ExternalInput").ap()
    u_in = nc.dram_tensor("u", [DSL, L], BF16, kind="ExternalInput").ap()
    bc_in = nc.dram_tensor("bc", [32, L], F32, kind="ExternalInput").ap()
    alog_in = nc.dram_tensor("alog", [128, NT], F32, kind="ExternalInput").ap()
    nsum_in = nc.dram_tensor("nsum", [128, 8], BF16, kind="ExternalInput").ap()
    repl_in = nc.dram_tensor("repl", [8, 128], BF16, kind="ExternalInput").ap()

    y_out = nc.dram_tensor("y", [DSL, L], BF16, kind="ExternalOutput").ap()

    es = ExitStack()
    T = lambda *a: es.enter_context(nc.sbuf_tensor(*a))
    P = lambda *a: es.enter_context(nc.psum_tensor(*a))

    apk = T("apk", [128, NT], F32)
    nsum_sb = T("nsum_sb", [128, 8], BF16)
    hcarry = T("hcarry", [128, NT], F32)
    bexp = [T(f"bexp{i}", [128, TC], BF16) for i in range(2)]
    cexp = [T(f"cexp{i}", [128, TC], BF16) for i in range(2)]
    dtp8 = [T(f"dtp8_{i}", [8, TC], BF16) for i in range(4)]
    up8 = [T(f"up8_{i}", [8, TC], BF16) for i in range(4)]
    repl_sb = T("repl_sb", [8, 128], BF16)
    dA = [T(f"dA{i}", [128, TC], F32) for i in range(4)]
    dBx = [T(f"dBx{i}", [128, TC], BF16) for i in range(4)]
    hs = [T(f"hs{i}", [128, TC], BF16) for i in range(4)]
    zs = [T(f"zs{i}", [128, TC], BF16) for i in range(4)]
    ps_y = [P(f"ps_y{i}", [8, 1024], F32) for i in range(1)]
    ps_dt = [P(f"ps_dt{i}", [128, 1024], F32) for i in range(1)]
    ps_u = [P(f"ps_u{i}", [128, 1024], F32) for i in range(2)]
    ysb = [T(f"ysb{i}", [8, TC], BF16) for i in range(4)]

    em = Em(nc, es)
    block = es.enter_context(nc.Block())
    with es:
        em.dma("sync", apk[:], alog_in, writes=["apk_raw"])
        em.dma("sync", nsum_sb[:], nsum_in, writes=["nsum_sb"])
        em.dma("sync", repl_sb[:], repl_in, writes=["repl_sb"])
        em.op("scalar", lambda e: e.activation(out=apk[:], in_=apk[:], func=AF.Exp),
              reads=["apk_raw"], writes=["apk_e"], tag="a_exp")
        em.op("vector", lambda e: e.tensor_scalar_mul(out=apk[:], in0=apk[:], scalar1=-1.0),
              reads=["apk_e"], writes=["apk"], tag="a_neg")
        em.op("vector", lambda e: e.memset(hcarry[:], 0.0), writes=["hcarry"], tag="h0")

        NT = 64
        def bcload(c):
            cb = c % 2
            t0 = c * TC
            em.dma("gpsimd", bexp[cb][:],
                   bc_in[0:16, t0:t0 + TC].unsqueeze(0).broadcast_to([8, 16, TC]),
                   writes=[f"bexp{cb}"], tag="bld")
            em.dma("gpsimd", cexp[cb][:],
                   bc_in[16:32, t0:t0 + TC].unsqueeze(0).broadcast_to([8, 16, TC]),
                   writes=[f"cexp{cb}"], tag="cld")

        def front(k):
            c, g = divmod(k, NT)
            t0 = c * TC
            cb = c % 2
            gb = k % 4
            pd = k % 2
            em.dma("scalar", dtp8[gb][:],
                   dt_in[g * 8:(g + 1) * 8, t0:t0 + TC],
                   writes=[f"dtp{gb}"], tag="dtld")
            em.dma("sync", up8[gb][:],
                   u_in[g * 8:(g + 1) * 8, t0:t0 + TC],
                   writes=[f"up8{gb}"], tag="uld")
            for half in range(2):
                for sub in range(2):
                    em.op("tensor",
                          lambda e, gb=gb, half=half, sub=sub: e.matmul(
                              ps_dt[0][:, sub * 512:(sub + 1) * 512], repl_sb[:],
                              dtp8[gb][:, half * 1024 + sub * 512:half * 1024 + (sub + 1) * 512],
                              start=True, stop=True),
                          reads=[f"dtp{gb}", "repl_sb"],
                          writes=["ps_dt0"], tag="dtrep")
                em.op("scalar",
                      lambda e, g=g, gb=gb, half=half: e.activation(
                          out=dA[gb][:, half * 1024:(half + 1) * 1024], in_=ps_dt[0][:, :], func=AF.Exp,
                          scale=apk[:, g:g + 1]),
                      reads=["ps_dt0", "apk"], writes=([f"dA{gb}"] if half == 1 else []), tag="exp")
                for sub in range(2):
                    em.op("tensor",
                          lambda e, gb=gb, pd=pd, half=half, sub=sub: e.matmul(
                              ps_u[pd][:, sub * 512:(sub + 1) * 512], repl_sb[:],
                              up8[gb][:, half * 1024 + sub * 512:half * 1024 + (sub + 1) * 512],
                              start=True, stop=True),
                          reads=[f"up8{gb}", "repl_sb"],
                          writes=[f"ps_u{pd}"], tag="urep")
                em.op("vector",
                      lambda e, gb=gb, pd=pd, cb=cb, half=half: e.tensor_mul(
                          out=dBx[gb][:, half * 1024:(half + 1) * 1024], in0=ps_u[pd][:, :],
                          in1=bexp[cb][:, half * 1024:(half + 1) * 1024]),
                      reads=[f"ps_u{pd}", f"bexp{cb}"],
                      writes=([f"dBx{gb}"] if half == 1 else []), tag="dbx")
            em.op("vector",
                  lambda e, g=g, gb=gb: e.tensor_tensor_scan(
                      hs[gb][:], dA[gb][:], dBx[gb][:], hcarry[:, g:g + 1], ALU.mult, ALU.add),
                  reads=[f"dA{gb}", f"dBx{gb}", "hcarry"], writes=[f"hs{gb}"], tag="scan")
            em.op("vector",
                  lambda e, g=g, gb=gb: e.tensor_copy(out=hcarry[:, g:g + 1], in_=hs[gb][:, TC - 1:TC]),
                  reads=[f"hs{gb}"], writes=["hcarry"], tag="hc")

        def mid(k):
            c = k // NT
            cb = c % 2
            gb = k % 4
            em.op("gpsimd",
                  lambda e, gb=gb, cb=cb: e.tensor_mul(out=zs[gb][:], in0=hs[gb][:], in1=cexp[cb][:]),
                  reads=[f"hs{gb}", f"cexp{cb}"], writes=[f"zs{gb}"], tag="z")

        def back(k):
            c, g = divmod(k, NT)
            t0 = c * TC
            gb = k % 4
            for yh in range(2):
                for sub in range(2):
                    em.op("tensor",
                          lambda e, gb=gb, yh=yh, sub=sub: e.matmul(
                              ps_y[0][:, sub * 512:(sub + 1) * 512], nsum_sb[:],
                              zs[gb][:, yh * 1024 + sub * 512:yh * 1024 + (sub + 1) * 512],
                              start=True, stop=True),
                          reads=[f"zs{gb}", "nsum_sb"],
                          writes=["ps_y0"], tag="nsum")
                em.op("scalar",
                      lambda e, gb=gb, yh=yh: e.copy(out=ysb[gb][:, yh * 1024:(yh + 1) * 1024],
                                                     in_=ps_y[0][:, :]),
                      reads=["ps_y0"], writes=[f"ysb{gb}"], tag="yev")
            em.dma("sync", y_out[g * 8:(g + 1) * 8, t0:t0 + TC],
                   ysb[gb][:], reads=[f"ysb{gb}"], tag="yst")

        TOT = NCH * NT
        for k in range(TOT):
            c, g = divmod(k, NT)
            if g == 0:
                bcload(c)
            front(k)
            if k >= 1:
                mid(k - 1)
            if k >= 2:
                back(k - 2)
        mid(TOT - 1)
        back(TOT - 2)
        back(TOT - 1)
        em.finish(block)
    return nc


_BUILD_CACHE = {}

def run_l0scan(l0a_out, inputs):
    alog = f32c(inputs["L0_attn_A_log"])
    in_maps = []
    for c in range(NCORES):
        b, dg = divmod(c, 4)
        dsl = slice(dg * 512, (dg + 1) * 512)
        al = alog[dsl].reshape(64, 8, 16).reshape(64, 128).T
        nsum = np.zeros((128, 8), np.float32)
        for di in range(8):
            nsum[di * 16:(di + 1) * 16, di] = 1.0
        repl = nsum.T.copy()
        in_maps.append(dict(
            dt=bf16c(l0a_out["dt"][b, dsl]),
            u=bf16c(l0a_out["u"][b, dsl]),
            bc=f32c(l0a_out["bc"][b]),
            alog=f32c(al),
            nsum=bf16c(nsum), repl=bf16c(repl),
        ))
    if "l0scan" not in _BUILD_CACHE:
        _BUILD_CACHE["l0scan"] = build_l0scan()
    res = run_bass_kernel_spmd(_BUILD_CACHE["l0scan"], in_maps, core_ids=list(range(NCORES)))
    y = np.empty((B, DIN, L), BF)
    for c in range(NCORES):
        b, dg = divmod(c, 4)
        y[b, dg * 512:(dg + 1) * 512] = res.results[c]["y"]
    return y


def build_l0b():
    nc = bass.Bass("TRN2", target_bir_lowering=False)
    TC = 512
    NCH = SEG // TC

    y_in = nc.dram_tensor("y", [DIN, SEG], BF16, kind="ExternalInput").ap()
    xs_in = nc.dram_tensor("xs", [DIN, SEG], BF16, kind="ExternalInput").ap()
    ra_in = nc.dram_tensor("ra", [DIN, SEG], BF16, kind="ExternalInput").ap()
    x_in = nc.dram_tensor("x", [SEG, D], F32, kind="ExternalInput").ap()
    wout_in = nc.dram_tensor("wout", [DIN, D], F32, kind="ExternalInput").ap()
    dvec_in = nc.dram_tensor("dvec", [128, 16, 1], F32, kind="ExternalInput").ap()
    ident_in = nc.dram_tensor("ident", [128, 128], BF16, kind="ExternalInput").ap()
    ones_in = nc.dram_tensor("ones", [128, 1], F32, kind="ExternalInput").ap()
    onesr_in = nc.dram_tensor("onesr", [1, 128], F32, kind="ExternalInput").ap()

    x1_out = nc.dram_tensor("x1", [D, SEG], BF16, kind="ExternalOutput").ap()

    es = ExitStack()
    T = lambda *a: es.enter_context(nc.sbuf_tensor(*a))
    P = lambda *a: es.enter_context(nc.psum_tensor(*a))

    wout_sb = T("wout_sb", [128, 16, D], BF16)
    dvec_sb = T("dvec_sb", [128, 16, 1], F32)
    id_sb = T("id_sb", [128, 128], BF16)
    ones_sb = T("ones_sb", [128, 1], F32)
    onesr_sb = T("onesr_sb", [1, 128], F32)
    ytile = T("ytile", [128, 2, TC], BF16)
    xstile = T("xstile", [128, 2, TC], BF16)
    ratile = T("ratile", [128, 2, TC], BF16)
    y2_sb = T("y2_sb", [128, 16, TC], BF16)
    tmp_bf = T("tmp_bf", [128, TC], BF16)
    xtok = T("xtok", [128, 4, D], F32)
    xtokb = T("xtokb", [128, 4, D], BF16)
    xfm = T("xfm", [128, 8, TC], BF16)
    s_sb = T("s_sb", [128, 8, TC], F32)
    sq_sb = T("sq_sb", [128, TC], F32)
    row_m = T("row_m", [1, TC], F32)
    row_v = T("row_v", [1, TC], F32)
    row_t = T("row_t", [1, TC], F32)
    row_b = T("row_b", [1, TC], F32)
    row_a = T("row_a", [1, TC], F32)
    tmp_f = T("tmp_f", [128, TC], F32)
    x1_sb = T("x1_sb", [128, 8, TC], BF16)

    ps_tr = [P(f"ps_tr{i}", [128, 128], BF16) for i in range(2)]
    ps_o = [P(f"ps_o{i}", [128, TC], F32) for i in range(2)]
    ps_sum = P("ps_sum", [1, TC], F32)
    ps_sq = P("ps_sq", [1, TC], F32)
    ps_a = P("ps_a", [128, TC], F32)
    ps_b = P("ps_b", [128, TC], F32)

    em = Em(nc, es)
    block = es.enter_context(nc.Block())
    with es:
        em.dma("gpsimd", wout_sb[:], wout_in.rearrange("(k p) n -> p k n", p=128), writes=["wout_sb"])
        em.dma("sync", dvec_sb[:], dvec_in, writes=["dvec_sb"])
        em.dma("sync", id_sb[:], ident_in, writes=["id_sb"], sem="wl2")
        em.dma("sync", ones_sb[:], ones_in, writes=["ones_sb"], sem="wl2")
        em.dma("sync", onesr_sb[:], onesr_in, writes=["onesr_sb"])

        for c in range(NCH):
            t0 = c * TC
            for oc in range(16):
                tb = oc % 2
                em.dma("sync", ytile[:, tb, :], rr(y_in)[oc, :, t0:t0 + TC], writes=[f"yt{tb}"], tag="yld")
                em.dma("sync", xstile[:, tb, :], rr(xs_in)[oc, :, t0:t0 + TC], writes=[f"xst{tb}"], tag="xsld")
                em.dma("scalar", ratile[:, tb, :], rr(ra_in)[oc, :, t0:t0 + TC], writes=[f"rat{tb}"], tag="rald")
                em.op("vector",
                      lambda e, oc=oc, tb=tb: e.scalar_tensor_tensor(
                          out=tmp_bf[:], in0=xstile[:, tb, :], scalar=dvec_sb[:, oc, 0:1],
                          in1=ytile[:, tb, :], op0=ALU.mult, op1=ALU.add),
                      reads=[f"xst{tb}", f"yt{tb}", "dvec_sb"], writes=["tmp_bf"], tag="y2a")
                em.op("vector",
                      lambda e, oc=oc, tb=tb: e.tensor_mul(out=y2_sb[:, oc, :], in0=tmp_bf[:], in1=ratile[:, tb, :]),
                      reads=["tmp_bf", f"rat{tb}"], writes=[f"y2_{oc}"], tag="y2b")
            for tt in range(4):
                row0 = t0 + tt * 128
                em.dma("sync", xtok[:, tt, :], x_in[row0:row0 + 128, :], writes=[f"xtok{tt}"], tag="xld")
                em.op("vector", lambda e, tt=tt: e.tensor_copy(out=xtokb[:, tt, :], in_=xtok[:, tt, :]),
                      reads=[f"xtok{tt}"], writes=[f"xtokb{tt}"], tag="xcv")
                for dc in range(8):
                    sl = (tt * 8 + dc) % 2
                    em.op("tensor",
                          lambda e, dc=dc, tt=tt, sl=sl: e.transpose(
                              ps_tr[sl][:, :], xtokb[:, tt, dc * 128:(dc + 1) * 128], id_sb[:]),
                          reads=[f"xtokb{tt}", "id_sb"], writes=[f"ps_tr{sl}"], tag="xtr")
                    em.op("scalar",
                          lambda e, dc=dc, tt=tt, sl=sl: e.copy(
                              out=xfm[:, dc, tt * 128:(tt + 1) * 128], in_=ps_tr[sl][:, :]),
                          reads=[f"ps_tr{sl}"], writes=[f"xfm{dc}_{tt}"], tag="xev")
            for dc in range(8):
                pb = dc % 2
                for kc in range(16):
                    em.op("tensor",
                          lambda e, dc=dc, kc=kc, pb=pb: e.matmul(
                              ps_o[pb][:, :], wout_sb[:, kc, dc * 128:(dc + 1) * 128], y2_sb[:, kc, :],
                              start=(kc == 0), stop=(kc == 15)),
                          reads=["wout_sb", f"y2_{kc}"],
                          writes=([f"ps_o{pb}"] if kc in (0, 15) else []), tag="wo")
                em.op("vector",
                      lambda e, dc=dc, pb=pb: e.tensor_add(
                          out=s_sb[:, dc, :], in0=ps_o[pb][:, :],
                          in1=xfm[:, dc, :]),
                      reads=[f"ps_o{pb}"] + [f"xfm{dc}_{tt}" for tt in range(4)],
                      writes=[f"s{dc}"], tag="resadd")
            for dc in range(8):
                em.op("tensor",
                      lambda e, dc=dc: e.matmul(ps_sum[:, :], ones_sb[:], s_sb[:, dc, :],
                                                start=(dc == 0), stop=(dc == 7)),
                      reads=[f"s{dc}", "ones_sb"], writes=(["ps_sum"] if dc in (0, 7) else []), tag="sum")
            for dc in range(8):
                em.op("scalar",
                      lambda e, dc=dc: e.square(out=sq_sb[:], in_=s_sb[:, dc, :]),
                      reads=[f"s{dc}"], writes=["sq_sb"], tag="sq")
                em.op("tensor",
                      lambda e, dc=dc: e.matmul(ps_sq[:, :], ones_sb[:], sq_sb[:],
                                                start=(dc == 0), stop=(dc == 7)),
                      reads=["sq_sb", "ones_sb"], writes=(["ps_sq"] if dc in (0, 7) else []), tag="sq_mm")
            em.op("vector", lambda e: e.tensor_scalar_mul(out=row_m[:], in0=ps_sum[:, :], scalar1=1.0 / D),
                  reads=["ps_sum"], writes=["row_m"], tag="m")
            em.op("vector", lambda e: e.tensor_scalar_mul(out=row_v[:], in0=ps_sq[:, :], scalar1=1.0 / D),
                  reads=["ps_sq"], writes=["row_v"], tag="msq")
            em.op("vector", lambda e: e.tensor_mul(out=row_t[:], in0=row_m[:], in1=row_m[:]),
                  reads=["row_m"], writes=["row_t"], tag="m2")
            em.op("vector", lambda e: e.tensor_sub(out=row_v[:], in0=row_v[:], in1=row_t[:]),
                  reads=["row_v", "row_t"], writes=["row_v"], tag="var")
            em.op("vector", lambda e: e.tensor_scalar_add(out=row_v[:], in0=row_v[:], scalar1=1e-5),
                  reads=["row_v"], writes=["row_v"], tag="eps")
            em.op("scalar", lambda e: e.activation(out=row_b[:], in_=row_v[:], func=AF.Ln),
                  reads=["row_v"], writes=["row_b"], tag="ln")
            em.op("scalar", lambda e: e.activation(out=row_b[:], in_=row_b[:], func=AF.Exp, scale=-0.5),
                  reads=["row_b"], writes=["row_b"], tag="rstd")
            em.op("vector", lambda e: e.tensor_mul(out=row_a[:], in0=row_m[:], in1=row_b[:]),
                  reads=["row_m", "row_b"], writes=["row_a"], tag="a1")
            em.op("vector", lambda e: e.tensor_scalar_mul(out=row_a[:], in0=row_a[:], scalar1=-1.0),
                  reads=["row_a"], writes=["row_a"], tag="a2")
            em.op("tensor", lambda e: e.matmul(ps_b[:, :], onesr_sb[:], row_b[:], start=True, stop=True),
                  reads=["row_b", "onesr_sb"], writes=["ps_b"], tag="bcb")
            em.op("tensor", lambda e: e.matmul(ps_a[:, :], onesr_sb[:], row_a[:], start=True, stop=True),
                  reads=["row_a", "onesr_sb"], writes=["ps_a"], tag="bca")
            for dc in range(8):
                em.op("vector",
                      lambda e, dc=dc: e.tensor_mul(out=tmp_f[:], in0=s_sb[:, dc, :], in1=ps_b[:, :]),
                      reads=[f"s{dc}", "ps_b"], writes=["tmp_f"], tag="apb")
                em.op("vector",
                      lambda e, dc=dc: e.tensor_add(out=x1_sb[:, dc, :], in0=tmp_f[:], in1=ps_a[:, :]),
                      reads=["tmp_f", "ps_a"], writes=[f"x1_{dc}"], tag="apa")
                em.dma("sync", rr(x1_out)[dc, :, t0:t0 + TC], x1_sb[:, dc, :],
                       reads=[f"x1_{dc}"], tag="x1st")
        em.finish(block)
    return nc


def run_l0b(inputs, l0a_out, y):
    wout = f32c(inputs["L0_attn_W_out"])
    ident = np.eye(128, dtype=np.float32).astype(BF)
    x = inputs["L0_attn_x"]
    in_maps = []
    for c in range(NCORES):
        b, sg = divmod(c, 4)
        t0 = sg * SEG
        in_maps.append(dict(
            y=bf16c(y[b, :, t0:t0 + SEG]),
            xs=bf16c(l0a_out["xs"][b, :, t0:t0 + SEG]),
            ra=bf16c(l0a_out["res_act"][b, :, t0:t0 + SEG]),
            x=f32c(x[b, t0:t0 + SEG]),
            wout=wout,
            dvec=f32c(inputs["L0_attn_D"]).reshape(16, 128, 1).transpose(1, 0, 2),
            ident=ident,
            ones=np.ones((128, 1), np.float32),
            onesr=np.ones((1, 128), np.float32),
        ))
    if "l0b" not in _BUILD_CACHE:
        _BUILD_CACHE["l0b"] = build_l0b()
    res = run_bass_kernel_spmd(_BUILD_CACHE["l0b"], in_maps, core_ids=list(range(NCORES)))
    x1 = np.empty((B, D, L), BF)
    for c in range(NCORES):
        b, sg = divmod(c, 4)
        x1[b, :, sg * SEG:(sg + 1) * SEG] = res.results[c]["x1"]
    return x1


def build_l0f():
    nc = bass.Bass("TRN2", target_bir_lowering=False)
    TC = 512
    NCH = SEG // TC
    G = TC // 32

    x1_in = nc.dram_tensor("x1", [D, SEG], BF16, kind="ExternalInput").ap()
    wc_in = nc.dram_tensor("wc", [D, 128], F32, kind="ExternalInput").ap()
    bcm_in = nc.dram_tensor("bcm", [128, 1], F32, kind="ExternalInput").ap()
    w1d_in = nc.dram_tensor("w1d", [D, 4096], F32, kind="ExternalInput").ap()
    b1p_in = nc.dram_tensor("b1p", [128, 32], F32, kind="ExternalInput").ap()
    w2d_in = nc.dram_tensor("w2d", [4096, D], F32, kind="ExternalInput").ap()
    b2e_in = nc.dram_tensor("b2e", [128, D], F32, kind="ExternalInput").ap()
    ones_in = nc.dram_tensor("ones", [128, 1], F32, kind="ExternalInput").ap()
    onesb_in = nc.dram_tensor("onesb", [128, 1], BF16, kind="ExternalInput").ap()
    onesr_in = nc.dram_tensor("onesr", [1, 128], F32, kind="ExternalInput").ap()

    x2_out = nc.dram_tensor("x2", [D, SEG], BF16, kind="ExternalOutput").ap()
    perm_dram = nc.dram_tensor("perm_scratch", [128, SEG], BF16, kind="Internal").ap()

    es = ExitStack()
    T = lambda *a: es.enter_context(nc.sbuf_tensor(*a))
    P = lambda *a: es.enter_context(nc.psum_tensor(*a))

    wc_sb = T("wc_sb", [128, 8, 128], BF16)
    bcm_sb = T("bcm_sb", [128, 1], F32)
    w1d_sb = T("w1d_sb", [128, 8, 4096], BF16)
    b1p_sb = T("b1p_sb", [128, 32], F32)
    w2d_sb = T("w2d_sb", [128, 32, D], BF16)
    b2e_sb = T("b2e_sb", [128, D], BF16)
    ones_sb = T("ones_sb", [128, 1], F32)
    onesb_sb = T("onesb_sb", [128, 1], BF16)
    onesr_sb = T("onesr_sb", [1, 128], F32)

    x1t = T("x1t", [128, 8, TC], BF16)
    lgs = T("lgs", [128, TC], F32)
    mx = T("mx", [128, TC // 32], F32)
    perm = T("perm", [128, TC], BF16)
    pe_sb = [T(f"pe_sb{i}", [128, TC], BF16) for i in range(2)]
    h1r = [T(f"h1r{i}", [128, TC], BF16) for i in range(2)]
    h1m = T("h1m", [128, 32, TC], BF16)
    s_sb = T("s_sb", [128, 8, TC], BF16)
    sq_sb = T("sq_sb", [128, TC], F32)
    row_m = T("row_m", [1, TC], F32)
    row_v = T("row_v", [1, TC], F32)
    row_t = T("row_t", [1, TC], F32)
    row_b = T("row_b", [1, TC], F32)
    row_a = T("row_a", [1, TC], F32)
    tmp_f = T("tmp_f", [128, TC], F32)
    x2_sb = T("x2_sb", [128, TC], BF16)

    ps_lg = P("ps_lg", [128, TC], F32)
    ps_h1 = [P(f"ps_h1{i}", [128, TC], F32) for i in range(2)]
    ps_r = [P(f"ps_r{i}", [128, TC], F32) for i in range(2)]
    ps_sum = P("ps_sum", [1, TC], F32)
    ps_sq = P("ps_sq", [1, TC], F32)
    ps_a = P("ps_a", [128, TC], F32)
    ps_b = ps_lg

    em = Em(nc, es)
    block = es.enter_context(nc.Block())
    with es:
        em.dma("gpsimd", wc_sb[:], wc_in.rearrange("(k p) n -> p k n", p=128), writes=["wc_sb"])
        em.dma("sync", bcm_sb[:], bcm_in, writes=["bcm_sb"])
        em.dma("gpsimd", w1d_sb[:], w1d_in.rearrange("(k p) n -> p k n", p=128), writes=["w1d_sb"])
        em.dma("sync", b1p_sb[:], b1p_in, writes=["b1p_sb"])
        em.dma("gpsimd", w2d_sb[:], w2d_in.rearrange("(k p) n -> p k n", p=128), writes=["w2d_sb"])
        em.dma("gpsimd", b2e_sb[:], b2e_in, writes=["b2e_sb"])
        em.dma("sync", ones_sb[:], ones_in, writes=["ones_sb"], sem="wl2")
        em.dma("sync", onesb_sb[:], onesb_in, writes=["onesb_sb"], sem="wl2")
        em.dma("sync", onesr_sb[:], onesr_in, writes=["onesr_sb"])

        for c in range(NCH):
            t0 = c * TC
            for dc in range(8):
                em.dma("sync", x1t[:, dc, :], rr(x1_in)[dc, :, t0:t0 + TC], writes=[f"x1t{dc}"], tag="x1ld")
            for kc in range(8):
                em.op("tensor",
                      lambda e, kc=kc: e.matmul(ps_lg[:, :], wc_sb[:, kc, :], x1t[:, kc, :],
                                                start=(kc == 0), stop=(kc == 7)),
                      reads=["wc_sb", f"x1t{kc}"], writes=(["ps_lg"] if kc in (0, 7) else []), tag="lg")
            em.op("scalar", lambda e: e.activation(out=lgs[:], in_=ps_lg[:, :], func=AF.Identity,
                                                   bias=bcm_sb[:, 0:1]),
                  reads=["ps_lg", "bcm_sb"], writes=["lgs"], tag="lgev")
            em.op("vector", lambda e: e.tensor_reduce(out=mx[:, :, None], in_=lgs[:].rearrange("p (g w) -> p g w", w=32),
                                                      axis=AX.X, op=ALU.max),
                  reads=["lgs"], writes=["mx"], tag="max")
            em.op("vector", lambda e: e.tensor_tensor(
                      out=perm[:].rearrange("p (g w) -> p g w", w=32),
                      in0=lgs[:].rearrange("p (g w) -> p g w", w=32),
                      in1=mx[:, :, None].broadcast_to([128, TC // 32, 32]),
                      op=ALU.is_equal),
                  reads=["lgs", "mx"], writes=["perm"], tag="onehot")
            em.dma("sync", perm_dram[:, t0:t0 + TC], perm[:], reads=["perm"], writes=["perm_dram"], tag="permst")
            for ch in range(32):
                hb = ch % 2
                for kc in range(8):
                    em.op("tensor",
                          lambda e, ch=ch, kc=kc, hb=hb: e.matmul(
                              ps_h1[hb][:, :], w1d_sb[:, kc, ch * 128:(ch + 1) * 128], x1t[:, kc, :],
                              start=(kc == 0), stop=(kc == 7)),
                          reads=["w1d_sb", f"x1t{kc}"], writes=([f"ps_h1{hb}"] if kc in (0, 7) else []),
                          tag="h1")
                em.op("scalar",
                      lambda e, ch=ch, hb=hb: e.activation(out=h1r[hb][:], in_=ps_h1[hb][:, :], func=AF.Relu,
                                                           bias=b1p_sb[:, ch:ch + 1]),
                      reads=[f"ps_h1{hb}", "b1p_sb"], writes=[f"h1r{hb}"], tag="relu")
                em.dma("gpsimd", pe_sb[hb][:],
                       perm_dram[ch * 4:(ch + 1) * 4, t0:t0 + TC].unsqueeze(1).broadcast_to([4, 32, TC]),
                       reads=["perm_dram"], writes=[f"pe{hb}"], tag="peld")
                em.op("vector",
                      lambda e, ch=ch, hb=hb: e.tensor_mul(out=h1m[:, ch, :], in0=h1r[hb][:], in1=pe_sb[hb][:]),
                      reads=[f"h1r{hb}", f"pe{hb}"], writes=[f"h1m{ch}"], tag="mask")
            for dc in range(8):
                pb = dc % 2
                for kc in range(32):
                    em.op("tensor",
                          lambda e, dc=dc, kc=kc, pb=pb: e.matmul(
                              ps_r[pb][:, :], w2d_sb[:, kc, dc * 128:(dc + 1) * 128], h1m[:, kc, :],
                              start=(kc == 0), stop=False),
                          reads=["w2d_sb", f"h1m{kc}"], writes=([f"ps_r{pb}"] if kc == 0 else []),
                          tag="w2")
                em.op("tensor",
                      lambda e, dc=dc, pb=pb: e.matmul(
                          ps_r[pb][:, :], b2e_sb[:, dc * 128:(dc + 1) * 128], perm[:],
                          start=False, stop=True),
                      reads=["b2e_sb", "perm"], writes=[f"ps_r{pb}"], tag="b2")
                em.op("vector",
                      lambda e, dc=dc, pb=pb: e.tensor_add(out=s_sb[:, dc, :], in0=ps_r[pb][:, :],
                                                           in1=x1t[:, dc, :]),
                      reads=[f"ps_r{pb}", f"x1t{dc}"], writes=[f"s{dc}"], tag="resadd")
            for dc in range(8):
                em.op("tensor",
                      lambda e, dc=dc: e.matmul(ps_sum[:, :], onesb_sb[:], s_sb[:, dc, :],
                                                start=(dc == 0), stop=(dc == 7)),
                      reads=[f"s{dc}", "onesb_sb"], writes=(["ps_sum"] if dc in (0, 7) else []), tag="sum")
            for dc in range(8):
                em.op("scalar",
                      lambda e, dc=dc: e.square(out=sq_sb[:], in_=s_sb[:, dc, :]),
                      reads=[f"s{dc}"], writes=["sq_sb"], tag="sq")
                em.op("tensor",
                      lambda e, dc=dc: e.matmul(ps_sq[:, :], ones_sb[:], sq_sb[:],
                                                start=(dc == 0), stop=(dc == 7)),
                      reads=["sq_sb", "ones_sb"], writes=(["ps_sq"] if dc in (0, 7) else []), tag="sqmm")
            em.op("vector", lambda e: e.tensor_scalar_mul(out=row_m[:], in0=ps_sum[:, :], scalar1=1.0 / D),
                  reads=["ps_sum"], writes=["row_m"], tag="m")
            em.op("vector", lambda e: e.tensor_scalar_mul(out=row_v[:], in0=ps_sq[:, :], scalar1=1.0 / D),
                  reads=["ps_sq"], writes=["row_v"], tag="msq")
            em.op("vector", lambda e: e.tensor_mul(out=row_t[:], in0=row_m[:], in1=row_m[:]),
                  reads=["row_m"], writes=["row_t"], tag="m2")
            em.op("vector", lambda e: e.tensor_sub(out=row_v[:], in0=row_v[:], in1=row_t[:]),
                  reads=["row_v", "row_t"], writes=["row_v"], tag="var")
            em.op("vector", lambda e: e.tensor_scalar_add(out=row_v[:], in0=row_v[:], scalar1=1e-5),
                  reads=["row_v"], writes=["row_v"], tag="eps")
            em.op("scalar", lambda e: e.activation(out=row_b[:], in_=row_v[:], func=AF.Ln),
                  reads=["row_v"], writes=["row_b"], tag="ln")
            em.op("scalar", lambda e: e.activation(out=row_b[:], in_=row_b[:], func=AF.Exp, scale=-0.5),
                  reads=["row_b"], writes=["row_b"], tag="rstd")
            em.op("vector", lambda e: e.tensor_mul(out=row_a[:], in0=row_m[:], in1=row_b[:]),
                  reads=["row_m", "row_b"], writes=["row_a"], tag="a1")
            em.op("vector", lambda e: e.tensor_scalar_mul(out=row_a[:], in0=row_a[:], scalar1=-1.0),
                  reads=["row_a"], writes=["row_a"], tag="a2")
            em.op("tensor", lambda e: e.matmul(ps_b[:, :], onesr_sb[:], row_b[:], start=True, stop=True),
                  reads=["row_b", "onesr_sb"], writes=["ps_lg"], tag="bcb")
            em.op("tensor", lambda e: e.matmul(ps_a[:, :], onesr_sb[:], row_a[:], start=True, stop=True),
                  reads=["row_a", "onesr_sb"], writes=["ps_a"], tag="bca")
            for dc in range(8):
                em.op("vector",
                      lambda e, dc=dc: e.tensor_mul(out=tmp_f[:], in0=s_sb[:, dc, :], in1=ps_b[:, :]),
                      reads=[f"s{dc}", "ps_lg"], writes=["tmp_f"], tag="apb")
                em.op("vector",
                      lambda e, dc=dc: e.tensor_add(out=x2_sb[:], in0=tmp_f[:], in1=ps_a[:, :]),
                      reads=["tmp_f", "ps_a"], writes=["x2_sb"], tag="apa")
                em.dma("sync", rr(x2_out)[dc, :, t0:t0 + TC], x2_sb[:],
                       reads=["x2_sb"], tag="x2st")
        em.finish(block)
    return nc


def run_l0f(inputs, x1):
    Wc = f32c(inputs["L0_ffn_Wc"]).reshape(D, 128)
    bcm = f32c(inputs["L0_ffn_bc"]).reshape(128, 1)
    W1 = f32c(inputs["L0_ffn_W1"])
    w1d = W1.transpose(2, 0, 1, 3).reshape(D, 4096)
    b1p = f32c(inputs["L0_ffn_b1"]).reshape(128, 32).T.copy()
    b1flat = f32c(inputs["L0_ffn_b1"]).reshape(4096)
    b1p = b1flat.reshape(32, 128).T.copy()
    W2 = f32c(inputs["L0_ffn_W2"])
    w2d = W2.transpose(0, 1, 3, 2).reshape(4096, D)
    b2e = f32c(inputs["L0_ffn_b2"]).reshape(128, D)
    in_maps = []
    for c in range(NCORES):
        b, sg = divmod(c, 4)
        in_maps.append(dict(
            x1=bf16c(x1[b, :, sg * SEG:(sg + 1) * SEG]),
            wc=Wc, bcm=bcm, w1d=w1d, b1p=b1p, w2d=w2d, b2e=b2e,
            ones=np.ones((128, 1), np.float32),
            onesb=np.ones((128, 1), np.float32).astype(BF),
            onesr=np.ones((1, 128), np.float32),
        ))
    if "l0f" not in _BUILD_CACHE:
        _BUILD_CACHE["l0f"] = build_l0f()
    res = run_bass_kernel_spmd(_BUILD_CACHE["l0f"], in_maps, core_ids=list(range(NCORES)))
    x2 = np.empty((B, D, L), BF)
    for c in range(NCORES):
        b, sg = divmod(c, 4)
        x2[b, :, sg * SEG:(sg + 1) * SEG] = res.results[c]["x2"]
    return x2


def build_l1a():
    nc = bass.Bass("TRN2", target_bir_lowering=False)
    TC = 512
    NCH = SEG // TC + 1
    EXT = NCH * TC
    KF = 64

    x2_in = nc.dram_tensor("x2e", [D, EXT], BF16, kind="ExternalInput").ap()
    qT_in = nc.dram_tensor("qT", [D, D], F32, kind="ExternalInput").ap()
    kT_in = nc.dram_tensor("kT", [D, D], F32, kind="ExternalInput").ap()
    vT_in = nc.dram_tensor("vT", [D, D], F32, kind="ExternalInput").ap()
    bT_in = nc.dram_tensor("bT", [D, 4], F32, kind="ExternalInput").ap()
    qcw_in = nc.dram_tensor("qcw", [128, 8, 4], F32, kind="ExternalInput").ap()
    kcw_in = nc.dram_tensor("kcw", [128, 8, 4], F32, kind="ExternalInput").ap()
    vcw_in = nc.dram_tensor("vcw", [128, 8, 4], F32, kind="ExternalInput").ap()
    fs_in = nc.dram_tensor("fs", [128, 8, 5], F32, kind="ExternalInput").ap()
    fl_in = nc.dram_tensor("fl", [128, 8, KF], F32, kind="ExternalInput").ap()

    q_out = nc.dram_tensor("q", [D, SEG], BF16, kind="ExternalOutput").ap()
    k_out = nc.dram_tensor("k", [D, SEG], BF16, kind="ExternalOutput").ap()
    v_out = nc.dram_tensor("v", [D, SEG], BF16, kind="ExternalOutput").ap()
    beta_out = nc.dram_tensor("beta", [4, SEG], F32, kind="ExternalOutput").ap()
    ls_out = nc.dram_tensor("ls", [D, SEG], BF16, kind="ExternalOutput").ap()
    ll_out = nc.dram_tensor("ll", [D, SEG], BF16, kind="ExternalOutput").ap()
    vsc = nc.dram_tensor("vscratch", [D, EXT], BF16, kind="Internal").ap()

    es = ExitStack()
    T = lambda *a: es.enter_context(nc.sbuf_tensor(*a))
    P = lambda *a: es.enter_context(nc.psum_tensor(*a))

    wq = T("wq", [128, 8, D], BF16)
    wk = T("wk", [128, 8, D], BF16)
    wv = T("wv", [128, 8, D], BF16)
    wb = T("wb", [128, 8, 4], BF16)
    qcw = T("qcw_s", [128, 8, 4], F32)
    kcw = T("kcw_s", [128, 8, 4], F32)
    vcw = T("vcw_s", [128, 8, 4], F32)
    fs_sb = T("fs_sb", [128, 8, 5], F32)
    fl_sb = T("fl_sb", [128, 8, KF], F32)

    xt = T("xt", [128, 8, TC], BF16)
    exts = {n: T(f"ext_{n}", [128, 8, 3 + TC], BF16) for n in "qkv"}
    conv_t = T("conv_t", [128, TC], F32)
    out_bf = T("out_bf", [128, 2, TC], BF16)
    bsig = T("bsig", [4, TC], F32)
    vw = T("vw", [128, 8, KF - 1 + TC], BF16)
    fir_t = T("fir_t", [128, TC], F32)
    fir_p = T("fir_p", [128, TC], F32)
    fir_o = T("fir_o", [128, 2, TC], BF16)

    ps_mm = [P(f"ps_mm{i}", [128, TC], F32) for i in range(4)]
    ps_b = P("ps_b", [4, TC], F32)

    em = Em(nc, es)
    block = es.enter_context(nc.Block())
    with es:
        em.dma("gpsimd", wq[:], qT_in.rearrange("(k p) n -> p k n", p=128), writes=["wq"])
        em.dma("gpsimd", wk[:], kT_in.rearrange("(k p) n -> p k n", p=128), writes=["wk"])
        em.dma("gpsimd", wv[:], vT_in.rearrange("(k p) n -> p k n", p=128), writes=["wv"])
        em.dma("gpsimd", wb[:], bT_in.rearrange("(k p) n -> p k n", p=128), writes=["wb"])
        em.dma("sync", qcw[:], qcw_in, writes=["qcw"])
        em.dma("sync", kcw[:], kcw_in, writes=["kcw"])
        em.dma("sync", vcw[:], vcw_in, writes=["vcw"])
        em.dma("sync", fs_sb[:], fs_in, writes=["fs_sb"])
        em.dma("sync", fl_sb[:], fl_in, writes=["fl_sb"])

        for n2 in "qkv":
            em.op("vector", lambda e, n2=n2: e.memset(exts[n2][:], 0.0),
                  writes=[f"exth_{n2}{oc}" for oc in range(8)] + [f"ext_{n2}{oc}" for oc in range(8)],
                  tag="extz")
        for c in range(NCH):
            t0 = c * TC
            for dc in range(8):
                em.dma("sync", xt[:, dc, :], rr(x2_in)[dc, :, t0:t0 + TC], writes=[f"xt{dc}"], tag="xld")
            for name, w, cw, dst in (("q", wq, qcw, q_out), ("k", wk, kcw, k_out), ("v", wv, vcw, v_out)):
                ext = exts[name]
                for oc in range(8):
                    pb = oc % 4
                    for kc in range(8):
                        em.op("tensor",
                              lambda e, oc=oc, kc=kc, pb=pb, w=w: e.matmul(
                                  ps_mm[pb][:, :], w[:, kc, oc * 128:(oc + 1) * 128], xt[:, kc, :],
                                  start=(kc == 0), stop=(kc == 7)),
                              reads=["w" + name, f"xt{kc}"],
                              writes=([f"ps_mm{pb}"] if kc in (0, 7) else []), tag="proj")
                    em.op("vector",
                          lambda e, oc=oc, pb=pb, ext=ext: e.tensor_copy(out=ext[:, oc, 3:], in_=ps_mm[pb][:, :]),
                          reads=[f"ps_mm{pb}", f"tl_{name}{oc}"], writes=[f"ext_{name}{oc}"], tag="pev")
                    em.op("vector",
                          lambda e, oc=oc, cw=cw, ext=ext: e.tensor_scalar(
                              out=conv_t[:], in0=ext[:, oc, 0:TC], scalar1=cw[:, oc, 0:1], scalar2=None,
                              op0=ALU.mult),
                          reads=[f"ext_{name}{oc}", f"exth_{name}{oc}", name + "cw"], writes=["conv_t"], tag="cv0")
                    for j in range(1, 4):
                        em.op("vector",
                              lambda e, oc=oc, j=j, cw=cw, ext=ext: e.scalar_tensor_tensor(
                                  out=conv_t[:], in0=ext[:, oc, j:j + TC], scalar=cw[:, oc, j:j + 1],
                                  in1=conv_t[:], op0=ALU.mult, op1=ALU.add),
                              reads=[f"ext_{name}{oc}", "conv_t"], writes=["conv_t"], tag=f"cv{j}")
                    ob = oc % 2
                    em.op("scalar",
                          lambda e, ob=ob: e.activation(out=out_bf[:, ob, :], in_=conv_t[:], func=AF.Silu),
                          reads=["conv_t"], writes=[f"out_bf{ob}"], tag="sil")
                    em.op("vector",
                          lambda e, oc=oc, ext=ext: e.tensor_copy(out=ext[:, oc, 0:3], in_=ext[:, oc, TC:TC + 3]),
                          reads=[f"ext_{name}{oc}"], writes=[f"exth_{name}{oc}", f"tl_{name}{oc}"], tag="tail")
                    if c > 0:
                        em.dma("scalar", rr(dst)[oc, :, (c - 1) * TC:c * TC], out_bf[:, ob, :],
                               reads=[f"out_bf{ob}"], tag="post")
                    if name == "v":
                        em.dma("sync", rr(vsc)[oc, :, t0:t0 + TC], out_bf[:, ob, :],
                               reads=[f"out_bf{ob}"], writes=[f"vsc{oc}_{c % 2}"], tag="vst")
            for kc in range(8):
                em.op("tensor",
                      lambda e, kc=kc: e.matmul(ps_b[:, :], wb[:, kc, :], xt[:, kc, :],
                                                start=(kc == 0), stop=(kc == 7)),
                      reads=["wb", f"xt{kc}"], writes=(["ps_b"] if kc in (0, 7) else []), tag="bmm")
            em.op("scalar", lambda e: e.activation(out=bsig[:], in_=ps_b[:, :], func=AF.Sigmoid),
                  reads=["ps_b"], writes=["bsig"], tag="bsig")
            if c > 0:
                em.dma("scalar", beta_out[:, (c - 1) * TC:c * TC], bsig[:], reads=["bsig"], tag="bst")
            if c > 0:
                w0 = c * TC - (KF - 1)
                for oc in range(8):
                    em.dma("sync", vw[:, oc, :], rr(vsc)[oc, :, w0:w0 + KF - 1 + TC],
                           reads=[f"vsc{oc}_{c % 2}", f"vsc{oc}_{(c + 1) % 2}"],
                           writes=[f"vw{oc}"], tag="vwld")
                    em.op("vector",
                          lambda e, oc=oc: e.tensor_scalar(
                              out=fir_t[:], in0=vw[:, oc, KF - 5:KF - 5 + TC], scalar1=fs_sb[:, oc, 0:1],
                              scalar2=None, op0=ALU.mult),
                          reads=[f"vw{oc}", "fs_sb"], writes=["fir_t"], tag="fs0")
                    for j in range(1, 5):
                        em.op("vector",
                              lambda e, oc=oc, j=j: e.scalar_tensor_tensor(
                                  out=fir_t[:], in0=vw[:, oc, KF - 5 + j:KF - 5 + j + TC],
                                  scalar=fs_sb[:, oc, j:j + 1], in1=fir_t[:], op0=ALU.mult, op1=ALU.add),
                              reads=[f"vw{oc}", "fir_t"], writes=["fir_t"], tag="fsj")
                    fb = oc % 2
                    em.op("vector",
                          lambda e, fb=fb: e.tensor_copy(out=fir_o[:, fb, :], in_=fir_t[:]),
                          reads=["fir_t"], writes=[f"fir_o{fb}"], tag="fscv")
                    em.dma("gpsimd", rr(ls_out)[oc, :, (c - 1) * TC:c * TC], fir_o[:, fb, :],
                           reads=[f"fir_o{fb}"], tag="lsst")
                    em.op("vector",
                          lambda e, oc=oc: e.tensor_scalar(
                              out=fir_t[:], in0=vw[:, oc, 0:TC], scalar1=fl_sb[:, oc, 0:1],
                              scalar2=None, op0=ALU.mult),
                          reads=[f"vw{oc}", "fl_sb"], writes=["fir_t"], tag="fl0")
                    for j in range(1, KF):
                        em.op("vector",
                              lambda e, oc=oc, j=j: e.scalar_tensor_tensor(
                                  out=fir_t[:], in0=vw[:, oc, j:j + TC],
                                  scalar=fl_sb[:, oc, j:j + 1], in1=fir_t[:], op0=ALU.mult, op1=ALU.add),
                              reads=[f"vw{oc}", "fir_t"], writes=["fir_t"], tag="flj")
                    em.op("vector",
                          lambda e, fb=fb: e.tensor_copy(out=fir_o[:, fb, :], in_=fir_t[:]),
                          reads=["fir_t"], writes=[f"fir_o{fb}"], tag="flcv")
                    em.dma("gpsimd", rr(ll_out)[oc, :, (c - 1) * TC:c * TC], fir_o[:, fb, :],
                           reads=[f"fir_o{fb}"], tag="llst")
        em.finish(block)
    return nc


def run_l1a(inputs, x2):
    def wT(name):
        return f32c(inputs[name]).T.copy()
    def packc(name):
        return f32c(inputs[name]).reshape(8, 128, 4).transpose(1, 0, 2).copy()
    fir_s = f32c(inputs["L1_attn_fir_short_w"]).reshape(1024, 5)
    fir_l = f32c(inputs["L1_attn_fir_long_w"]).reshape(1024, 64)
    in_maps = []
    for c in range(NCORES):
        b, sg = divmod(c, 4)
        t0 = sg * SEG
        x2e = np.zeros((D, 512 + SEG), BF)
        x2e[:, 512:] = x2[b, :, t0:t0 + SEG]
        if sg > 0:
            x2e[:, :512] = x2[b, :, t0 - 512:t0]
        in_maps.append(dict(
            x2e=x2e,
            qT=wT("L1_attn_q_proj_w"), kT=wT("L1_attn_k_proj_w"), vT=wT("L1_attn_v_proj_w"),
            bT=wT("L1_attn_b_proj_w"),
            qcw=packc("L1_attn_q_conv_w"), kcw=packc("L1_attn_k_conv_w"), vcw=packc("L1_attn_v_conv_w"),
            fs=fir_s.reshape(8, 128, 5).transpose(1, 0, 2).copy(),
            fl=fir_l.reshape(8, 128, 64).transpose(1, 0, 2).copy(),
        ))
    if "l1a" not in _BUILD_CACHE:
        _BUILD_CACHE["l1a"] = build_l1a()
    res = run_bass_kernel_spmd(_BUILD_CACHE["l1a"], in_maps, core_ids=list(range(NCORES)))
    out = {}
    for nm, width in (("q", D), ("k", D), ("v", D), ("ls", D), ("ll", D)):
        full = np.empty((B, width, L), BF)
        for c in range(NCORES):
            b, sg = divmod(c, 4)
            full[b, :, sg * SEG:(sg + 1) * SEG] = res.results[c][nm]
        out[nm] = full
    beta = np.empty((B, 4, L), np.float32)
    for c in range(NCORES):
        b, sg = divmod(c, 4)
        beta[b, :, sg * SEG:(sg + 1) * SEG] = res.results[c]["beta"]
    out["beta"] = beta
    return out


def build_l1b(nsc=None):
    nc = bass.Bass("TRN2", target_bir_lowering=False)
    DH = 256
    SC = 512
    NSC = nsc if nsc is not None else L // SC
    LL = NSC * SC
    NTT = 4

    q_in = nc.dram_tensor("q", [DH, L], BF16, kind="ExternalInput").ap()
    k_in = nc.dram_tensor("k", [DH, L], BF16, kind="ExternalInput").ap()
    v_in = nc.dram_tensor("v", [DH, L], BF16, kind="ExternalInput").ap()
    beta_in = nc.dram_tensor("beta", [128, L // 128], F32, kind="ExternalInput").ap()
    id_in = nc.dram_tensor("identb", [128, 128], BF16, kind="ExternalInput").ap()
    maskl_in = nc.dram_tensor("maskl", [128, 128], BF16, kind="ExternalInput").ap()
    masku_in = nc.dram_tensor("masku", [128, 128], BF16, kind="ExternalInput").ap()
    ones_in = nc.dram_tensor("ones", [128, 1], F32, kind="ExternalInput").ap()
    onesr_in = nc.dram_tensor("onesr", [1, 128], F32, kind="ExternalInput").ap()

    do_out = nc.dram_tensor("dout", [DH, L], BF16, kind="ExternalOutput").ap()
    do_tm = nc.dram_tensor("do_tm", [L, DH], BF16, kind="Internal").ap()

    es = ExitStack()
    T = lambda *a: es.enter_context(nc.sbuf_tensor(*a))
    P = lambda *a: es.enter_context(nc.psum_tensor(*a))

    id_sb = T("id_sb", [128, 128], BF16)
    maskl_sb = T("maskl_sb", [128, 128], BF16)
    masku_sb = T("masku_sb", [128, 128], BF16)
    ones_sb = T("ones_sb", [128, 1], F32)
    onesr_sb = T("onesr_sb", [1, 128], F32)
    beta_sb = T("beta_sb", [128, L // 128], F32)
    negb_sb = T("negb_sb", [128, L // 128], F32)

    qf = T("qf", [128, 2, SC], BF16)
    kf = T("kf", [128, 2, SC], BF16)
    vf = T("vf", [128, 2, SC], BF16)
    qn = T("qn", [128, 2, SC], BF16)
    kn = T("kn", [128, 2, SC], BF16)
    sqs = T("sqs", [128, SC], F32)
    row1 = T("row1", [1, SC], F32)
    ktm = T("ktm", [128, NTT, DH], BF16)
    vtm = T("vtm", [128, NTT, DH], BF16)
    kbtm = T("kbtm", [128, NTT, DH], BF16)
    A_sb = T("A_sb", [128, 128], BF16)
    Bt_sb = T("Bt_sb", [128, 128], BF16)
    A2_sb = T("A2_sb", [128, 128], BF16)
    B2_sb = T("B2_sb", [128, 128], BF16)
    A4_sb = T("A4_sb", [128, 128], BF16)
    B4_sb = T("B4_sb", [128, 128], BF16)
    A8_sb = T("A8_sb", [128, 128], BF16)
    B8_sb = T("B8_sb", [128, 128], BF16)
    A16_sb = T("A16_sb", [128, 128], BF16)
    N_sb = T("N_sb", [128, 128], BF16)
    N2_sb = T("N2_sb", [128, 128], BF16)
    utm = T("utm", [128, NTT, DH], BF16)
    wfm = T("wfm", [128, 2, NTT, 128], BF16)
    attnT = T("attnT", [128, NTT, 128], BF16)
    S_bf = T("S_bf", [128, 2, DH], BF16)
    uadj = T("uadj", [128, DH], BF16)
    do_sb = T("do_sb", [32, DH], BF16)
    ttm = T("ttm", [128, DH], BF16)
    dofm = T("dofm", [128, 2, 128], BF16)

    ps_S = [P(f"ps_S{i}", [128, DH], F32) for i in range(2)]
    ps_pw = [P(f"ps_pw{i}", [128, 128], F32) for i in range(2)]
    ps_tr = P("ps_tr", [128, 128], BF16)
    ps_u = P("ps_u", [128, DH], F32)
    ps_o = P("ps_o", [32, DH], F32)
    ps_bc = P("ps_bc", [128, SC], F32)

    em = Em(nc, es)
    block = es.enter_context(nc.Block())
    with es:
        em.dma("sync", id_sb[:], id_in, writes=["id_sb"], sem="wl2")
        em.dma("sync", maskl_sb[:], maskl_in, writes=["maskl_sb"])
        em.dma("sync", masku_sb[:], masku_in, writes=["masku_sb"])
        em.dma("sync", ones_sb[:], ones_in, writes=["ones_sb"], sem="wl2")
        em.dma("sync", onesr_sb[:], onesr_in, writes=["onesr_sb"], sem="wl2")
        em.dma("sync", beta_sb[:], beta_in, writes=["beta_sb"])
        em.op("vector", lambda e: e.tensor_scalar_mul(out=negb_sb[:], in0=beta_sb[:], scalar1=-1.0),
              reads=["beta_sb"], writes=["negb_sb"], tag="negb")

        for sc in range(NSC):
            t0 = sc * SC
            for cc in range(2):
                em.dma("sync", qf[:, cc, :], q_in[cc * 128:(cc + 1) * 128, t0:t0 + SC],
                       writes=[f"qf{cc}"], tag="qld")
                em.dma("sync", kf[:, cc, :], k_in[cc * 128:(cc + 1) * 128, t0:t0 + SC],
                       writes=[f"kf{cc}"], tag="kld")
                em.dma("sync", vf[:, cc, :], v_in[cc * 128:(cc + 1) * 128, t0:t0 + SC],
                       writes=[f"vf{cc}"], tag="vld")
            for nm, src, dst in (("q", qf, qn), ("k", kf, kn)):
                for cc in range(2):
                    em.op("scalar", lambda e, cc=cc, src=src: e.square(out=sqs[:], in_=src[:, cc, :]),
                          reads=[f"{nm}f{cc}"], writes=["sqs"], tag="l2sq")
                    em.op("tensor",
                          lambda e, cc=cc: e.matmul(ps_bc[0:1, :], ones_sb[:], sqs[:],
                                                    start=(cc == 0), stop=(cc == 1)),
                          reads=["sqs", "ones_sb"], writes=["ps_row", "ps_bcast"], tag="l2mm")
                em.op("vector", lambda e: e.tensor_copy(out=row1[:], in_=ps_bc[0:1, :]),
                      reads=["ps_row"], writes=["row1"], tag="l2cp")
                em.op("scalar", lambda e: e.activation(out=row1[:], in_=row1[:], func=AF.Ln),
                      reads=["row1"], writes=["row1"], tag="l2ln")
                em.op("scalar", lambda e: e.activation(out=row1[:], in_=row1[:], func=AF.Exp, scale=-0.5),
                      reads=["row1"], writes=["row1"], tag="l2ex")
                em.op("tensor", lambda e: e.matmul(ps_bc[:, :], onesr_sb[:], row1[:], start=True, stop=True),
                      reads=["row1", "onesr_sb"], writes=["ps_row", "ps_bcast"], tag="l2bc")
                for cc in range(2):
                    em.op("vector",
                          lambda e, cc=cc, src=src, dst=dst: e.tensor_mul(
                              out=dst[:, cc, :], in0=src[:, cc, :], in1=ps_bc[:, :]),
                          reads=[f"{nm}f{cc}", "ps_bcast"], writes=[f"{nm}n{cc}"], tag="l2ap")
            for tt in range(NTT):
                for cc in range(2):
                    em.op("tensor",
                          lambda e, tt=tt, cc=cc: e.transpose(
                              ps_tr[:, :], kn[:, cc, tt * 128:(tt + 1) * 128], id_sb[:]),
                          reads=[f"kn{cc}", "id_sb"], writes=["ps_tr"], tag="ktr")
                    em.op("scalar",
                          lambda e, tt=tt, cc=cc: e.copy(out=ktm[:, tt, cc * 128:(cc + 1) * 128],
                                                         in_=ps_tr[:, :]),
                          reads=["ps_tr"], writes=[f"ktm{tt}"], tag="ktev")
                    em.op("tensor",
                          lambda e, tt=tt, cc=cc: e.transpose(
                              ps_tr[:, :], vf[:, cc, tt * 128:(tt + 1) * 128], id_sb[:]),
                          reads=[f"vf{cc}", "id_sb"], writes=["ps_tr"], tag="vtr")
                    em.op("scalar",
                          lambda e, tt=tt, cc=cc: e.copy(out=vtm[:, tt, cc * 128:(cc + 1) * 128],
                                                         in_=ps_tr[:, :]),
                          reads=["ps_tr"], writes=[f"vtm{tt}"], tag="vtev")
                em.op("vector",
                      lambda e, tt=tt, sc=sc: e.tensor_scalar_mul(
                          out=kbtm[:, tt, :], in0=ktm[:, tt, :],
                          scalar1=beta_sb[:, sc * NTT + tt:sc * NTT + tt + 1]),
                      reads=[f"ktm{tt}", "beta_sb"], writes=[f"kbtm{tt}"], tag="kb")
                em.op("vector",
                      lambda e, tt=tt, sc=sc: e.tensor_scalar_mul(
                          out=vtm[:, tt, :], in0=vtm[:, tt, :],
                          scalar1=beta_sb[:, sc * NTT + tt:sc * NTT + tt + 1]),
                      reads=[f"vtm{tt}", "beta_sb"], writes=[f"vtm{tt}"], tag="vb")
            for tt in range(NTT):
                for cc in range(2):
                    em.op("tensor",
                          lambda e, tt=tt, cc=cc: e.matmul(
                              ps_pw[0][:, :], kn[:, cc, tt * 128:(tt + 1) * 128],
                              kn[:, cc, tt * 128:(tt + 1) * 128],
                              start=(cc == 0), stop=(cc == 1)),
                          reads=[f"kn{cc}"], writes=(["ps_pw0"] if cc in (0, 1) else []), tag="gram")
                em.op("vector",
                      lambda e, tt=tt, sc=sc: e.scalar_tensor_tensor(
                          out=A_sb[:], in0=ps_pw[0][:, :],
                          scalar=negb_sb[:, sc * NTT + tt:sc * NTT + tt + 1],
                          in1=maskl_sb[:], op0=ALU.mult, op1=ALU.mult),
                      reads=["ps_pw0", "negb_sb", "maskl_sb"], writes=["A_sb"], tag="Amask")
                em.op("tensor", lambda e: e.transpose(ps_tr[:, :], A_sb[:], id_sb[:]),
                      reads=["A_sb", "id_sb"], writes=["ps_tr"], tag="Atr")
                em.op("scalar", lambda e: e.copy(out=Bt_sb[:], in_=ps_tr[:, :]),
                      reads=["ps_tr"], writes=["Bt_sb"], tag="Btev")
                for (lt, rh, dst, nm) in ((Bt_sb, A_sb, A2_sb, "A2"), (A_sb, Bt_sb, B2_sb, "B2"),
                                          (B2_sb, A2_sb, A4_sb, "A4"), (A2_sb, B2_sb, B4_sb, "B4"),
                                          (B4_sb, A4_sb, A8_sb, "A8"), (A4_sb, B4_sb, B8_sb, "B8"),
                                          (B8_sb, A8_sb, A16_sb, "A16")):
                    pw = 0 if nm in ("A2", "A4", "A8", "A16") else 1
                    em.op("tensor",
                          lambda e, lt=lt, rh=rh, pw=pw: e.matmul(ps_pw[pw][:, :], lt[:], rh[:],
                                                                  start=True, stop=True),
                          reads=["A_sb", "Bt_sb", "A2_sb", "B2_sb", "A4_sb", "B4_sb", "A8_sb", "B8_sb"],
                          writes=[f"ps_pw{pw}"], tag=f"pw{nm}")
                    em.op("scalar", lambda e, dst=dst, pw=pw: e.copy(out=dst[:], in_=ps_pw[pw][:, :]),
                          reads=[f"ps_pw{pw}"], writes=[f"{nm}_sb"], tag=f"ev{nm}")
                em.op("vector", lambda e: e.tensor_add(out=N_sb[:], in0=Bt_sb[:], in1=id_sb[:]),
                      reads=["Bt_sb", "id_sb"], writes=["N_sb"], tag="N1")
                for i, Ak in enumerate((A2_sb, A4_sb, A8_sb, A16_sb)):
                    nmA = ("A2_sb", "A4_sb", "A8_sb", "A16_sb")[i]
                    em.op("tensor",
                          lambda e, Ak=Ak: e.matmul(ps_pw[0][:, :], Ak[:], N_sb[:], start=True, stop=False),
                          reads=[nmA, "N_sb"], writes=["ps_pw0"], tag="Np")
                    em.op("tensor",
                          lambda e: e.matmul(ps_pw[0][:, :], id_sb[:], N_sb[:], start=False, stop=True),
                          reads=["id_sb", "N_sb"], writes=["ps_pw0"], tag="Ni")
                    em.op("scalar", lambda e: e.copy(out=N_sb[:], in_=ps_pw[0][:, :]),
                          reads=["ps_pw0"], writes=["N_sb"], tag="Nev")
                em.op("tensor",
                      lambda e, tt=tt: e.matmul(ps_u[:, :], N_sb[:], vtm[:, tt, :], start=True, stop=True),
                      reads=["N_sb", f"vtm{tt}"], writes=["ps_u"], tag="umm")
                em.op("scalar", lambda e, tt=tt: e.copy(out=utm[:, tt, :], in_=ps_u[:, :]),
                      reads=["ps_u"], writes=[f"utm{tt}"], tag="uev")
                for cc in range(2):
                    em.op("tensor",
                          lambda e, tt=tt, cc=cc: e.matmul(
                              ps_u[:, 0:128], kbtm[:, tt, cc * 128:(cc + 1) * 128], N_sb[:],
                              start=True, stop=True),
                          reads=[f"kbtm{tt}", "N_sb"], writes=["ps_u"], tag="wmm")
                    em.op("scalar", lambda e, tt=tt, cc=cc: e.copy(out=wfm[:, cc, tt, :], in_=ps_u[:, 0:128]),
                          reads=["ps_u"], writes=[f"wfm{tt}"], tag="wev")
                for cc in range(2):
                    em.op("tensor",
                          lambda e, tt=tt, cc=cc: e.matmul(
                              ps_pw[0][:, :], kn[:, cc, tt * 128:(tt + 1) * 128],
                              qn[:, cc, tt * 128:(tt + 1) * 128],
                              start=(cc == 0), stop=(cc == 1)),
                          reads=[f"kn{cc}", f"qn{cc}"], writes=(["ps_pw0"] if cc in (0, 1) else []), tag="atmm")
                em.op("vector",
                      lambda e, tt=tt: e.tensor_mul(out=attnT[:, tt, :], in0=ps_pw[0][:, :], in1=masku_sb[:]),
                      reads=["ps_pw0", "masku_sb"], writes=[f"attnT{tt}"], tag="atmask")
            for j in range(16):
                jj = sc * 16 + j
                tt = j // 4
                p = (j % 4) * 32
                first = (jj == 0)
                if not first:
                    for cc in range(2):
                        em.op("vector",
                              lambda e, cc=cc: e.tensor_copy(out=S_bf[:, cc, :], in_=ps_S[cc][:, :]),
                              reads=[f"ps_S{cc}"], writes=[f"S_bf{cc}"], tag="Scp")
                    for cc in range(2):
                        em.op("tensor",
                              lambda e, tt=tt, p=p, cc=cc: e.matmul(
                                  ps_o[:, :], wfm[:, cc, tt, p:p + 32], S_bf[:, cc, :],
                                  start=(cc == 0), stop=(cc == 1)),
                              reads=[f"wfm{tt}", f"S_bf{cc}"], writes=(["ps_o"] if cc in (0, 1) else []),
                              tag="wS")
                    em.op("vector",
                          lambda e, tt=tt, p=p: e.tensor_sub(
                              out=uadj[p:p + 32, :], in0=utm[p:p + 32, tt, :], in1=ps_o[:, :]),
                          reads=[f"utm{tt}", "ps_o"], writes=["uadj"], tag="uadj")
                else:
                    em.op("vector",
                          lambda e, tt=tt, p=p: e.tensor_copy(out=uadj[p:p + 32, :], in_=utm[p:p + 32, tt, :]),
                          reads=[f"utm{tt}"], writes=["uadj"], tag="uadj0")
                if not first:
                    for cc in range(2):
                        em.op("tensor",
                              lambda e, tt=tt, p=p, cc=cc: e.matmul(
                                  ps_o[:, :], qn[:, cc, (tt * 128 + p):(tt * 128 + p + 32)], S_bf[:, cc, :],
                                  start=(cc == 0), stop=False),
                              reads=[f"qn{cc}", f"S_bf{cc}"], writes=(["ps_o"] if cc == 0 else []), tag="qS")
                    em.op("tensor",
                          lambda e, tt=tt, p=p: e.matmul(
                              ps_o[:, :], attnT[p:p + 32, tt, p:p + 32], uadj[p:p + 32, :],
                              start=False, stop=True, tile_position=(p, 0)),
                          reads=[f"attnT{tt}", "uadj"], writes=["ps_o"], tag="attnterm")
                else:
                    em.op("tensor",
                          lambda e, tt=tt, p=p: e.matmul(
                              ps_o[:, :], attnT[p:p + 32, tt, p:p + 32], uadj[p:p + 32, :],
                              start=True, stop=True, tile_position=(p, 0)),
                          reads=[f"attnT{tt}", "uadj"], writes=["ps_o"], tag="attnterm0")
                em.op("scalar", lambda e: e.copy(out=do_sb[:], in_=ps_o[:, :]),
                      reads=["ps_o"], writes=["do_sb"], tag="doev")
                em.dma("sync", do_tm[jj * 32:(jj + 1) * 32, :], do_sb[:], reads=["do_sb"], tag="dost")
                for cc in range(2):
                    em.op("tensor",
                          lambda e, tt=tt, p=p, cc=cc, first=first: e.matmul(
                              ps_S[cc][:, :], ktm[p:p + 32, tt, cc * 128:(cc + 1) * 128], uadj[p:p + 32, :],
                              start=first, stop=False, tile_position=(p, 0), skip_group_check=True),
                          reads=[f"ktm{tt}", "uadj"], writes=[f"ps_S{cc}"], tag="Supd")
        all_dost = ("d:do_sb", em.cnt.get("d:do_sb", 0))
        for tt in range(LL // 128):
            em.dma("sync", ttm[:], do_tm[tt * 128:(tt + 1) * 128, :], writes=["ttm"],
                   deps=[all_dost], tag="dold")
            for cc in range(2):
                em.op("tensor",
                      lambda e, cc=cc: e.transpose(ps_tr[:, :], ttm[:, cc * 128:(cc + 1) * 128], id_sb[:]),
                      reads=["ttm", "id_sb"], writes=["ps_tr"], tag="dtr")
                em.op("scalar", lambda e, cc=cc: e.copy(out=dofm[:, cc, :], in_=ps_tr[:, :]),
                      reads=["ps_tr"], writes=[f"dofm{cc}"], tag="dofm")
                em.dma("scalar", do_out[cc * 128:(cc + 1) * 128, tt * 128:(tt + 1) * 128], dofm[:, cc, :],
                       reads=[f"dofm{cc}"], tag="dofst")
        em.finish(block)
    return nc


def run_l1b(l1a_out):
    maskl = np.zeros((128, 128), np.float32)
    masku = np.zeros((128, 128), np.float32)
    for g in range(4):
        for i in range(32):
            for jx in range(32):
                if jx < i:
                    maskl[g * 32 + i, g * 32 + jx] = 1.0
                if i < jx:
                    masku[g * 32 + i, g * 32 + jx] = 1.0
    ident = np.eye(128, dtype=np.float32).astype(BF)
    in_maps = []
    for c in range(NCORES):
        b, h = divmod(c, 4)
        sl = slice(h * 256, (h + 1) * 256)
        in_maps.append(dict(
            q=bf16c(l1a_out["q"][b, sl]),
            k=bf16c(l1a_out["k"][b, sl]),
            v=bf16c(l1a_out["v"][b, sl]),
            beta=f32c(l1a_out["beta"][b, h].reshape(L // 128, 128).T),
            identb=ident,
            maskl=maskl.astype(BF), masku=masku.astype(BF),
            ones=np.ones((128, 1), np.float32),
            onesr=np.ones((1, 128), np.float32),
        ))
    if "l1b" not in _BUILD_CACHE:
        _BUILD_CACHE["l1b"] = build_l1b()
    res = run_bass_kernel_spmd(_BUILD_CACHE["l1b"], in_maps, core_ids=list(range(NCORES)))
    do = np.empty((B, D, L), BF)
    for c in range(NCORES):
        b, h = divmod(c, 4)
        do[b, h * 256:(h + 1) * 256] = res.results[c]["dout"]
    return do


def build_l1c():
    nc = bass.Bass("TRN2", target_bir_lowering=False)
    TC = 512
    NCH = SEG // TC

    x2_in = nc.dram_tensor("x2", [D, SEG], BF16, kind="ExternalInput").ap()
    ls_in = nc.dram_tensor("ls", [D, SEG], BF16, kind="ExternalInput").ap()
    ll_in = nc.dram_tensor("ll", [D, SEG], BF16, kind="ExternalInput").ap()
    do_in = nc.dram_tensor("do", [D, SEG], BF16, kind="ExternalInput").ap()
    v_in = nc.dram_tensor("v", [D, SEG], BF16, kind="ExternalInput").ap()
    w1aT_in = nc.dram_tensor("w1aT", [D, D], F32, kind="ExternalInput").ap()
    w1bT_in = nc.dram_tensor("w1bT", [16, D], F32, kind="ExternalInput").ap()
    fb1_in = nc.dram_tensor("fb1", [128, 8], F32, kind="ExternalInput").ap()
    w2T_in = nc.dram_tensor("w2T", [D, 4], F32, kind="ExternalInput").ap()
    fb2_in = nc.dram_tensor("fb2", [4, 1], F32, kind="ExternalInput").ap()
    lt_in = nc.dram_tensor("lt", [1, 1], F32, kind="ExternalInput").ap()
    rss_in = nc.dram_tensor("rss", [128, 1], F32, kind="ExternalInput").ap()
    rsl_in = nc.dram_tensor("rsl", [128, 1], F32, kind="ExternalInput").ap()
    onw_in = nc.dram_tensor("onw", [128, 2, 1], F32, kind="ExternalInput").ap()
    opT_in = nc.dram_tensor("opT", [D, D], F32, kind="ExternalInput").ap()
    id_in = nc.dram_tensor("identb", [128, 128], BF16, kind="ExternalInput").ap()
    ones_in = nc.dram_tensor("ones", [128, 1], F32, kind="ExternalInput").ap()
    onesb_in = nc.dram_tensor("onesb", [128, 1], BF16, kind="ExternalInput").ap()
    onesr_in = nc.dram_tensor("onesr", [1, 128], F32, kind="ExternalInput").ap()

    x3_out = nc.dram_tensor("x3", [D, SEG], BF16, kind="ExternalOutput").ap()

    es = ExitStack()
    T = lambda *a: es.enter_context(nc.sbuf_tensor(*a))
    P = lambda *a: es.enter_context(nc.psum_tensor(*a))

    w1a = T("w1a", [128, 8, D], BF16)
    w1b = T("w1b", [16, D], BF16)
    fb1 = T("fb1_s", [128, 8], F32)
    w2T = T("w2T_s", [128, 8, 4], BF16)
    fb2 = T("fb2_s", [4, 1], F32)
    opT = T("opT_s", [128, 8, D], BF16)
    id_sb = T("id_sb", [128, 128], BF16)
    ones_sb = T("ones_sb", [128, 1], F32)
    onesb_sb = T("onesb_sb", [128, 1], BF16)
    onesr_sb = T("onesr_sb", [1, 128], F32)
    rss_sb = T("rss_sb", [128, 1], F32)
    rsl_sb = T("rsl_sb", [128, 1], F32)
    onw_sb = T("onw_sb", [128, 2, 1], F32)
    lt_sb = T("lt_sb", [1, 1], F32)
    it_sb = T("it_sb", [1, 1], F32)
    itb_sb = T("itb_sb", [128, 1], F32)
    idf = T("idf", [128, 128], F32)

    comp = {n: T(f"c_{n}", [128, 8, TC], BF16) for n in ("x2", "ls", "ll", "do", "v")}
    sqf = T("sqf", [128, TC], F32)
    sqf2 = T("sqf2", [128, TC], F32)
    row1 = T("row1", [1, TC], F32)
    row2 = T("row2", [1, TC], F32)
    statb = T("statb", [16, 4, TC], BF16)
    shared = T("shared", [128, 8, TC], F32)
    tmpf = T("tmpf", [128, TC], F32)
    hdn2 = [T(f"hdn2_{i}", [128, TC], BF16) for i in range(2)]
    tmpg = [T(f"tmpg{i}", [128, TC], F32) for i in range(2)]
    lgh = T("lgh", [4, TC], F32)
    fwT = T("fwT", [128, 4], F32)
    mxc = T("mxc", [128, 1], F32)
    smc = T("smc", [128, 1], F32)
    fvt = T("fvt", [4, 128], F32)
    rcomp = T("rcomp", [1, 4, 4, TC], F32)
    o_sb = T("o_sb", [128, TC], F32)
    on_sb = T("on_sb", [128, 8, TC], BF16)
    s_sb = T("s_sb", [128, 8, TC], F32)
    row_m = T("row_m", [1, TC], F32)
    row_v = T("row_v", [1, TC], F32)
    row_t = T("row_t", [1, TC], F32)
    row_b = T("row_b", [1, TC], F32)
    row_a = T("row_a", [1, TC], F32)
    x3_sb = T("x3_sb", [128, TC], BF16)

    ps_r1 = P("ps_r1", [1, TC], F32)
    ps_r2 = P("ps_r2", [1, TC], F32)
    ps_b1 = P("ps_b1", [128, TC], F32)
    ps_b2 = P("ps_b2", [128, TC], F32)
    ps_lg = P("ps_lg", [4, TC], F32)
    ps_tr = P("ps_tr", [128, 128], F32)
    ps_mm = [P(f"ps_mm{i}", [128, TC], F32) for i in range(2)]

    em = Em(nc, es)
    block = es.enter_context(nc.Block())
    with es:
        em.dma("gpsimd", w1a[:], w1aT_in.rearrange("(k p) n -> p k n", p=128), writes=["w1a"], sem="wl")
        em.dma("gpsimd", w1b[:], w1bT_in, writes=["w1b"], sem="wl")
        em.dma("sync", fb1[:], fb1_in, writes=["fb1"], sem="wl2")
        em.dma("gpsimd", w2T[:], w2T_in.rearrange("(k p) n -> p k n", p=128), writes=["w2T"], sem="wl")
        em.dma("sync", fb2[:], fb2_in, writes=["fb2"], sem="wl2")
        em.dma("gpsimd", opT[:], opT_in.rearrange("(k p) n -> p k n", p=128), writes=["opT"], sem="wl")
        em.dma("sync", id_sb[:], id_in, writes=["id_sb"], sem="wl2")
        em.dma("sync", ones_sb[:], ones_in, writes=["ones_sb"], sem="wl2")
        em.dma("sync", onesb_sb[:], onesb_in, writes=["onesb_sb"], sem="wl2")
        em.dma("sync", onesr_sb[:], onesr_in, writes=["onesr_sb"], sem="wl2")
        em.dma("sync", rss_sb[:], rss_in, writes=["rss_sb"], sem="wl2")
        em.dma("sync", rsl_sb[:], rsl_in, writes=["rsl_sb"], sem="wl2")
        em.dma("sync", onw_sb[:], onw_in, writes=["onw_sb"], sem="wl2")
        em.dma("sync", lt_sb[:], lt_in, writes=["lt_sb"], sem="wl2")
        em.coalesce(["w1a", "w1b", "w2T", "opT"], "wl")
        em.coalesce(["fb1", "fb2", "id_sb", "ones_sb", "onesb_sb", "onesr_sb",
                     "rss_sb", "rsl_sb", "onw_sb", "lt_sb"], "wl2")
        em.op("scalar", lambda e: e.activation(out=it_sb[:], in_=lt_sb[:], func=AF.Abs),
              reads=["lt_sb"], writes=["it1"], tag="t1")
        em.op("scalar", lambda e: e.activation(out=it_sb[:], in_=it_sb[:], func=AF.Exp, scale=-1.0),
              reads=["it1"], writes=["it2"], tag="t2")
        em.op("scalar", lambda e: e.activation(out=it_sb[:], in_=it_sb[:], func=AF.Ln, bias=1.0),
              reads=["it2"], writes=["it3"], tag="t3")
        em.op("vector", lambda e: e.tensor_scalar_max(out=lt_sb[:], in0=lt_sb[:], scalar1=0.0),
              reads=["lt_sb"], writes=["lt_r", "lt_sb"], tag="t4")
        em.op("vector", lambda e: e.tensor_add(out=it_sb[:], in0=it_sb[:], in1=lt_sb[:]),
              reads=["it3", "lt_r"], writes=["it4"], tag="t5")
        em.op("vector", lambda e: e.tensor_scalar_add(out=it_sb[:], in0=it_sb[:], scalar1=1e-4),
              reads=["it4"], writes=["it5"], tag="t6")
        em.op("vector", lambda e: e.reciprocal(out=it_sb[:], in_=it_sb[:]),
              reads=["it5"], writes=["it6"], tag="t7")
        em.op("vector", lambda e: e.tensor_copy(out=idf[:], in_=id_sb[:]),
              reads=["id_sb"], writes=["idf"], tag="idf")
        em.op("tensor", lambda e: e.matmul(ps_b1[:, 0:1], onesr_sb[:], it_sb[:], start=True, stop=True),
              reads=["it6", "onesr_sb"], writes=["ps_b1"], tag="t8")
        em.op("vector", lambda e: e.tensor_copy(out=itb_sb[:], in_=ps_b1[:, 0:1]),
              reads=["ps_b1"], writes=["itb_sb"], tag="t9")

        names = ("ls", "ll", "do", "v")
        for c in range(NCH):
            t0 = c * TC
            for nm in ("x2", "ls", "ll", "do", "v"):
                src = {"x2": x2_in, "ls": ls_in, "ll": ll_in, "do": do_in, "v": v_in}[nm]
                for dc in range(8):
                    em.dma("sync", comp[nm][:, dc, :], rr(src)[dc, :, t0:t0 + TC],
                           writes=[f"{nm}_{dc}"], sem="comps", tag=nm + "ld")
            em.coalesce([f"{nm}_{dc}" for nm in ("x2", "ls", "ll", "do", "v") for dc in range(8)], "comps")
            for ti, nm in enumerate(names):
                for h in range(4):
                    for cc in (2 * h, 2 * h + 1):
                        em.op("tensor",
                              lambda e, nm=nm, cc=cc, h=h: e.matmul(
                                  ps_r1[:, :], onesb_sb[:], comp[nm][:, cc, :],
                                  start=(cc == 2 * h), stop=(cc == 2 * h + 1)),
                              reads=[f"{nm}_{cc}", "onesb_sb"], writes=["ps_r1"], tag="st_s")
                    for cc in (2 * h, 2 * h + 1):
                        em.op("scalar",
                              lambda e, nm=nm, cc=cc: e.square(out=sqf[:], in_=comp[nm][:, cc, :]),
                              reads=[f"{nm}_{cc}"], writes=["sqf"], tag="st_sq")
                        em.op("tensor",
                              lambda e, cc=cc, h=h: e.matmul(ps_r2[:, :], ones_sb[:], sqf[:],
                                                             start=(cc == 2 * h), stop=(cc == 2 * h + 1)),
                              reads=["sqf", "ones_sb"], writes=["ps_r2"], tag="st_sqm")
                    em.op("vector", lambda e: e.tensor_scalar_mul(out=row1[:], in0=ps_r1[:, :], scalar1=1.0 / 256),
                          reads=["ps_r1"], writes=["row1"], tag="st_m")
                    em.dma("gpsimd", statb[4 * ti:4 * ti + 1, h, :], row1[:],
                           reads=["row1"], writes=[f"stat{ti}0_{h}"], sem="st_r1", tag="strow0")
                    em.op("vector", lambda e: e.tensor_scalar_mul(out=row2[:], in0=ps_r2[:, :], scalar1=1.0 / 256),
                          reads=["ps_r2"], writes=["row2"], tag="st_msq")
                    em.op("vector", lambda e: e.tensor_mul(out=row_t[:], in0=row1[:], in1=row1[:]),
                          reads=["row1"], writes=["row_t"], tag="st_m2")
                    em.op("vector", lambda e: e.tensor_sub(out=row2[:], in0=row2[:], in1=row_t[:]),
                          reads=["row2", "row_t"], writes=["row2v", "row2"], tag="st_v")
                    em.dma("gpsimd", statb[4 * ti + 1:4 * ti + 2, h, :], row2[:],
                           reads=["row2v", "row2"], writes=[f"stat{ti}1_{h}"], sem="st_r2", tag="strow1")
                    em.op("scalar", lambda e: e.sqrt(out=row2[:], in_=ps_r2[:, :]),
                          reads=["ps_r2"], writes=["row2l", "row2"], tag="st_l2")
                    em.dma("gpsimd", statb[4 * ti + 3:4 * ti + 4, h, :], row2[:],
                           reads=["row2l", "row2"], writes=[f"stat{ti}3_{h}"], sem="st_r2", tag="strow3")
                    for cc in (2 * h, 2 * h + 1):
                        em.op("scalar",
                              lambda e, nm=nm, cc=cc: e.activation(out=sqf[:], in_=comp[nm][:, cc, :], func=AF.Abs),
                              reads=[f"{nm}_{cc}"], writes=["sqf"], tag="st_ab")
                        em.op("tensor",
                              lambda e, cc=cc, h=h: e.matmul(ps_r2[:, :], ones_sb[:], sqf[:],
                                                             start=(cc == 2 * h), stop=(cc == 2 * h + 1)),
                              reads=["sqf", "ones_sb"], writes=["ps_r2"], tag="st_abm")
                    em.op("vector", lambda e: e.tensor_scalar_mul(out=row2[:], in0=ps_r2[:, :], scalar1=1.0 / 256),
                          reads=["ps_r2"], writes=["row2a", "row2"], tag="st_am")
                    em.dma("gpsimd", statb[4 * ti + 2:4 * ti + 3, h, :], row2[:],
                           reads=["row2a", "row2"], writes=[f"stat{ti}2_{h}"], sem="st_r2", tag="strow2")
            em.coalesce([f"stat{ti}0_{h}" for ti in range(4) for h in range(4)], "st_r1")
            em.coalesce([f"stat{ti}{si}_{h}" for ti in range(4) for si in (1, 2, 3) for h in range(4)], "st_r2")
            for hc in range(8):
                pb = hc % 2
                for kc in range(8):
                    em.op("tensor",
                          lambda e, hc=hc, kc=kc, pb=pb: e.matmul(
                              ps_mm[pb][:, :], w1a[:, kc, hc * 128:(hc + 1) * 128], comp["x2"][:, kc, :],
                              start=(kc == 0), stop=(kc == 7)),
                          reads=["w1a", f"x2_{kc}"], writes=([f"ps_mm{pb}"] if kc in (0, 7) else []), tag="sh")
                em.op("vector",
                      lambda e, hc=hc, pb=pb: e.tensor_copy(out=shared[:, hc, :], in_=ps_mm[pb][:, :]),
                      reads=[f"ps_mm{pb}"], writes=[f"shared{hc}"], tag="shev")
            allstats = [f"stat{ti}{si}_{h}" for ti in range(4) for si in range(4) for h in range(4)]
            for h in range(4):
                for hc in range(8):
                    pb = hc % 2
                    em.op("tensor",
                          lambda e, h=h, hc=hc, pb=pb: e.matmul(
                              ps_mm[pb][:, :], w1b[:, hc * 128:(hc + 1) * 128], statb[:, h, :],
                              start=True, stop=True),
                          reads=allstats + ["w1b"], writes=[f"ps_mm{pb}"], tag="gb")
                    em.op("vector",
                          lambda e, hc=hc, pb=pb: e.tensor_add(out=tmpg[hc % 2][:], in0=ps_mm[pb][:, :],
                                                               in1=shared[:, hc, :]),
                          reads=[f"ps_mm{pb}", f"shared{hc}"], writes=[f"tmpg{hc % 2}"], tag="gadd")
                    em.op("scalar",
                          lambda e, hc=hc: e.activation(out=hdn2[hc % 2][:], in_=tmpg[hc % 2][:], func=AF.Gelu,
                                                        bias=fb1[:, hc:hc + 1]),
                          reads=[f"tmpg{hc % 2}", "fb1"], writes=[f"hdn2{hc % 2}"], tag="gelu")
                    em.op("tensor",
                          lambda e, hc=hc: e.matmul(ps_lg[:, :], w2T[:, hc, :], hdn2[hc % 2][:],
                                                    start=(hc == 0), stop=(hc == 7)),
                          reads=[f"hdn2{hc % 2}", "w2T"], writes=(["ps_lg"] if hc in (0, 7) else []), tag="lgm")
                em.op("scalar", lambda e: e.activation(out=lgh[:], in_=ps_lg[:, :], func=AF.Identity,
                                                       bias=fb2[:, 0:1]),
                      reads=["ps_lg", "fb2"], writes=["lgh"], tag="lgev")
                for tt in range(4):
                    em.op("tensor",
                          lambda e, tt=tt: e.transpose(ps_tr[:, 0:4], lgh[0:4, tt * 128:(tt + 1) * 128], idf[0:4, 0:4]),
                          reads=["lgh", "id_sb"], writes=["ps_tr"], tag="lgtr")
                    em.op("vector", lambda e: e.tensor_copy(out=fwT[:], in_=ps_tr[:, 0:4]),
                          reads=["ps_tr"], writes=["fwT"], tag="fwcp")
                    em.op("vector", lambda e: e.tensor_reduce(out=mxc[:], in_=fwT[:], axis=AX.X, op=ALU.max),
                          reads=["fwT"], writes=["mxc"], tag="fwmax")
                    em.op("vector", lambda e: e.tensor_scalar(out=fwT[:], in0=fwT[:], scalar1=mxc[:, 0:1],
                                                              scalar2=None, op0=ALU.subtract),
                          reads=["fwT", "mxc"], writes=["fwT"], tag="fwsub")
                    em.op("scalar", lambda e: e.activation(out=fwT[:], in_=fwT[:], func=AF.Exp,
                                                           scale=itb_sb[:, 0:1]),
                          reads=["fwT", "itb_sb"], writes=["fwT"], tag="fwexp")
                    em.op("vector", lambda e: e.tensor_reduce(out=smc[:], in_=fwT[:], axis=AX.X, op=ALU.add),
                          reads=["fwT"], writes=["smc"], tag="fwsum")
                    em.op("vector", lambda e: e.reciprocal(out=smc[:], in_=smc[:]),
                          reads=["smc"], writes=["smc"], tag="fwrec")
                    em.op("vector", lambda e: e.tensor_scalar(out=fwT[:], in0=fwT[:], scalar1=smc[:, 0:1],
                                                              scalar2=None, op0=ALU.mult),
                          reads=["fwT", "smc"], writes=["fwT"], tag="fwmul")
                    em.op("vector", lambda e: e.tensor_add(out=fwT[:, 0:1], in0=fwT[:, 0:1], in1=rss_sb[:]),
                          reads=["fwT", "rss_sb"], writes=["fwT"], tag="fwrss")
                    em.op("vector", lambda e: e.tensor_add(out=fwT[:, 1:2], in0=fwT[:, 1:2], in1=rsl_sb[:]),
                          reads=["fwT", "rsl_sb"], writes=["fwT"], tag="fwrsl")
                    em.op("vector", lambda e: e.tensor_copy(out=tmpf[:, 0:4], in_=fwT[:]),
                          reads=["fwT"], writes=["tmp4"], tag="fw2t")
                    em.op("tensor",
                          lambda e: e.transpose(ps_tr[0:4, 0:128], tmpf[:, 0:4], idf[:]),
                          reads=["tmp4", "idf"], writes=["ps_tr"], tag="fwtr")
                    em.op("scalar", lambda e: e.copy(out=fvt[:], in_=ps_tr[0:4, 0:128]),
                          reads=["ps_tr"], writes=["fvt"], tag="fvev")
                    for comp_i in range(4):
                        em.dma("scalar", rcomp[0:1, comp_i, h, tt * 128:(tt + 1) * 128],
                               fvt[comp_i:comp_i + 1, :], reads=["fvt"],
                               writes=[f"rc{comp_i}_{h}"], sem="rc", tag="rcst")
            em.coalesce([f"rc{ci}_{h}" for ci in range(4) for h in range(4)], "rc")
            for h in range(4):
                for cc in (2 * h, 2 * h + 1):
                    first = True
                    for comp_i, nm in enumerate(names):
                        em.op("tensor",
                              lambda e, comp_i=comp_i, h=h: e.matmul(
                                  ps_b2[:, :], onesr_sb[:], rcomp[0:1, comp_i, h, :],
                                  start=True, stop=True),
                              reads=[f"rc{comp_i}_{h}", "onesr_sb"], writes=["ps_b2"], tag="fbb")
                        if first:
                            em.op("vector",
                                  lambda e, nm=nm, cc=cc: e.tensor_mul(out=o_sb[:], in0=comp[nm][:, cc, :],
                                                                       in1=ps_b2[:, :]),
                                  reads=[f"{nm}_{cc}", "ps_b2"], writes=["o_sb"], tag="omul")
                            first = False
                        else:
                            em.op("vector",
                                  lambda e, nm=nm, cc=cc: e.tensor_mul(out=tmpf[:], in0=comp[nm][:, cc, :],
                                                                       in1=ps_b2[:, :]),
                                  reads=[f"{nm}_{cc}", "ps_b2"], writes=["tmpf"], tag="omul2")
                            em.op("vector",
                                  lambda e: e.tensor_add(out=o_sb[:], in0=o_sb[:], in1=tmpf[:]),
                                  reads=["o_sb", "tmpf"], writes=["o_sb"], tag="oacc")
                    em.op("scalar", lambda e, cc=cc: e.square(out=sqf[:], in_=o_sb[:]),
                          reads=["o_sb"], writes=["sqf"], tag="osq")
                    em.op("tensor",
                          lambda e, cc=cc, h=h: e.matmul(ps_r2[:, :], ones_sb[:], sqf[:],
                                                         start=(cc == 2 * h), stop=(cc == 2 * h + 1)),
                          reads=["sqf", "ones_sb"], writes=["ps_r2"], tag="osqm")
                    em.op("vector",
                          lambda e, cc=cc: e.tensor_copy(out=s_sb[:, cc, :], in_=o_sb[:]),
                          reads=["o_sb"], writes=[f"oh{cc}", f"s{cc}"], tag="okeep")
                em.op("vector", lambda e: e.tensor_scalar_mul(out=row1[:], in0=ps_r2[:, :], scalar1=1.0 / 256),
                      reads=["ps_r2"], writes=["row1"], tag="oms")
                em.op("vector", lambda e: e.tensor_scalar_add(out=row1[:], in0=row1[:], scalar1=1e-5),
                      reads=["row1"], writes=["row1"], tag="oeps")
                em.op("scalar", lambda e: e.activation(out=row1[:], in_=row1[:], func=AF.Ln),
                      reads=["row1"], writes=["row1"], tag="oln")
                em.op("scalar", lambda e: e.activation(out=row1[:], in_=row1[:], func=AF.Exp, scale=-0.5),
                      reads=["row1"], writes=["row1"], tag="oex")
                em.op("tensor", lambda e: e.matmul(ps_b2[:, :], onesr_sb[:], row1[:], start=True, stop=True),
                      reads=["row1", "onesr_sb"], writes=["ps_b2"], tag="obc")
                for cc in (2 * h, 2 * h + 1):
                    em.op("vector",
                          lambda e, cc=cc: e.scalar_tensor_tensor(
                              out=on_sb[:, cc, :], in0=s_sb[:, cc, :], scalar=onw_sb[:, cc % 2, 0:1],
                              in1=ps_b2[:, :], op0=ALU.mult, op1=ALU.mult),
                          reads=[f"oh{cc}", "onw_sb", "ps_b2"], writes=[f"on{cc}"], tag="onorm")
            for dc in range(8):
                pb = dc % 2
                for kc in range(8):
                    em.op("tensor",
                          lambda e, dc=dc, kc=kc, pb=pb: e.matmul(
                              ps_mm[pb][:, :], opT[:, kc, dc * 128:(dc + 1) * 128], on_sb[:, kc, :],
                              start=(kc == 0), stop=(kc == 7)),
                          reads=["opT", f"on{kc}"], writes=([f"ps_mm{pb}"] if kc in (0, 7) else []), tag="op")
                em.op("vector",
                      lambda e, dc=dc, pb=pb: e.tensor_add(out=s_sb[:, dc, :], in0=ps_mm[pb][:, :],
                                                           in1=comp["x2"][:, dc, :]),
                      reads=[f"ps_mm{pb}", f"x2_{dc}"], writes=[f"s{dc}"], tag="resadd")
            for dc in range(8):
                em.op("tensor",
                      lambda e, dc=dc: e.matmul(ps_r1[:, :], ones_sb[:], s_sb[:, dc, :],
                                                start=(dc == 0), stop=(dc == 7)),
                      reads=[f"s{dc}", "ones_sb"], writes=(["ps_r1"] if dc in (0, 7) else []), tag="lsum")
            for dc in range(8):
                em.op("scalar",
                      lambda e, dc=dc: e.square(out=sqf[:], in_=s_sb[:, dc, :]),
                      reads=[f"s{dc}"], writes=["sqf"], tag="lsq")
                em.op("tensor",
                      lambda e, dc=dc: e.matmul(ps_r2[:, :], ones_sb[:], sqf[:],
                                                start=(dc == 0), stop=(dc == 7)),
                      reads=["sqf", "ones_sb"], writes=(["ps_r2"] if dc in (0, 7) else []), tag="lsqm")
            em.op("vector", lambda e: e.tensor_scalar_mul(out=row_m[:], in0=ps_r1[:, :], scalar1=1.0 / D),
                  reads=["ps_r1"], writes=["row_m"], tag="lm")
            em.op("vector", lambda e: e.tensor_scalar_mul(out=row_v[:], in0=ps_r2[:, :], scalar1=1.0 / D),
                  reads=["ps_r2"], writes=["row_v"], tag="lv")
            em.op("vector", lambda e: e.tensor_mul(out=row_t[:], in0=row_m[:], in1=row_m[:]),
                  reads=["row_m"], writes=["row_t"], tag="lm2")
            em.op("vector", lambda e: e.tensor_sub(out=row_v[:], in0=row_v[:], in1=row_t[:]),
                  reads=["row_v", "row_t"], writes=["row_v"], tag="lvar")
            em.op("vector", lambda e: e.tensor_scalar_add(out=row_v[:], in0=row_v[:], scalar1=1e-5),
                  reads=["row_v"], writes=["row_v"], tag="leps")
            em.op("scalar", lambda e: e.activation(out=row_b[:], in_=row_v[:], func=AF.Ln),
                  reads=["row_v"], writes=["row_b"], tag="lln")
            em.op("scalar", lambda e: e.activation(out=row_b[:], in_=row_b[:], func=AF.Exp, scale=-0.5),
                  reads=["row_b"], writes=["row_b"], tag="lrstd")
            em.op("vector", lambda e: e.tensor_mul(out=row_a[:], in0=row_m[:], in1=row_b[:]),
                  reads=["row_m", "row_b"], writes=["row_a"], tag="la1")
            em.op("vector", lambda e: e.tensor_scalar_mul(out=row_a[:], in0=row_a[:], scalar1=-1.0),
                  reads=["row_a"], writes=["row_a"], tag="la2")
            em.op("tensor", lambda e: e.matmul(ps_b1[:, :], onesr_sb[:], row_b[:], start=True, stop=True),
                  reads=["row_b", "onesr_sb"], writes=["ps_b1"], tag="lbcb")
            em.op("tensor", lambda e: e.matmul(ps_b2[:, :], onesr_sb[:], row_a[:], start=True, stop=True),
                  reads=["row_a", "onesr_sb"], writes=["ps_b2"], tag="lbca")
            for dc in range(8):
                em.op("vector",
                      lambda e, dc=dc: e.tensor_mul(out=tmpf[:], in0=s_sb[:, dc, :], in1=ps_b1[:, :]),
                      reads=[f"s{dc}", "ps_b1"], writes=["tmpf"], tag="lapb")
                em.op("vector",
                      lambda e, dc=dc: e.tensor_add(out=x3_sb[:], in0=tmpf[:], in1=ps_b2[:, :]),
                      reads=["tmpf", "ps_b2"], writes=["x3_sb"], tag="lapa")
                em.dma("sync", rr(x3_out)[dc, :, t0:t0 + TC], x3_sb[:],
                       reads=["x3_sb"], tag="x3st")
        em.finish(block)
    return nc


def run_l1c(inputs, x2, l1a_out, do):
    fw1 = f32c(inputs["L1_attn_fuse_w1"])
    in_common = dict(
        w1aT=fw1[:, :D].T.copy(),
        w1bT=fw1[:, D:].T.copy(),
        fb1=f32c(inputs["L1_attn_fuse_b1"]).reshape(8, 128).T.copy(),
        w2T=f32c(inputs["L1_attn_fuse_w2"]).T.copy(),
        fb2=f32c(inputs["L1_attn_fuse_b2"]).reshape(4, 1),
        lt=f32c(inputs["L1_attn_logit_temp"]).reshape(1, 1),
        rss=np.broadcast_to(f32c(inputs["L1_attn_res_scale_short"]).reshape(1, 1), (128, 1)).copy(),
        rsl=np.broadcast_to(f32c(inputs["L1_attn_res_scale_long"]).reshape(1, 1), (128, 1)).copy(),
        onw=f32c(inputs["L1_attn_o_norm_w"]).reshape(2, 128, 1).transpose(1, 0, 2).copy(),
        opT=f32c(inputs["L1_attn_o_proj_w"]).T.copy(),
        identb=np.eye(128, dtype=np.float32).astype(BF),
        ones=np.ones((128, 1), np.float32),
        onesb=np.ones((128, 1), np.float32).astype(BF),
        onesr=np.ones((1, 128), np.float32),
    )
    in_maps = []
    for c in range(NCORES):
        b, sg = divmod(c, 4)
        t0 = sg * SEG
        m = dict(in_common)
        m.update(
            x2=bf16c(x2[b, :, t0:t0 + SEG]),
            ls=bf16c(l1a_out["ls"][b, :, t0:t0 + SEG]),
            ll=bf16c(l1a_out["ll"][b, :, t0:t0 + SEG]),
            do=bf16c(do[b, :, t0:t0 + SEG]),
            v=bf16c(l1a_out["v"][b, :, t0:t0 + SEG]),
        )
        in_maps.append(m)
    if "l1c" not in _BUILD_CACHE:
        _BUILD_CACHE["l1c"] = build_l1c()
    res = run_bass_kernel_spmd(_BUILD_CACHE["l1c"], in_maps, core_ids=list(range(NCORES)))
    x3 = np.empty((B, D, L), BF)
    for c in range(NCORES):
        b, sg = divmod(c, 4)
        x3[b, :, sg * SEG:(sg + 1) * SEG] = res.results[c]["x3"]
    return x3


def build_l1f():
    nc = bass.Bass("TRN2", target_bir_lowering=False)
    TC = 512
    NCH = SEG // TC

    x3_in = nc.dram_tensor("x3", [D, SEG], BF16, kind="ExternalInput").ap()
    xp_in = nc.dram_tensor("xp", [128, 8, 1], BF16, kind="ExternalInput").ap()
    wrT_in = nc.dram_tensor("wrT", [D, D], F32, kind="ExternalInput").ap()
    wkT_in = nc.dram_tensor("wkT", [D, D], F32, kind="ExternalInput").ap()
    wvT_in = nc.dram_tensor("wvT", [D, D], F32, kind="ExternalInput").ap()
    mur_in = nc.dram_tensor("mur", [128, 8, 1], F32, kind="ExternalInput").ap()
    muk_in = nc.dram_tensor("muk", [128, 8, 1], F32, kind="ExternalInput").ap()
    ones_in = nc.dram_tensor("ones", [128, 1], F32, kind="ExternalInput").ap()
    onesr_in = nc.dram_tensor("onesr", [1, 128], F32, kind="ExternalInput").ap()

    x4_out = nc.dram_tensor("x4", [D, SEG], F32, kind="ExternalOutput").ap()

    es = ExitStack()
    T = lambda *a: es.enter_context(nc.sbuf_tensor(*a))
    P = lambda *a: es.enter_context(nc.psum_tensor(*a))

    wr = T("wr_s", [128, 8, D], BF16)
    wk = T("wk_s", [128, 8, D], BF16)
    wv = T("wv_s", [128, 8, D], BF16)
    mur = T("mur_s", [128, 8, 1], F32)
    muk = T("muk_s", [128, 8, 1], F32)
    ones_sb = T("ones_sb", [128, 1], F32)
    onesr_sb = T("onesr_sb", [1, 128], F32)
    ext = T("ext", [128, 8, 1 + TC], BF16)
    dcol = T("dcol", [128, TC], BF16)
    xr_sb = T("xr_sb", [128, 8, TC], BF16)
    xk_sb = T("xk_sb", [128, 8, TC], BF16)
    r_sb = T("r_sb", [128, 8, TC], BF16)
    k2_sb = T("k2_sb", [128, 8, TC], BF16)
    ka = T("ka", [128, TC], F32)
    tmpf = T("tmpf", [128, TC], F32)
    sqf = T("sqf", [128, TC], F32)
    sqf2 = T("sqf2", [128, TC], F32)
    s_sb = T("s_sb", [128, 8, TC], F32)
    row_m = T("row_m", [1, TC], F32)
    row_v = T("row_v", [1, TC], F32)
    row_t = T("row_t", [1, TC], F32)
    row_b = T("row_b", [1, TC], F32)
    row_a = T("row_a", [1, TC], F32)
    x4_sb = T("x4_sb", [128, TC], F32)
    onesb_sb = T("onesb_sb", [128, 1], BF16)

    ps_mm = [P(f"ps_mm{i}", [128, TC], F32) for i in range(2)]
    ps_r1 = P("ps_r1", [1, TC], F32)
    ps_r2 = P("ps_r2", [1, TC], F32)
    ps_a = P("ps_a", [128, TC], F32)
    ps_b = P("ps_b", [128, TC], F32)

    em = Em(nc, es)
    block = es.enter_context(nc.Block())
    with es:
        em.dma("gpsimd", wr[:], wrT_in.rearrange("(k p) n -> p k n", p=128), writes=["wr"], sem="wl")
        em.dma("gpsimd", wk[:], wkT_in.rearrange("(k p) n -> p k n", p=128), writes=["wk"], sem="wl")
        em.dma("gpsimd", wv[:], wvT_in.rearrange("(k p) n -> p k n", p=128), writes=["wv"], sem="wl")
        em.dma("sync", mur[:], mur_in, writes=["mur"], sem="wl2")
        em.dma("sync", muk[:], muk_in, writes=["muk"], sem="wl2")
        em.dma("sync", ones_sb[:], ones_in, writes=["ones_sb"], sem="wl2")
        em.dma("sync", onesr_sb[:], onesr_in, writes=["onesr_sb"], sem="wl2")
        em.dma("sync", ext[:, :, 0:1], xp_in, writes=[f"extc{dc}" for dc in range(8)], sem="wl2",
               allow_slow_non_contiguous=True)
        em.coalesce(["wr", "wk", "wv"], "wl")
        em.coalesce(["mur", "muk", "ones_sb", "onesr_sb"] + [f"extc{dc}" for dc in range(8)], "wl2")
        em.op("vector", lambda e: e.memset(onesb_sb[:], 1.0), writes=["onesb_sb"], tag="ob")

        for c in range(NCH):
            t0 = c * TC
            for dc in range(8):
                em.dma("sync", ext[:, dc, 1:], rr(x3_in)[dc, :, t0:t0 + TC],
                       reads=[f"extc{dc}"], writes=[f"ext{dc}"], sem="xl", tag="xld")
            em.coalesce([f"ext{dc}" for dc in range(8)], "xl")
            for dc in range(8):
                em.op("vector",
                      lambda e, dc=dc: e.tensor_sub(out=dcol[:], in0=ext[:, dc, 1:], in1=ext[:, dc, 0:TC]),
                      reads=[f"ext{dc}", f"extc{dc}"], writes=["dcol"], tag="d")
                em.op("vector",
                      lambda e, dc=dc: e.scalar_tensor_tensor(
                          out=xr_sb[:, dc, :], in0=dcol[:], scalar=mur[:, dc, 0:1],
                          in1=ext[:, dc, 0:TC], op0=ALU.mult, op1=ALU.add),
                      reads=["dcol", "mur", f"ext{dc}"], writes=[f"xr{dc}"], tag="xr")
                em.op("vector",
                      lambda e, dc=dc: e.scalar_tensor_tensor(
                          out=xk_sb[:, dc, :], in0=dcol[:], scalar=muk[:, dc, 0:1],
                          in1=ext[:, dc, 0:TC], op0=ALU.mult, op1=ALU.add),
                      reads=["dcol", "muk", f"ext{dc}"], writes=[f"xk{dc}"], tag="xk")
                em.op("vector",
                      lambda e, dc=dc: e.tensor_copy(out=ext[:, dc, 0:1], in_=ext[:, dc, TC:TC + 1]),
                      reads=[f"ext{dc}", f"xr{dc}", f"xk{dc}"], writes=[f"extc{dc}"], tag="tail")
            for dc in range(8):
                pb = dc % 2
                for kc in range(8):
                    em.op("tensor",
                          lambda e, dc=dc, kc=kc, pb=pb: e.matmul(
                              ps_mm[pb][:, :], wr[:, kc, dc * 128:(dc + 1) * 128], xr_sb[:, kc, :],
                              start=(kc == 0), stop=(kc == 7)),
                          reads=["wr", f"xr{kc}"], writes=([f"ps_mm{pb}"] if kc in (0, 7) else []), tag="rmm")
                em.op("scalar",
                      lambda e, dc=dc, pb=pb: e.activation(out=r_sb[:, dc, :], in_=ps_mm[pb][:, :],
                                                           func=AF.Sigmoid),
                      reads=[f"ps_mm{pb}"], writes=[f"r{dc}"], tag="rsig")
            for dc in range(8):
                pb = dc % 2
                for kc in range(8):
                    em.op("tensor",
                          lambda e, dc=dc, kc=kc, pb=pb: e.matmul(
                              ps_mm[pb][:, :], wk[:, kc, dc * 128:(dc + 1) * 128], xk_sb[:, kc, :],
                              start=(kc == 0), stop=(kc == 7)),
                          reads=["wk", f"xk{kc}"], writes=([f"ps_mm{pb}"] if kc in (0, 7) else []), tag="kmm")
                em.op("scalar",
                      lambda e, dc=dc, pb=pb: e.activation(out=ka[:], in_=ps_mm[pb][:, :], func=AF.Relu),
                      reads=[f"ps_mm{pb}"], writes=["ka"], tag="krelu")
                em.op("scalar",
                      lambda e, dc=dc: e.square(out=k2_sb[:, dc, :], in_=ka[:]),
                      reads=["ka"], writes=[f"k2{dc}"], tag="ksq")
            for dc in range(8):
                pb = dc % 2
                for kc in range(8):
                    em.op("tensor",
                          lambda e, dc=dc, kc=kc, pb=pb: e.matmul(
                              ps_mm[pb][:, :], wv[:, kc, dc * 128:(dc + 1) * 128], k2_sb[:, kc, :],
                              start=(kc == 0), stop=(kc == 7)),
                          reads=["wv", f"k2{kc}"], writes=([f"ps_mm{pb}"] if kc in (0, 7) else []), tag="vmm")
                em.op("vector",
                      lambda e, dc=dc, pb=pb: e.tensor_mul(out=tmpf[:], in0=r_sb[:, dc, :], in1=ps_mm[pb][:, :]),
                      reads=[f"r{dc}", f"ps_mm{pb}"], writes=["tmpf"], tag="rkv")
                em.op("vector",
                      lambda e, dc=dc: e.tensor_add(out=s_sb[:, dc, :], in0=tmpf[:], in1=ext[:, dc, 1:]),
                      reads=["tmpf", f"ext{dc}"], writes=[f"s{dc}"], tag="res")
            for dc in range(8):
                em.op("tensor",
                      lambda e, dc=dc: e.matmul(ps_r1[:, :], ones_sb[:], s_sb[:, dc, :],
                                                start=(dc == 0), stop=(dc == 7)),
                      reads=[f"s{dc}", "ones_sb"], writes=(["ps_r1"] if dc in (0, 7) else []), tag="lsum")
            for dc in range(8):
                em.op("scalar",
                      lambda e, dc=dc: e.square(out=sqf[:], in_=s_sb[:, dc, :]),
                      reads=[f"s{dc}"], writes=["sqf"], tag="lsq")
                em.op("tensor",
                      lambda e, dc=dc: e.matmul(ps_r2[:, :], ones_sb[:], sqf[:],
                                                start=(dc == 0), stop=(dc == 7)),
                      reads=["sqf", "ones_sb"], writes=(["ps_r2"] if dc in (0, 7) else []), tag="lsqm")
            em.op("vector", lambda e: e.tensor_scalar_mul(out=row_m[:], in0=ps_r1[:, :], scalar1=1.0 / D),
                  reads=["ps_r1"], writes=["row_m"], tag="lm")
            em.op("vector", lambda e: e.tensor_scalar_mul(out=row_v[:], in0=ps_r2[:, :], scalar1=1.0 / D),
                  reads=["ps_r2"], writes=["row_v"], tag="lv")
            em.op("vector", lambda e: e.tensor_mul(out=row_t[:], in0=row_m[:], in1=row_m[:]),
                  reads=["row_m"], writes=["row_t"], tag="lm2")
            em.op("vector", lambda e: e.tensor_sub(out=row_v[:], in0=row_v[:], in1=row_t[:]),
                  reads=["row_v", "row_t"], writes=["row_v"], tag="lvar")
            em.op("vector", lambda e: e.tensor_scalar_add(out=row_v[:], in0=row_v[:], scalar1=1e-5),
                  reads=["row_v"], writes=["row_v"], tag="leps")
            em.op("scalar", lambda e: e.activation(out=row_b[:], in_=row_v[:], func=AF.Ln),
                  reads=["row_v"], writes=["row_b"], tag="lln")
            em.op("scalar", lambda e: e.activation(out=row_b[:], in_=row_b[:], func=AF.Exp, scale=-0.5),
                  reads=["row_b"], writes=["row_b"], tag="lrstd")
            em.op("vector", lambda e: e.tensor_mul(out=row_a[:], in0=row_m[:], in1=row_b[:]),
                  reads=["row_m", "row_b"], writes=["row_a"], tag="la1")
            em.op("vector", lambda e: e.tensor_scalar_mul(out=row_a[:], in0=row_a[:], scalar1=-1.0),
                  reads=["row_a"], writes=["row_a"], tag="la2")
            em.op("tensor", lambda e: e.matmul(ps_b[:, :], onesr_sb[:], row_b[:], start=True, stop=True),
                  reads=["row_b", "onesr_sb"], writes=["ps_b"], tag="lbcb")
            em.op("tensor", lambda e: e.matmul(ps_a[:, :], onesr_sb[:], row_a[:], start=True, stop=True),
                  reads=["row_a", "onesr_sb"], writes=["ps_a"], tag="lbca")
            for dc in range(8):
                em.op("vector",
                      lambda e, dc=dc: e.tensor_mul(out=x4_sb[:], in0=s_sb[:, dc, :], in1=ps_b[:, :]),
                      reads=[f"s{dc}", "ps_b"], writes=["x4_sb"], tag="lapb")
                em.op("vector",
                      lambda e, dc=dc: e.tensor_add(out=x4_sb[:], in0=x4_sb[:], in1=ps_a[:, :]),
                      reads=["x4_sb", "ps_a"], writes=["x4_sb"], tag="lapa")
                em.dma("sync", rr(x4_out)[dc, :, t0:t0 + TC], x4_sb[:],
                       reads=["x4_sb"], sem="x4s", tag="x4st")
        em.finish(block)
    return nc


def run_l1f(inputs, x3):
    in_common = dict(
        wrT=f32c(inputs["L1_ffn_W_r"]).T.copy(),
        wkT=f32c(inputs["L1_ffn_W_k"]).T.copy(),
        wvT=f32c(inputs["L1_ffn_W_v"]).T.copy(),
        mur=f32c(inputs["L1_ffn_mu_r"]).reshape(8, 128, 1).transpose(1, 0, 2).copy(),
        muk=f32c(inputs["L1_ffn_mu_k"]).reshape(8, 128, 1).transpose(1, 0, 2).copy(),
        ones=np.ones((128, 1), np.float32),
        onesr=np.ones((1, 128), np.float32),
    )
    in_maps = []
    for c in range(NCORES):
        b, sg = divmod(c, 4)
        t0 = sg * SEG
        xp = np.zeros((128, 8, 1), BF)
        if sg > 0:
            xp = np.ascontiguousarray(x3[b, :, t0 - 1].reshape(8, 128, 1).transpose(1, 0, 2))
        m = dict(in_common)
        m.update(x3=bf16c(x3[b, :, t0:t0 + SEG]), xp=xp)
        in_maps.append(m)
    if "l1f" not in _BUILD_CACHE:
        _BUILD_CACHE["l1f"] = build_l1f()
    res = run_bass_kernel_spmd(_BUILD_CACHE["l1f"], in_maps, core_ids=list(range(NCORES)))
    x4 = np.empty((B, D, L), np.float32)
    for c in range(NCORES):
        b, sg = divmod(c, 4)
        x4[b, :, sg * SEG:(sg + 1) * SEG] = res.results[c]["x4"]
    return x4


def kernel(**inputs):
    l0a = run_l0a(inputs)
    y = run_l0scan(l0a, inputs)
    x1 = run_l0b(inputs, l0a, y)
    x2 = run_l0f(inputs, x1)
    l1a = run_l1a(inputs, x2)
    do = run_l1b(l1a)
    x3 = run_l1c(inputs, x2, l1a, do)
    x4 = run_l1f(inputs, x3)
    return np.ascontiguousarray(x4.transpose(0, 2, 1))


# version 03
# speedup vs baseline: 1.1902x; 1.0251x over previous
import numpy as np
import ml_dtypes
from contextlib import ExitStack

import concourse.bass as bass
import concourse.mybir as mybir
from concourse.bass_utils import run_bass_kernel_spmd
from concourse.bass import IndirectOffsetOnAxis

F32 = mybir.dt.float32
BF16 = mybir.dt.bfloat16
I32 = mybir.dt.int32
AF = mybir.ActivationFunctionType
ALU = mybir.AluOpType
AX = mybir.AxisListType
BF = ml_dtypes.bfloat16

NCORES = 8
B = 2
L = 16384
D = 1024
SEG = L // 4
DIN = 2048
DSTATE = 16
DTRANK = 64

def f32c(x):
    return np.ascontiguousarray(np.asarray(x), dtype=np.float32)

def bf16c(x):
    return np.ascontiguousarray(np.asarray(x).astype(BF))


class Em:
    ENGINES = ("tensor", "vector", "scalar", "gpsimd", "sync")

    def __init__(self, nc, es):
        self.nc = nc
        self.es = es
        self.q = {e: [] for e in self.ENGINES}
        self.cnt = {}
        self.sems = {}
        self.writer = {}
        self.readers = {}

    def _sem(self, key):
        if key not in self.sems:
            self.sems[key] = self.es.enter_context(self.nc.semaphore(f"s_{key}"))
            self.cnt[key] = 0
        return self.sems[key]

    def op(self, engine, fn, reads=(), writes=(), deps=(), dma=False, tag="", semkey=None):
        key = semkey if semkey is not None else engine
        self._sem(key)
        inc = 16 if dma else 1
        waits = [d for d in deps if d is not None]
        for r in reads:
            if r in self.writer:
                waits.append(self.writer[r])
        for w in writes:
            if w in self.writer:
                waits.append(self.writer[w])
            for t in self.readers.get(w, ()):
                waits.append(t)
        self.cnt[key] += inc
        tok = (key, self.cnt[key])
        for r in reads:
            self.readers.setdefault(r, []).append(tok)
        for w in writes:
            self.writer[w] = tok
            self.readers[w] = []
        wm = {}
        for (k, v) in waits:
            wm[k] = max(wm.get(k, 0), v)
        self.q[engine].append((sorted(wm.items()), fn, key, inc, tag))
        return tok

    def dma(self, engine, out, in_, reads=(), writes=(), deps=(), tag="", sem=None, **kw):
        if sem is None:
            sem = "d:" + (writes[0] if writes else (reads[0] if reads else engine))
        return self.op(engine, lambda e: e.dma_start(out=out, in_=in_, **kw),
                       reads=reads, writes=writes, deps=deps, dma=True, tag=tag,
                       semkey=sem)

    def coalesce(self, names, semkey):
        tok = (semkey, self.cnt[semkey])
        for n in names:
            self.writer[n] = tok

    def build(self, block):
        for e in self.ENGINES:
            if not self.q[e]:
                continue

            def make(e):
                def body(eng):
                    done = {}
                    for waits, fn, key, inc, tag in self.q[e]:
                        for (dk, val) in waits:
                            if done.get(dk, 0) >= val:
                                continue
                            eng.wait_ge(self._sem(dk), val)
                            done[dk] = val
                        fn(eng).then_inc(self.sems[key], inc)
                return body
            getattr(block, e)(make(e))

    def finish(self, block):
        toks = [(k, self.cnt[k]) for k in self.cnt]
        self.op("sync", lambda e: e.nop(), deps=toks, tag="final")
        self.build(block)


def rr(ap, p=128):
    return ap.rearrange("(a p) t -> a p t", p=p)


def build_l0a(nch=None, level=9):
    nc = bass.Bass("TRN2", target_bir_lowering=False)
    TC = 512
    NCH = nch if nch is not None else SEG // TC
    H = 3

    x_in = nc.dram_tensor("x", [H + SEG, D], F32, kind="ExternalInput").ap()
    w_in = nc.dram_tensor("w_in", [D, 2 * DIN], F32, kind="ExternalInput").ap()
    conv_w = nc.dram_tensor("conv_w", [128, 16, 4], F32, kind="ExternalInput").ap()
    conv_b = nc.dram_tensor("conv_b", [128, 16, 1], F32, kind="ExternalInput").ap()
    w_x = nc.dram_tensor("w_x", [DIN, 96], F32, kind="ExternalInput").ap()
    w_dt = nc.dram_tensor("w_dt", [DTRANK, DIN], F32, kind="ExternalInput").ap()
    b_dt = nc.dram_tensor("b_dt", [128, 16, 1], F32, kind="ExternalInput").ap()
    xhalo = nc.dram_tensor("xhalo", [128, 8, 3], F32, kind="ExternalInput").ap()
    ident_in = nc.dram_tensor("ident", [128, 128], BF16, kind="ExternalInput").ap()

    xs_out = nc.dram_tensor("xs", [DIN, SEG], BF16, kind="ExternalOutput").ap()
    dt_out = nc.dram_tensor("dt", [DIN, SEG], BF16, kind="ExternalOutput").ap()
    u_out = nc.dram_tensor("u", [DIN, SEG], BF16, kind="ExternalOutput").ap()
    ra_out = nc.dram_tensor("res_act", [DIN, SEG], BF16, kind="ExternalOutput").ap()
    bc_out = nc.dram_tensor("bc", [32, SEG], F32, kind="ExternalOutput").ap()

    es = ExitStack()
    T = lambda *a: es.enter_context(nc.sbuf_tensor(*a))
    P = lambda *a: es.enter_context(nc.psum_tensor(*a))

    w_sb = T("w_sb", [128, 8, 2 * DIN], BF16)
    wx_sb = T("wx_sb", [128, 16, 96], BF16)
    wdt_sb = T("wdt_sb", [DTRANK, DIN], BF16)
    cw_sb = T("cw_sb", [128, 16, 4], F32)
    cb_sb = T("cb_sb", [128, 16, 1], F32)
    bdt_sb = T("bdt_sb", [128, 16, 1], F32)
    id_sb = T("id_sb", [128, 128], BF16)

    xtok = T("xtok", [128, 4, D], F32)
    xtokb = T("xtokb", [128, 4, D], BF16)
    xT = [T(f"xT{i}", [128, 8, TC], BF16) for i in range(2)]
    xhT = T("xhT", [128, 8, H], BF16)
    xc_ext = T("xc_ext", [128, 16, H + TC], BF16)
    xs_sb = T("xs_sb", [128, 16, TC], BF16)
    dt_sb = T("dt_sb", [128, 16, TC], BF16)
    res_bf = T("res_bf", [128, 2, TC], BF16)
    xdbl_sb = T("xdbl_sb", [96, TC], F32)
    dtr_bf = T("dtr_bf", [DTRANK, TC], BF16)
    sp_t1 = T("sp_t1", [128, TC], F32)
    sp_t2 = T("sp_t2", [128, TC], F32)
    u_sb = T("u_sb", [128, 2, TC], BF16)

    ps_tr = [P(f"ps_tr{i}", [128, 128], BF16) for i in range(2)]
    ps_mm = [P(f"ps_mm{i}", [128, TC], F32) for i in range(2)]
    ps_dbl = P("ps_dbl", [96, TC], F32)
    ps_dt = [P(f"ps_dt{i}", [128, TC], F32) for i in range(2)]

    em = Em(nc, es)
    block = es.enter_context(nc.Block())
    with es:
        em.dma("gpsimd", w_sb[:], w_in.rearrange("(k p) n -> p k n", p=128), writes=["w_sb"])
        em.dma("gpsimd", wx_sb[:], w_x.rearrange("(k p) n -> p k n", p=128), writes=["wx_sb"])
        em.dma("gpsimd", wdt_sb[:], w_dt, writes=["wdt_sb"])
        em.dma("sync", cw_sb[:], conv_w, writes=["cw_sb"])
        em.dma("sync", cb_sb[:], conv_b, writes=["cb_sb"])
        em.dma("sync", bdt_sb[:], b_dt, writes=["bdt_sb"])
        em.dma("sync", id_sb[:], ident_in, writes=["id_sb"])
        if level >= 1.2:
            for kc in range(8):
                em.dma("gpsimd", xhT[:, kc, :], xhalo[:, kc, :], writes=[f"xhT{kc}"])

        for oc in range(16 if level >= 2 else 0):
            for kc in range(8):
                em.op("tensor",
                      lambda e, oc=oc, kc=kc: e.matmul(
                          ps_dt[0][:, 0:H], w_sb[:, kc, oc * 128:(oc + 1) * 128], xhT[:, kc, :],
                          start=(kc == 0), stop=(kc == 7)),
                      reads=["w_sb", f"xhT{kc}"], writes=(["ps_dt0"] if kc in (0, 7) else []),
                      tag=f"hmm{oc}")
            em.op("scalar",
                  lambda e, oc=oc: e.copy(out=xc_ext[:, oc, 0:H], in_=ps_dt[0][:, 0:H]),
                  reads=["ps_dt0"], writes=[f"ext_halo{oc}"], tag=f"hev{oc}")

        for c in range(NCH if level >= 1.4 else 0):
            buf = c % 2
            xTn = f"xT{buf}"
            evs = []
            for tt in range(4):
                row0 = H + c * TC + tt * 128
                ld = em.dma("sync", xtok[:, tt, :], x_in[row0:row0 + 128, :],
                            writes=[f"xtok{tt}"], tag=f"xld{c}_{tt}")
                cv = em.op("vector", lambda e, tt=tt: e.tensor_copy(out=xtokb[:, tt, :], in_=xtok[:, tt, :]),
                           reads=[f"xtok{tt}"], writes=[f"xtokb{tt}"], tag="xcv")
                for dc in range(8 if level >= 1.6 else 0):
                    sl = (tt * 8 + dc) % 2
                    tr = em.op("tensor",
                               lambda e, dc=dc, tt=tt, sl=sl: e.transpose(
                                   ps_tr[sl][:, :], xtokb[:, tt, dc * 128:(dc + 1) * 128], id_sb[:]),
                               reads=[f"xtokb{tt}", "id_sb"], writes=[f"ps_tr{sl}"], tag="xtr")
                    ev = em.op("scalar",
                               lambda e, dc=dc, tt=tt, b=buf, sl=sl: e.copy(
                                   out=xT[b][:, dc, tt * 128:(tt + 1) * 128], in_=ps_tr[sl][:, :]),
                               reads=[f"ps_tr{sl}"], writes=[xTn + f"_{dc}_{tt}"], tag="xev")
                    evs.append(ev)

            xT_all = [xTn + f"_{dc}_{tt}" for dc in range(8) for tt in range(4)]
            for oc in range(32 if level >= 3 else 0):
                pb = oc % 2
                if oc < 16 and level < 4:
                    continue
                if oc >= 16 and level < 5:
                    continue
                for kc in range(8):
                    em.op("tensor",
                          lambda e, oc=oc, kc=kc, b=buf, pb=pb: e.matmul(
                              ps_mm[pb][:, :], w_sb[:, kc, oc * 128:(oc + 1) * 128], xT[b][:, kc, :],
                              start=(kc == 0), stop=(kc == 7)),
                          reads=["w_sb"] + [xTn + f"_{kc}_{tt}" for tt in range(4)],
                          writes=([f"ps_mm{pb}"] if kc in (0, 7) else []), tag=f"mm{c}_{oc}")
                if oc < 16:
                    em.op("vector",
                          lambda e, oc=oc, pb=pb: e.tensor_copy(out=xc_ext[:, oc, H:], in_=ps_mm[pb][:, :]),
                          reads=[f"ps_mm{pb}", f"ext_tail{oc}"], writes=[f"ext{oc}"], tag=f"xcev{oc}")
                    t1 = em.op("vector",
                               lambda e, oc=oc: e.tensor_scalar(
                                   out=sp_t1[:], in0=xc_ext[:, oc, 0:TC],
                                   scalar1=cw_sb[:, oc, 0:1],
                                   scalar2=cb_sb[:, oc, 0:1],
                                   op0=ALU.mult, op1=ALU.add),
                               reads=[f"ext{oc}", f"ext_halo{oc}", "cw_sb", "cb_sb"],
                               writes=["sp_t1"], tag="cv0")
                    for j in range(1, 4):
                        t1 = em.op("vector",
                                   lambda e, oc=oc, j=j: e.scalar_tensor_tensor(
                                       out=sp_t1[:], in0=xc_ext[:, oc, j:j + TC],
                                       scalar=cw_sb[:, oc, j:j + 1],
                                       in1=sp_t1[:], op0=ALU.mult, op1=ALU.add),
                                   reads=[f"ext{oc}", "sp_t1"], writes=["sp_t1"], tag=f"cv{j}")
                    em.op("scalar",
                          lambda e, oc=oc: e.activation(out=xs_sb[:, oc, :], in_=sp_t1[:], func=AF.Silu),
                          reads=["sp_t1"], writes=[f"xs{oc}"], tag="sil")
                    em.op("vector",
                          lambda e, oc=oc: e.tensor_copy(out=xc_ext[:, oc, 0:H], in_=xc_ext[:, oc, TC:TC + H]),
                          reads=[f"ext{oc}"], writes=[f"ext_halo{oc}", f"ext_tail{oc}"], tag="tail")
                    em.dma("scalar", rr(xs_out)[oc, :, c * TC:(c + 1) * TC], xs_sb[:, oc, :],
                           reads=[f"xs{oc}"], tag="xs_st")
                    em.op("tensor",
                          lambda e, oc=oc: e.matmul(
                              ps_dbl[:, :], wx_sb[:, oc, :], xs_sb[:, oc, :],
                              start=(oc == 0), stop=(oc == 15)),
                          reads=["wx_sb", f"xs{oc}"], writes=(["ps_dbl"] if oc in (0, 15) else []), tag="dbl")
                else:
                    oc2 = oc - 16
                    rb = oc2 % 2
                    em.op("scalar",
                          lambda e, oc2=oc2, pb=pb, rb=rb: e.activation(
                              out=res_bf[:, rb, :], in_=ps_mm[pb][:, :], func=AF.Silu),
                          reads=[f"ps_mm{pb}"], writes=[f"res_bf{rb}"], tag="ressil")
                    em.dma("scalar", rr(ra_out)[oc2, :, c * TC:(c + 1) * TC], res_bf[:, rb, :],
                           reads=[f"res_bf{rb}"], tag="ra_st")

            em.op("scalar", lambda e: e.copy(out=xdbl_sb[:], in_=ps_dbl[:, :]),
                  reads=["ps_dbl"], writes=["xdbl_sb"], tag="dble")
            em.dma("sync", bc_out[:, c * TC:(c + 1) * TC], xdbl_sb[64:96, :],
                   reads=["xdbl_sb"], tag="bc_st")
            em.op("vector", lambda e: e.tensor_copy(out=dtr_bf[:], in_=xdbl_sb[0:64, :]),
                  reads=["xdbl_sb"], writes=["dtr_bf"], tag="dtrcv")
            for oc in range(16 if level >= 7 else 0):
                db = oc % 2
                em.op("tensor",
                      lambda e, oc=oc, db=db: e.matmul(
                          ps_dt[db][:, :], wdt_sb[:, oc * 128:(oc + 1) * 128], dtr_bf[:],
                          start=True, stop=True),
                      reads=["wdt_sb", "dtr_bf"], writes=[f"ps_dt{db}"], tag="dtmm")
                em.op("scalar",
                      lambda e, oc=oc, db=db: e.activation(out=sp_t1[:], in_=ps_dt[db][:, :], func=AF.Abs,
                                                           bias=bdt_sb[:, oc, 0:1]),
                      reads=[f"ps_dt{db}", "bdt_sb"], writes=["sp_t1"], tag="sp_abs")
                em.op("scalar",
                      lambda e: e.activation(out=sp_t1[:], in_=sp_t1[:], func=AF.Exp, scale=-1.0),
                      reads=["sp_t1"], writes=["sp_t1"], tag="sp_exp")
                em.op("scalar",
                      lambda e: e.activation(out=sp_t1[:], in_=sp_t1[:], func=AF.Ln, bias=1.0),
                      reads=["sp_t1"], writes=["sp_t1"], tag="sp_ln")
                em.op("scalar",
                      lambda e, oc=oc, db=db: e.activation(out=sp_t2[:], in_=ps_dt[db][:, :], func=AF.Relu,
                                                           bias=bdt_sb[:, oc, 0:1]),
                      reads=[f"ps_dt{db}", "bdt_sb"], writes=["sp_t2"], tag="sp_relu")
                em.op("vector",
                      lambda e, oc=oc: e.tensor_add(out=dt_sb[:, oc, :], in0=sp_t1[:], in1=sp_t2[:]),
                      reads=["sp_t1", "sp_t2"], writes=[f"dt{oc}"], tag="sp_add")
                em.dma("sync", rr(dt_out)[oc, :, c * TC:(c + 1) * TC], dt_sb[:, oc, :],
                       reads=[f"dt{oc}"], tag="dt_st")
                ub = oc % 2
                em.op("vector",
                      lambda e, oc=oc, ub=ub: e.tensor_mul(out=u_sb[:, ub, :], in0=dt_sb[:, oc, :], in1=xs_sb[:, oc, :]),
                      reads=[f"dt{oc}", f"xs{oc}"], writes=[f"u_sb{ub}"], tag="u")
                em.dma("gpsimd", rr(u_out)[oc, :, c * TC:(c + 1) * TC], u_sb[:, ub, :],
                       reads=[f"u_sb{ub}"], tag="u_st")
        em.finish(block)
    return nc


_L0A_CACHE = {}

def run_l0a(inputs):
    x = inputs["L0_attn_x"]
    ident = np.eye(128, dtype=np.float32).astype(BF)
    in_maps = []
    for c in range(NCORES):
        b, s = divmod(c, 4)
        t0 = s * SEG
        xs_seg = np.zeros((3 + SEG, D), np.float32)
        xs_seg[3:] = x[b, t0:t0 + SEG]
        if s > 0:
            xs_seg[0:3] = x[b, t0 - 3:t0]
        in_maps.append(dict(
            x=f32c(xs_seg),
            w_in=f32c(inputs["L0_attn_W_in"]),
            conv_w=f32c(inputs["L0_attn_conv_w"]).reshape(16, 128, 4).transpose(1, 0, 2),
            conv_b=f32c(inputs["L0_attn_conv_b"]).reshape(16, 128, 1).transpose(1, 0, 2),
            w_x=f32c(inputs["L0_attn_W_x"]),
            w_dt=f32c(inputs["L0_attn_W_dt"]),
            b_dt=f32c(inputs["L0_attn_b_dt"]).reshape(16, 128, 1).transpose(1, 0, 2),
            xhalo=f32c(xs_seg[0:3].T.reshape(8, 128, 3).transpose(1, 0, 2)),
            ident=ident,
        ))
    if "l0a" not in _L0A_CACHE:
        _L0A_CACHE["l0a"] = build_l0a()
    res = run_bass_kernel_spmd(_L0A_CACHE["l0a"], in_maps, core_ids=list(range(NCORES)))
    out = {}
    for name, dt_, width in [("xs", BF, DIN), ("dt", BF, DIN), ("u", BF, DIN),
                             ("res_act", BF, DIN), ("bc", np.float32, 32)]:
        full = np.empty((B, width, L), dt_)
        for c in range(NCORES):
            b, s = divmod(c, 4)
            full[b, :, s * SEG:(s + 1) * SEG] = res.results[c][name]
        out[name] = full
    return out


def kernel(**inputs):
    raise NotImplementedError


def build_l0scan():
    nc = bass.Bass("TRN2", target_bir_lowering=False)
    DSL = 512
    NT = DSL // 8
    TC = 2048
    NCH = L // TC
    NSUB = TC // 512

    dt_in = nc.dram_tensor("dt", [DSL, L], BF16, kind="ExternalInput").ap()
    u_in = nc.dram_tensor("u", [DSL, L], BF16, kind="ExternalInput").ap()
    bc_in = nc.dram_tensor("bc", [32, L], F32, kind="ExternalInput").ap()
    alog_in = nc.dram_tensor("alog", [128, NT], F32, kind="ExternalInput").ap()
    nsum_in = nc.dram_tensor("nsum", [128, 8], BF16, kind="ExternalInput").ap()
    repl_in = nc.dram_tensor("repl", [8, 128], BF16, kind="ExternalInput").ap()

    y_out = nc.dram_tensor("y", [DSL, L], BF16, kind="ExternalOutput").ap()

    es = ExitStack()
    T = lambda *a: es.enter_context(nc.sbuf_tensor(*a))
    P = lambda *a: es.enter_context(nc.psum_tensor(*a))

    apk = T("apk", [128, NT], F32)
    nsum_sb = T("nsum_sb", [128, 8], BF16)
    hcarry = T("hcarry", [128, NT], F32)
    bexp = [T(f"bexp{i}", [128, TC], BF16) for i in range(2)]
    cexp = [T(f"cexp{i}", [128, TC], BF16) for i in range(2)]
    dtp8 = [T(f"dtp8_{i}", [8, TC], BF16) for i in range(4)]
    up8 = [T(f"up8_{i}", [8, TC], BF16) for i in range(4)]
    repl_sb = T("repl_sb", [8, 128], BF16)
    dA = [T(f"dA{i}", [128, TC], F32) for i in range(4)]
    dBx = [T(f"dBx{i}", [128, TC], BF16) for i in range(4)]
    hs = [T(f"hs{i}", [128, TC], BF16) for i in range(4)]
    zs = [T(f"zs{i}", [128, TC], BF16) for i in range(4)]
    ps_y = [P(f"ps_y{i}", [8, 1024], F32) for i in range(1)]
    ps_dt = [P(f"ps_dt{i}", [128, 1024], F32) for i in range(1)]
    ps_u = [P(f"ps_u{i}", [128, 1024], F32) for i in range(2)]
    ysb = [T(f"ysb{i}", [8, TC], BF16) for i in range(4)]

    em = Em(nc, es)
    block = es.enter_context(nc.Block())
    with es:
        em.dma("sync", apk[:], alog_in, writes=["apk_raw"])
        em.dma("sync", nsum_sb[:], nsum_in, writes=["nsum_sb"])
        em.dma("sync", repl_sb[:], repl_in, writes=["repl_sb"])
        em.op("scalar", lambda e: e.activation(out=apk[:], in_=apk[:], func=AF.Exp),
              reads=["apk_raw"], writes=["apk_e"], tag="a_exp")
        em.op("vector", lambda e: e.tensor_scalar_mul(out=apk[:], in0=apk[:], scalar1=-1.0),
              reads=["apk_e"], writes=["apk"], tag="a_neg")
        em.op("vector", lambda e: e.memset(hcarry[:], 0.0), writes=["hcarry"], tag="h0")

        NT = 64
        def bcload(c):
            cb = c % 2
            t0 = c * TC
            em.dma("gpsimd", bexp[cb][:],
                   bc_in[0:16, t0:t0 + TC].unsqueeze(0).broadcast_to([8, 16, TC]),
                   writes=[f"bexp{cb}"], tag="bld")
            em.dma("gpsimd", cexp[cb][:],
                   bc_in[16:32, t0:t0 + TC].unsqueeze(0).broadcast_to([8, 16, TC]),
                   writes=[f"cexp{cb}"], tag="cld")

        def front(k):
            c, g = divmod(k, NT)
            t0 = c * TC
            cb = c % 2
            gb = k % 4
            pd = k % 2
            em.dma("scalar", dtp8[gb][:],
                   dt_in[g * 8:(g + 1) * 8, t0:t0 + TC],
                   writes=[f"dtp{gb}"], tag="dtld")
            em.dma("sync", up8[gb][:],
                   u_in[g * 8:(g + 1) * 8, t0:t0 + TC],
                   writes=[f"up8{gb}"], tag="uld")
            for half in range(2):
                for sub in range(2):
                    em.op("tensor",
                          lambda e, gb=gb, half=half, sub=sub: e.matmul(
                              ps_dt[0][:, sub * 512:(sub + 1) * 512], repl_sb[:],
                              dtp8[gb][:, half * 1024 + sub * 512:half * 1024 + (sub + 1) * 512],
                              start=True, stop=True),
                          reads=[f"dtp{gb}", "repl_sb"],
                          writes=["ps_dt0"], tag="dtrep")
                em.op("scalar",
                      lambda e, g=g, gb=gb, half=half: e.activation(
                          out=dA[gb][:, half * 1024:(half + 1) * 1024], in_=ps_dt[0][:, :], func=AF.Exp,
                          scale=apk[:, g:g + 1]),
                      reads=["ps_dt0", "apk"], writes=([f"dA{gb}"] if half == 1 else []), tag="exp")
                for sub in range(2):
                    em.op("tensor",
                          lambda e, gb=gb, pd=pd, half=half, sub=sub: e.matmul(
                              ps_u[pd][:, sub * 512:(sub + 1) * 512], repl_sb[:],
                              up8[gb][:, half * 1024 + sub * 512:half * 1024 + (sub + 1) * 512],
                              start=True, stop=True),
                          reads=[f"up8{gb}", "repl_sb"],
                          writes=[f"ps_u{pd}"], tag="urep")
                em.op("vector",
                      lambda e, gb=gb, pd=pd, cb=cb, half=half: e.tensor_mul(
                          out=dBx[gb][:, half * 1024:(half + 1) * 1024], in0=ps_u[pd][:, :],
                          in1=bexp[cb][:, half * 1024:(half + 1) * 1024]),
                      reads=[f"ps_u{pd}", f"bexp{cb}"],
                      writes=([f"dBx{gb}"] if half == 1 else []), tag="dbx")
            em.op("vector",
                  lambda e, g=g, gb=gb: e.tensor_tensor_scan(
                      hs[gb][:], dA[gb][:], dBx[gb][:], hcarry[:, g:g + 1], ALU.mult, ALU.add),
                  reads=[f"dA{gb}", f"dBx{gb}", "hcarry"], writes=[f"hs{gb}"], tag="scan")
            em.op("vector",
                  lambda e, g=g, gb=gb: e.tensor_copy(out=hcarry[:, g:g + 1], in_=hs[gb][:, TC - 1:TC]),
                  reads=[f"hs{gb}"], writes=["hcarry"], tag="hc")

        def mid(k):
            c = k // NT
            cb = c % 2
            gb = k % 4
            em.op("gpsimd",
                  lambda e, gb=gb, cb=cb: e.tensor_mul(out=zs[gb][:], in0=hs[gb][:], in1=cexp[cb][:]),
                  reads=[f"hs{gb}", f"cexp{cb}"], writes=[f"zs{gb}"], tag="z")

        def back(k):
            c, g = divmod(k, NT)
            t0 = c * TC
            gb = k % 4
            for yh in range(2):
                for sub in range(2):
                    em.op("tensor",
                          lambda e, gb=gb, yh=yh, sub=sub: e.matmul(
                              ps_y[0][:, sub * 512:(sub + 1) * 512], nsum_sb[:],
                              zs[gb][:, yh * 1024 + sub * 512:yh * 1024 + (sub + 1) * 512],
                              start=True, stop=True),
                          reads=[f"zs{gb}", "nsum_sb"],
                          writes=["ps_y0"], tag="nsum")
                em.op("scalar",
                      lambda e, gb=gb, yh=yh: e.copy(out=ysb[gb][:, yh * 1024:(yh + 1) * 1024],
                                                     in_=ps_y[0][:, :]),
                      reads=["ps_y0"], writes=[f"ysb{gb}"], tag="yev")
            em.dma("sync", y_out[g * 8:(g + 1) * 8, t0:t0 + TC],
                   ysb[gb][:], reads=[f"ysb{gb}"], tag="yst")

        TOT = NCH * NT
        for k in range(TOT):
            c, g = divmod(k, NT)
            if g == 0:
                bcload(c)
            front(k)
            if k >= 1:
                mid(k - 1)
            if k >= 2:
                back(k - 2)
        mid(TOT - 1)
        back(TOT - 2)
        back(TOT - 1)
        em.finish(block)
    return nc


_BUILD_CACHE = {}

def run_l0scan(l0a_out, inputs):
    alog = f32c(inputs["L0_attn_A_log"])
    in_maps = []
    for c in range(NCORES):
        b, dg = divmod(c, 4)
        dsl = slice(dg * 512, (dg + 1) * 512)
        al = alog[dsl].reshape(64, 8, 16).reshape(64, 128).T
        nsum = np.zeros((128, 8), np.float32)
        for di in range(8):
            nsum[di * 16:(di + 1) * 16, di] = 1.0
        repl = nsum.T.copy()
        in_maps.append(dict(
            dt=bf16c(l0a_out["dt"][b, dsl]),
            u=bf16c(l0a_out["u"][b, dsl]),
            bc=f32c(l0a_out["bc"][b]),
            alog=f32c(al),
            nsum=bf16c(nsum), repl=bf16c(repl),
        ))
    if "l0scan" not in _BUILD_CACHE:
        _BUILD_CACHE["l0scan"] = build_l0scan()
    res = run_bass_kernel_spmd(_BUILD_CACHE["l0scan"], in_maps, core_ids=list(range(NCORES)))
    y = np.empty((B, DIN, L), BF)
    for c in range(NCORES):
        b, dg = divmod(c, 4)
        y[b, dg * 512:(dg + 1) * 512] = res.results[c]["y"]
    return y


def build_l0b():
    nc = bass.Bass("TRN2", target_bir_lowering=False)
    TC = 512
    NCH = SEG // TC

    y_in = nc.dram_tensor("y", [DIN, SEG], BF16, kind="ExternalInput").ap()
    xs_in = nc.dram_tensor("xs", [DIN, SEG], BF16, kind="ExternalInput").ap()
    ra_in = nc.dram_tensor("ra", [DIN, SEG], BF16, kind="ExternalInput").ap()
    x_in = nc.dram_tensor("x", [SEG, D], F32, kind="ExternalInput").ap()
    wout_in = nc.dram_tensor("wout", [DIN, D], F32, kind="ExternalInput").ap()
    dvec_in = nc.dram_tensor("dvec", [128, 16, 1], F32, kind="ExternalInput").ap()
    ident_in = nc.dram_tensor("ident", [128, 128], BF16, kind="ExternalInput").ap()
    ones_in = nc.dram_tensor("ones", [128, 1], F32, kind="ExternalInput").ap()
    onesr_in = nc.dram_tensor("onesr", [1, 128], F32, kind="ExternalInput").ap()

    x1_out = nc.dram_tensor("x1", [D, SEG], BF16, kind="ExternalOutput").ap()

    es = ExitStack()
    T = lambda *a: es.enter_context(nc.sbuf_tensor(*a))
    P = lambda *a: es.enter_context(nc.psum_tensor(*a))

    wout_sb = T("wout_sb", [128, 16, D], BF16)
    dvec_sb = T("dvec_sb", [128, 16, 1], F32)
    id_sb = T("id_sb", [128, 128], BF16)
    ones_sb = T("ones_sb", [128, 1], F32)
    onesr_sb = T("onesr_sb", [1, 128], F32)
    ytile = T("ytile", [128, 2, TC], BF16)
    xstile = T("xstile", [128, 2, TC], BF16)
    ratile = T("ratile", [128, 2, TC], BF16)
    y2_sb = T("y2_sb", [128, 16, TC], BF16)
    tmp_bf = T("tmp_bf", [128, TC], BF16)
    xtok = T("xtok", [128, 4, D], F32)
    xtokb = T("xtokb", [128, 4, D], BF16)
    xfm = T("xfm", [128, 8, TC], BF16)
    s_sb = T("s_sb", [128, 8, TC], F32)
    sq_sb = T("sq_sb", [128, TC], F32)
    row_m = T("row_m", [1, TC], F32)
    row_v = T("row_v", [1, TC], F32)
    row_t = T("row_t", [1, TC], F32)
    row_b = T("row_b", [1, TC], F32)
    row_a = T("row_a", [1, TC], F32)
    tmp_f = T("tmp_f", [128, TC], F32)
    x1_sb = T("x1_sb", [128, 8, TC], BF16)

    ps_tr = [P(f"ps_tr{i}", [128, 128], BF16) for i in range(2)]
    ps_o = [P(f"ps_o{i}", [128, TC], F32) for i in range(2)]
    ps_sum = P("ps_sum", [1, TC], F32)
    ps_sq = P("ps_sq", [1, TC], F32)
    ps_a = P("ps_a", [128, TC], F32)
    ps_b = P("ps_b", [128, TC], F32)

    em = Em(nc, es)
    block = es.enter_context(nc.Block())
    with es:
        em.dma("gpsimd", wout_sb[:], wout_in.rearrange("(k p) n -> p k n", p=128), writes=["wout_sb"])
        em.dma("sync", dvec_sb[:], dvec_in, writes=["dvec_sb"])
        em.dma("sync", id_sb[:], ident_in, writes=["id_sb"], sem="wl2")
        em.dma("sync", ones_sb[:], ones_in, writes=["ones_sb"], sem="wl2")
        em.dma("sync", onesr_sb[:], onesr_in, writes=["onesr_sb"])

        for c in range(NCH):
            t0 = c * TC
            for oc in range(16):
                tb = oc % 2
                em.dma("sync", ytile[:, tb, :], rr(y_in)[oc, :, t0:t0 + TC], writes=[f"yt{tb}"], tag="yld")
                em.dma("sync", xstile[:, tb, :], rr(xs_in)[oc, :, t0:t0 + TC], writes=[f"xst{tb}"], tag="xsld")
                em.dma("scalar", ratile[:, tb, :], rr(ra_in)[oc, :, t0:t0 + TC], writes=[f"rat{tb}"], tag="rald")
                em.op("vector",
                      lambda e, oc=oc, tb=tb: e.scalar_tensor_tensor(
                          out=tmp_bf[:], in0=xstile[:, tb, :], scalar=dvec_sb[:, oc, 0:1],
                          in1=ytile[:, tb, :], op0=ALU.mult, op1=ALU.add),
                      reads=[f"xst{tb}", f"yt{tb}", "dvec_sb"], writes=["tmp_bf"], tag="y2a")
                em.op("vector",
                      lambda e, oc=oc, tb=tb: e.tensor_mul(out=y2_sb[:, oc, :], in0=tmp_bf[:], in1=ratile[:, tb, :]),
                      reads=["tmp_bf", f"rat{tb}"], writes=[f"y2_{oc}"], tag="y2b")
            for tt in range(4):
                row0 = t0 + tt * 128
                em.dma("sync", xtok[:, tt, :], x_in[row0:row0 + 128, :], writes=[f"xtok{tt}"], tag="xld")
                em.op("vector", lambda e, tt=tt: e.tensor_copy(out=xtokb[:, tt, :], in_=xtok[:, tt, :]),
                      reads=[f"xtok{tt}"], writes=[f"xtokb{tt}"], tag="xcv")
                for dc in range(8):
                    sl = (tt * 8 + dc) % 2
                    em.op("tensor",
                          lambda e, dc=dc, tt=tt, sl=sl: e.transpose(
                              ps_tr[sl][:, :], xtokb[:, tt, dc * 128:(dc + 1) * 128], id_sb[:]),
                          reads=[f"xtokb{tt}", "id_sb"], writes=[f"ps_tr{sl}"], tag="xtr")
                    em.op("scalar",
                          lambda e, dc=dc, tt=tt, sl=sl: e.copy(
                              out=xfm[:, dc, tt * 128:(tt + 1) * 128], in_=ps_tr[sl][:, :]),
                          reads=[f"ps_tr{sl}"], writes=[f"xfm{dc}_{tt}"], tag="xev")
            for dc in range(8):
                pb = dc % 2
                for kc in range(16):
                    em.op("tensor",
                          lambda e, dc=dc, kc=kc, pb=pb: e.matmul(
                              ps_o[pb][:, :], wout_sb[:, kc, dc * 128:(dc + 1) * 128], y2_sb[:, kc, :],
                              start=(kc == 0), stop=(kc == 15)),
                          reads=["wout_sb", f"y2_{kc}"],
                          writes=([f"ps_o{pb}"] if kc in (0, 15) else []), tag="wo")
                em.op("vector",
                      lambda e, dc=dc, pb=pb: e.tensor_add(
                          out=s_sb[:, dc, :], in0=ps_o[pb][:, :],
                          in1=xfm[:, dc, :]),
                      reads=[f"ps_o{pb}"] + [f"xfm{dc}_{tt}" for tt in range(4)],
                      writes=[f"s{dc}"], tag="resadd")
            for dc in range(8):
                em.op("tensor",
                      lambda e, dc=dc: e.matmul(ps_sum[:, :], ones_sb[:], s_sb[:, dc, :],
                                                start=(dc == 0), stop=(dc == 7)),
                      reads=[f"s{dc}", "ones_sb"], writes=(["ps_sum"] if dc in (0, 7) else []), tag="sum")
            for dc in range(8):
                em.op("scalar",
                      lambda e, dc=dc: e.square(out=sq_sb[:], in_=s_sb[:, dc, :]),
                      reads=[f"s{dc}"], writes=["sq_sb"], tag="sq")
                em.op("tensor",
                      lambda e, dc=dc: e.matmul(ps_sq[:, :], ones_sb[:], sq_sb[:],
                                                start=(dc == 0), stop=(dc == 7)),
                      reads=["sq_sb", "ones_sb"], writes=(["ps_sq"] if dc in (0, 7) else []), tag="sq_mm")
            em.op("vector", lambda e: e.tensor_scalar_mul(out=row_m[:], in0=ps_sum[:, :], scalar1=1.0 / D),
                  reads=["ps_sum"], writes=["row_m"], tag="m")
            em.op("vector", lambda e: e.tensor_scalar_mul(out=row_v[:], in0=ps_sq[:, :], scalar1=1.0 / D),
                  reads=["ps_sq"], writes=["row_v"], tag="msq")
            em.op("vector", lambda e: e.tensor_mul(out=row_t[:], in0=row_m[:], in1=row_m[:]),
                  reads=["row_m"], writes=["row_t"], tag="m2")
            em.op("vector", lambda e: e.tensor_sub(out=row_v[:], in0=row_v[:], in1=row_t[:]),
                  reads=["row_v", "row_t"], writes=["row_v"], tag="var")
            em.op("vector", lambda e: e.tensor_scalar_add(out=row_v[:], in0=row_v[:], scalar1=1e-5),
                  reads=["row_v"], writes=["row_v"], tag="eps")
            em.op("scalar", lambda e: e.activation(out=row_b[:], in_=row_v[:], func=AF.Ln),
                  reads=["row_v"], writes=["row_b"], tag="ln")
            em.op("scalar", lambda e: e.activation(out=row_b[:], in_=row_b[:], func=AF.Exp, scale=-0.5),
                  reads=["row_b"], writes=["row_b"], tag="rstd")
            em.op("vector", lambda e: e.tensor_mul(out=row_a[:], in0=row_m[:], in1=row_b[:]),
                  reads=["row_m", "row_b"], writes=["row_a"], tag="a1")
            em.op("vector", lambda e: e.tensor_scalar_mul(out=row_a[:], in0=row_a[:], scalar1=-1.0),
                  reads=["row_a"], writes=["row_a"], tag="a2")
            em.op("tensor", lambda e: e.matmul(ps_b[:, :], onesr_sb[:], row_b[:], start=True, stop=True),
                  reads=["row_b", "onesr_sb"], writes=["ps_b"], tag="bcb")
            em.op("tensor", lambda e: e.matmul(ps_a[:, :], onesr_sb[:], row_a[:], start=True, stop=True),
                  reads=["row_a", "onesr_sb"], writes=["ps_a"], tag="bca")
            for dc in range(8):
                em.op("vector",
                      lambda e, dc=dc: e.tensor_mul(out=tmp_f[:], in0=s_sb[:, dc, :], in1=ps_b[:, :]),
                      reads=[f"s{dc}", "ps_b"], writes=["tmp_f"], tag="apb")
                em.op("vector",
                      lambda e, dc=dc: e.tensor_add(out=x1_sb[:, dc, :], in0=tmp_f[:], in1=ps_a[:, :]),
                      reads=["tmp_f", "ps_a"], writes=[f"x1_{dc}"], tag="apa")
                em.dma("sync", rr(x1_out)[dc, :, t0:t0 + TC], x1_sb[:, dc, :],
                       reads=[f"x1_{dc}"], tag="x1st")
        em.finish(block)
    return nc


def run_l0b(inputs, l0a_out, y):
    wout = f32c(inputs["L0_attn_W_out"])
    ident = np.eye(128, dtype=np.float32).astype(BF)
    x = inputs["L0_attn_x"]
    in_maps = []
    for c in range(NCORES):
        b, sg = divmod(c, 4)
        t0 = sg * SEG
        in_maps.append(dict(
            y=bf16c(y[b, :, t0:t0 + SEG]),
            xs=bf16c(l0a_out["xs"][b, :, t0:t0 + SEG]),
            ra=bf16c(l0a_out["res_act"][b, :, t0:t0 + SEG]),
            x=f32c(x[b, t0:t0 + SEG]),
            wout=wout,
            dvec=f32c(inputs["L0_attn_D"]).reshape(16, 128, 1).transpose(1, 0, 2),
            ident=ident,
            ones=np.ones((128, 1), np.float32),
            onesr=np.ones((1, 128), np.float32),
        ))
    if "l0b" not in _BUILD_CACHE:
        _BUILD_CACHE["l0b"] = build_l0b()
    res = run_bass_kernel_spmd(_BUILD_CACHE["l0b"], in_maps, core_ids=list(range(NCORES)))
    x1 = np.empty((B, D, L), BF)
    for c in range(NCORES):
        b, sg = divmod(c, 4)
        x1[b, :, sg * SEG:(sg + 1) * SEG] = res.results[c]["x1"]
    return x1


def build_l0f():
    nc = bass.Bass("TRN2", target_bir_lowering=False)
    TC = 512
    NCH = SEG // TC
    G = TC // 32

    x1_in = nc.dram_tensor("x1", [D, SEG], BF16, kind="ExternalInput").ap()
    wc_in = nc.dram_tensor("wc", [D, 128], F32, kind="ExternalInput").ap()
    bcm_in = nc.dram_tensor("bcm", [128, 1], F32, kind="ExternalInput").ap()
    w1d_in = nc.dram_tensor("w1d", [D, 4096], F32, kind="ExternalInput").ap()
    b1p_in = nc.dram_tensor("b1p", [128, 32], F32, kind="ExternalInput").ap()
    w2d_in = nc.dram_tensor("w2d", [4096, D], F32, kind="ExternalInput").ap()
    b2e_in = nc.dram_tensor("b2e", [128, D], F32, kind="ExternalInput").ap()
    ones_in = nc.dram_tensor("ones", [128, 1], F32, kind="ExternalInput").ap()
    onesb_in = nc.dram_tensor("onesb", [128, 1], BF16, kind="ExternalInput").ap()
    onesr_in = nc.dram_tensor("onesr", [1, 128], F32, kind="ExternalInput").ap()

    x2_out = nc.dram_tensor("x2", [D, SEG], BF16, kind="ExternalOutput").ap()
    perm_dram = nc.dram_tensor("perm_scratch", [128, SEG], BF16, kind="Internal").ap()

    es = ExitStack()
    T = lambda *a: es.enter_context(nc.sbuf_tensor(*a))
    P = lambda *a: es.enter_context(nc.psum_tensor(*a))

    wc_sb = T("wc_sb", [128, 8, 128], BF16)
    bcm_sb = T("bcm_sb", [128, 1], F32)
    w1d_sb = T("w1d_sb", [128, 8, 4096], BF16)
    b1p_sb = T("b1p_sb", [128, 32], F32)
    w2d_sb = T("w2d_sb", [128, 32, D], BF16)
    b2e_sb = T("b2e_sb", [128, D], BF16)
    ones_sb = T("ones_sb", [128, 1], F32)
    onesb_sb = T("onesb_sb", [128, 1], BF16)
    onesr_sb = T("onesr_sb", [1, 128], F32)

    x1t = T("x1t", [128, 8, TC], BF16)
    lgs = T("lgs", [128, TC], F32)
    mx = T("mx", [128, TC // 32], F32)
    perm = T("perm", [128, TC], BF16)
    pe_sb = [T(f"pe_sb{i}", [128, TC], BF16) for i in range(2)]
    h1r = [T(f"h1r{i}", [128, TC], BF16) for i in range(2)]
    h1m = T("h1m", [128, 32, TC], BF16)
    s_sb = T("s_sb", [128, 8, TC], BF16)
    sq_sb = T("sq_sb", [128, TC], F32)
    row_m = T("row_m", [1, TC], F32)
    row_v = T("row_v", [1, TC], F32)
    row_t = T("row_t", [1, TC], F32)
    row_b = T("row_b", [1, TC], F32)
    row_a = T("row_a", [1, TC], F32)
    tmp_f = T("tmp_f", [128, TC], F32)
    x2_sb = T("x2_sb", [128, TC], BF16)

    ps_lg = P("ps_lg", [128, TC], F32)
    ps_h1 = [P(f"ps_h1{i}", [128, TC], F32) for i in range(2)]
    ps_r = [P(f"ps_r{i}", [128, TC], F32) for i in range(2)]
    ps_sum = P("ps_sum", [1, TC], F32)
    ps_sq = P("ps_sq", [1, TC], F32)
    ps_a = P("ps_a", [128, TC], F32)
    ps_b = ps_lg

    em = Em(nc, es)
    block = es.enter_context(nc.Block())
    with es:
        em.dma("gpsimd", wc_sb[:], wc_in.rearrange("(k p) n -> p k n", p=128), writes=["wc_sb"])
        em.dma("sync", bcm_sb[:], bcm_in, writes=["bcm_sb"])
        em.dma("gpsimd", w1d_sb[:], w1d_in.rearrange("(k p) n -> p k n", p=128), writes=["w1d_sb"])
        em.dma("sync", b1p_sb[:], b1p_in, writes=["b1p_sb"])
        em.dma("gpsimd", w2d_sb[:], w2d_in.rearrange("(k p) n -> p k n", p=128), writes=["w2d_sb"])
        em.dma("gpsimd", b2e_sb[:], b2e_in, writes=["b2e_sb"])
        em.dma("sync", ones_sb[:], ones_in, writes=["ones_sb"], sem="wl2")
        em.dma("sync", onesb_sb[:], onesb_in, writes=["onesb_sb"], sem="wl2")
        em.dma("sync", onesr_sb[:], onesr_in, writes=["onesr_sb"])

        for c in range(NCH):
            t0 = c * TC
            for dc in range(8):
                em.dma("sync", x1t[:, dc, :], rr(x1_in)[dc, :, t0:t0 + TC], writes=[f"x1t{dc}"], tag="x1ld")
            for kc in range(8):
                em.op("tensor",
                      lambda e, kc=kc: e.matmul(ps_lg[:, :], wc_sb[:, kc, :], x1t[:, kc, :],
                                                start=(kc == 0), stop=(kc == 7)),
                      reads=["wc_sb", f"x1t{kc}"], writes=(["ps_lg"] if kc in (0, 7) else []), tag="lg")
            em.op("scalar", lambda e: e.activation(out=lgs[:], in_=ps_lg[:, :], func=AF.Identity,
                                                   bias=bcm_sb[:, 0:1]),
                  reads=["ps_lg", "bcm_sb"], writes=["lgs"], tag="lgev")
            em.op("vector", lambda e: e.tensor_reduce(out=mx[:, :, None], in_=lgs[:].rearrange("p (g w) -> p g w", w=32),
                                                      axis=AX.X, op=ALU.max),
                  reads=["lgs"], writes=["mx"], tag="max")
            em.op("vector", lambda e: e.tensor_tensor(
                      out=perm[:].rearrange("p (g w) -> p g w", w=32),
                      in0=lgs[:].rearrange("p (g w) -> p g w", w=32),
                      in1=mx[:, :, None].broadcast_to([128, TC // 32, 32]),
                      op=ALU.is_equal),
                  reads=["lgs", "mx"], writes=["perm"], tag="onehot")
            em.dma("sync", perm_dram[:, t0:t0 + TC], perm[:], reads=["perm"], writes=["perm_dram"], tag="permst")
            for ch in range(32):
                hb = ch % 2
                for kc in range(8):
                    em.op("tensor",
                          lambda e, ch=ch, kc=kc, hb=hb: e.matmul(
                              ps_h1[hb][:, :], w1d_sb[:, kc, ch * 128:(ch + 1) * 128], x1t[:, kc, :],
                              start=(kc == 0), stop=(kc == 7)),
                          reads=["w1d_sb", f"x1t{kc}"], writes=([f"ps_h1{hb}"] if kc in (0, 7) else []),
                          tag="h1")
                em.op("scalar",
                      lambda e, ch=ch, hb=hb: e.activation(out=h1r[hb][:], in_=ps_h1[hb][:, :], func=AF.Relu,
                                                           bias=b1p_sb[:, ch:ch + 1]),
                      reads=[f"ps_h1{hb}", "b1p_sb"], writes=[f"h1r{hb}"], tag="relu")
                em.dma("gpsimd", pe_sb[hb][:],
                       perm_dram[ch * 4:(ch + 1) * 4, t0:t0 + TC].unsqueeze(1).broadcast_to([4, 32, TC]),
                       reads=["perm_dram"], writes=[f"pe{hb}"], tag="peld")
                em.op("vector",
                      lambda e, ch=ch, hb=hb: e.tensor_mul(out=h1m[:, ch, :], in0=h1r[hb][:], in1=pe_sb[hb][:]),
                      reads=[f"h1r{hb}", f"pe{hb}"], writes=[f"h1m{ch}"], tag="mask")
            for dc in range(8):
                pb = dc % 2
                for kc in range(32):
                    em.op("tensor",
                          lambda e, dc=dc, kc=kc, pb=pb: e.matmul(
                              ps_r[pb][:, :], w2d_sb[:, kc, dc * 128:(dc + 1) * 128], h1m[:, kc, :],
                              start=(kc == 0), stop=False),
                          reads=["w2d_sb", f"h1m{kc}"], writes=([f"ps_r{pb}"] if kc == 0 else []),
                          tag="w2")
                em.op("tensor",
                      lambda e, dc=dc, pb=pb: e.matmul(
                          ps_r[pb][:, :], b2e_sb[:, dc * 128:(dc + 1) * 128], perm[:],
                          start=False, stop=True),
                      reads=["b2e_sb", "perm"], writes=[f"ps_r{pb}"], tag="b2")
                em.op("vector",
                      lambda e, dc=dc, pb=pb: e.tensor_add(out=s_sb[:, dc, :], in0=ps_r[pb][:, :],
                                                           in1=x1t[:, dc, :]),
                      reads=[f"ps_r{pb}", f"x1t{dc}"], writes=[f"s{dc}"], tag="resadd")
            for dc in range(8):
                em.op("tensor",
                      lambda e, dc=dc: e.matmul(ps_sum[:, :], onesb_sb[:], s_sb[:, dc, :],
                                                start=(dc == 0), stop=(dc == 7)),
                      reads=[f"s{dc}", "onesb_sb"], writes=(["ps_sum"] if dc in (0, 7) else []), tag="sum")
            for dc in range(8):
                em.op("scalar",
                      lambda e, dc=dc: e.square(out=sq_sb[:], in_=s_sb[:, dc, :]),
                      reads=[f"s{dc}"], writes=["sq_sb"], tag="sq")
                em.op("tensor",
                      lambda e, dc=dc: e.matmul(ps_sq[:, :], ones_sb[:], sq_sb[:],
                                                start=(dc == 0), stop=(dc == 7)),
                      reads=["sq_sb", "ones_sb"], writes=(["ps_sq"] if dc in (0, 7) else []), tag="sqmm")
            em.op("vector", lambda e: e.tensor_scalar_mul(out=row_m[:], in0=ps_sum[:, :], scalar1=1.0 / D),
                  reads=["ps_sum"], writes=["row_m"], tag="m")
            em.op("vector", lambda e: e.tensor_scalar_mul(out=row_v[:], in0=ps_sq[:, :], scalar1=1.0 / D),
                  reads=["ps_sq"], writes=["row_v"], tag="msq")
            em.op("vector", lambda e: e.tensor_mul(out=row_t[:], in0=row_m[:], in1=row_m[:]),
                  reads=["row_m"], writes=["row_t"], tag="m2")
            em.op("vector", lambda e: e.tensor_sub(out=row_v[:], in0=row_v[:], in1=row_t[:]),
                  reads=["row_v", "row_t"], writes=["row_v"], tag="var")
            em.op("vector", lambda e: e.tensor_scalar_add(out=row_v[:], in0=row_v[:], scalar1=1e-5),
                  reads=["row_v"], writes=["row_v"], tag="eps")
            em.op("scalar", lambda e: e.activation(out=row_b[:], in_=row_v[:], func=AF.Ln),
                  reads=["row_v"], writes=["row_b"], tag="ln")
            em.op("scalar", lambda e: e.activation(out=row_b[:], in_=row_b[:], func=AF.Exp, scale=-0.5),
                  reads=["row_b"], writes=["row_b"], tag="rstd")
            em.op("vector", lambda e: e.tensor_mul(out=row_a[:], in0=row_m[:], in1=row_b[:]),
                  reads=["row_m", "row_b"], writes=["row_a"], tag="a1")
            em.op("vector", lambda e: e.tensor_scalar_mul(out=row_a[:], in0=row_a[:], scalar1=-1.0),
                  reads=["row_a"], writes=["row_a"], tag="a2")
            em.op("tensor", lambda e: e.matmul(ps_b[:, :], onesr_sb[:], row_b[:], start=True, stop=True),
                  reads=["row_b", "onesr_sb"], writes=["ps_lg"], tag="bcb")
            em.op("tensor", lambda e: e.matmul(ps_a[:, :], onesr_sb[:], row_a[:], start=True, stop=True),
                  reads=["row_a", "onesr_sb"], writes=["ps_a"], tag="bca")
            for dc in range(8):
                em.op("vector",
                      lambda e, dc=dc: e.tensor_mul(out=tmp_f[:], in0=s_sb[:, dc, :], in1=ps_b[:, :]),
                      reads=[f"s{dc}", "ps_lg"], writes=["tmp_f"], tag="apb")
                em.op("vector",
                      lambda e, dc=dc: e.tensor_add(out=x2_sb[:], in0=tmp_f[:], in1=ps_a[:, :]),
                      reads=["tmp_f", "ps_a"], writes=["x2_sb"], tag="apa")
                em.dma("sync", rr(x2_out)[dc, :, t0:t0 + TC], x2_sb[:],
                       reads=["x2_sb"], tag="x2st")
        em.finish(block)
    return nc


def run_l0f(inputs, x1):
    Wc = f32c(inputs["L0_ffn_Wc"]).reshape(D, 128)
    bcm = f32c(inputs["L0_ffn_bc"]).reshape(128, 1)
    W1 = f32c(inputs["L0_ffn_W1"])
    w1d = W1.transpose(2, 0, 1, 3).reshape(D, 4096)
    b1p = f32c(inputs["L0_ffn_b1"]).reshape(128, 32).T.copy()
    b1flat = f32c(inputs["L0_ffn_b1"]).reshape(4096)
    b1p = b1flat.reshape(32, 128).T.copy()
    W2 = f32c(inputs["L0_ffn_W2"])
    w2d = W2.transpose(0, 1, 3, 2).reshape(4096, D)
    b2e = f32c(inputs["L0_ffn_b2"]).reshape(128, D)
    in_maps = []
    for c in range(NCORES):
        b, sg = divmod(c, 4)
        in_maps.append(dict(
            x1=bf16c(x1[b, :, sg * SEG:(sg + 1) * SEG]),
            wc=Wc, bcm=bcm, w1d=w1d, b1p=b1p, w2d=w2d, b2e=b2e,
            ones=np.ones((128, 1), np.float32),
            onesb=np.ones((128, 1), np.float32).astype(BF),
            onesr=np.ones((1, 128), np.float32),
        ))
    if "l0f" not in _BUILD_CACHE:
        _BUILD_CACHE["l0f"] = build_l0f()
    res = run_bass_kernel_spmd(_BUILD_CACHE["l0f"], in_maps, core_ids=list(range(NCORES)))
    x2 = np.empty((B, D, L), BF)
    for c in range(NCORES):
        b, sg = divmod(c, 4)
        x2[b, :, sg * SEG:(sg + 1) * SEG] = res.results[c]["x2"]
    return x2


def build_l1a():
    nc = bass.Bass("TRN2", target_bir_lowering=False)
    TC = 512
    NCH = SEG // TC + 1
    EXT = NCH * TC
    KF = 64

    x2_in = nc.dram_tensor("x2e", [D, EXT], BF16, kind="ExternalInput").ap()
    qT_in = nc.dram_tensor("qT", [D, D], F32, kind="ExternalInput").ap()
    kT_in = nc.dram_tensor("kT", [D, D], F32, kind="ExternalInput").ap()
    vT_in = nc.dram_tensor("vT", [D, D], F32, kind="ExternalInput").ap()
    bT_in = nc.dram_tensor("bT", [D, 4], F32, kind="ExternalInput").ap()
    qcw_in = nc.dram_tensor("qcw", [128, 8, 4], F32, kind="ExternalInput").ap()
    kcw_in = nc.dram_tensor("kcw", [128, 8, 4], F32, kind="ExternalInput").ap()
    vcw_in = nc.dram_tensor("vcw", [128, 8, 4], F32, kind="ExternalInput").ap()
    fs_in = nc.dram_tensor("fs", [128, 8, 5], F32, kind="ExternalInput").ap()
    fl_in = nc.dram_tensor("fl", [128, 8, KF], F32, kind="ExternalInput").ap()

    q_out = nc.dram_tensor("q", [D, SEG], BF16, kind="ExternalOutput").ap()
    k_out = nc.dram_tensor("k", [D, SEG], BF16, kind="ExternalOutput").ap()
    v_out = nc.dram_tensor("v", [D, SEG], BF16, kind="ExternalOutput").ap()
    beta_out = nc.dram_tensor("beta", [4, SEG], F32, kind="ExternalOutput").ap()
    ls_out = nc.dram_tensor("ls", [D, SEG], BF16, kind="ExternalOutput").ap()
    ll_out = nc.dram_tensor("ll", [D, SEG], BF16, kind="ExternalOutput").ap()
    vsc = nc.dram_tensor("vscratch", [D, EXT], BF16, kind="Internal").ap()

    es = ExitStack()
    T = lambda *a: es.enter_context(nc.sbuf_tensor(*a))
    P = lambda *a: es.enter_context(nc.psum_tensor(*a))

    wq = T("wq", [128, 8, D], BF16)
    wk = T("wk", [128, 8, D], BF16)
    wv = T("wv", [128, 8, D], BF16)
    wb = T("wb", [128, 8, 4], BF16)
    qcw = T("qcw_s", [128, 8, 4], F32)
    kcw = T("kcw_s", [128, 8, 4], F32)
    vcw = T("vcw_s", [128, 8, 4], F32)
    fs_sb = T("fs_sb", [128, 8, 5], F32)
    fl_sb = T("fl_sb", [128, 8, KF], F32)

    xt = T("xt", [128, 8, TC], BF16)
    exts = {n: T(f"ext_{n}", [128, 8, 3 + TC], BF16) for n in "qkv"}
    conv_t = T("conv_t", [128, TC], F32)
    out_bf = T("out_bf", [128, 2, TC], BF16)
    bsig = T("bsig", [4, TC], F32)
    vw = T("vw", [128, 8, KF - 1 + TC], BF16)
    fir_t = T("fir_t", [128, TC], F32)
    fir_p = T("fir_p", [128, TC], F32)
    fir_o = T("fir_o", [128, 2, TC], BF16)

    ps_mm = [P(f"ps_mm{i}", [128, TC], F32) for i in range(4)]
    ps_b = P("ps_b", [4, TC], F32)

    em = Em(nc, es)
    block = es.enter_context(nc.Block())
    with es:
        em.dma("gpsimd", wq[:], qT_in.rearrange("(k p) n -> p k n", p=128), writes=["wq"])
        em.dma("gpsimd", wk[:], kT_in.rearrange("(k p) n -> p k n", p=128), writes=["wk"])
        em.dma("gpsimd", wv[:], vT_in.rearrange("(k p) n -> p k n", p=128), writes=["wv"])
        em.dma("gpsimd", wb[:], bT_in.rearrange("(k p) n -> p k n", p=128), writes=["wb"])
        em.dma("sync", qcw[:], qcw_in, writes=["qcw"])
        em.dma("sync", kcw[:], kcw_in, writes=["kcw"])
        em.dma("sync", vcw[:], vcw_in, writes=["vcw"])
        em.dma("sync", fs_sb[:], fs_in, writes=["fs_sb"])
        em.dma("sync", fl_sb[:], fl_in, writes=["fl_sb"])

        for n2 in "qkv":
            em.op("vector", lambda e, n2=n2: e.memset(exts[n2][:], 0.0),
                  writes=[f"exth_{n2}{oc}" for oc in range(8)] + [f"ext_{n2}{oc}" for oc in range(8)],
                  tag="extz")
        for c in range(NCH):
            t0 = c * TC
            for dc in range(8):
                em.dma("sync", xt[:, dc, :], rr(x2_in)[dc, :, t0:t0 + TC], writes=[f"xt{dc}"], tag="xld")
            for name, w, cw, dst in (("q", wq, qcw, q_out), ("k", wk, kcw, k_out), ("v", wv, vcw, v_out)):
                ext = exts[name]
                for oc in range(8):
                    pb = oc % 4
                    for kc in range(8):
                        em.op("tensor",
                              lambda e, oc=oc, kc=kc, pb=pb, w=w: e.matmul(
                                  ps_mm[pb][:, :], w[:, kc, oc * 128:(oc + 1) * 128], xt[:, kc, :],
                                  start=(kc == 0), stop=(kc == 7)),
                              reads=["w" + name, f"xt{kc}"],
                              writes=([f"ps_mm{pb}"] if kc in (0, 7) else []), tag="proj")
                    em.op("vector",
                          lambda e, oc=oc, pb=pb, ext=ext: e.tensor_copy(out=ext[:, oc, 3:], in_=ps_mm[pb][:, :]),
                          reads=[f"ps_mm{pb}", f"tl_{name}{oc}"], writes=[f"ext_{name}{oc}"], tag="pev")
                    em.op("vector",
                          lambda e, oc=oc, cw=cw, ext=ext: e.tensor_scalar(
                              out=conv_t[:], in0=ext[:, oc, 0:TC], scalar1=cw[:, oc, 0:1], scalar2=None,
                              op0=ALU.mult),
                          reads=[f"ext_{name}{oc}", f"exth_{name}{oc}", name + "cw"], writes=["conv_t"], tag="cv0")
                    for j in range(1, 4):
                        em.op("vector",
                              lambda e, oc=oc, j=j, cw=cw, ext=ext: e.scalar_tensor_tensor(
                                  out=conv_t[:], in0=ext[:, oc, j:j + TC], scalar=cw[:, oc, j:j + 1],
                                  in1=conv_t[:], op0=ALU.mult, op1=ALU.add),
                              reads=[f"ext_{name}{oc}", "conv_t"], writes=["conv_t"], tag=f"cv{j}")
                    ob = oc % 2
                    em.op("scalar",
                          lambda e, ob=ob: e.activation(out=out_bf[:, ob, :], in_=conv_t[:], func=AF.Silu),
                          reads=["conv_t"], writes=[f"out_bf{ob}"], tag="sil")
                    em.op("vector",
                          lambda e, oc=oc, ext=ext: e.tensor_copy(out=ext[:, oc, 0:3], in_=ext[:, oc, TC:TC + 3]),
                          reads=[f"ext_{name}{oc}"], writes=[f"exth_{name}{oc}", f"tl_{name}{oc}"], tag="tail")
                    if c > 0:
                        em.dma("scalar", rr(dst)[oc, :, (c - 1) * TC:c * TC], out_bf[:, ob, :],
                               reads=[f"out_bf{ob}"], tag="post")
                    if name == "v":
                        em.dma("sync", rr(vsc)[oc, :, t0:t0 + TC], out_bf[:, ob, :],
                               reads=[f"out_bf{ob}"], writes=[f"vsc{oc}_{c % 2}"], tag="vst")
            for kc in range(8):
                em.op("tensor",
                      lambda e, kc=kc: e.matmul(ps_b[:, :], wb[:, kc, :], xt[:, kc, :],
                                                start=(kc == 0), stop=(kc == 7)),
                      reads=["wb", f"xt{kc}"], writes=(["ps_b"] if kc in (0, 7) else []), tag="bmm")
            em.op("scalar", lambda e: e.activation(out=bsig[:], in_=ps_b[:, :], func=AF.Sigmoid),
                  reads=["ps_b"], writes=["bsig"], tag="bsig")
            if c > 0:
                em.dma("scalar", beta_out[:, (c - 1) * TC:c * TC], bsig[:], reads=["bsig"], tag="bst")
            if c > 0:
                w0 = c * TC - (KF - 1)
                for oc in range(8):
                    em.dma("sync", vw[:, oc, :], rr(vsc)[oc, :, w0:w0 + KF - 1 + TC],
                           reads=[f"vsc{oc}_{c % 2}", f"vsc{oc}_{(c + 1) % 2}"],
                           writes=[f"vw{oc}"], tag="vwld")
                    em.op("vector",
                          lambda e, oc=oc: e.tensor_scalar(
                              out=fir_t[:], in0=vw[:, oc, KF - 5:KF - 5 + TC], scalar1=fs_sb[:, oc, 0:1],
                              scalar2=None, op0=ALU.mult),
                          reads=[f"vw{oc}", "fs_sb"], writes=["fir_t"], tag="fs0")
                    for j in range(1, 5):
                        em.op("vector",
                              lambda e, oc=oc, j=j: e.scalar_tensor_tensor(
                                  out=fir_t[:], in0=vw[:, oc, KF - 5 + j:KF - 5 + j + TC],
                                  scalar=fs_sb[:, oc, j:j + 1], in1=fir_t[:], op0=ALU.mult, op1=ALU.add),
                              reads=[f"vw{oc}", "fir_t"], writes=["fir_t"], tag="fsj")
                    fb = oc % 2
                    em.op("vector",
                          lambda e, fb=fb: e.tensor_copy(out=fir_o[:, fb, :], in_=fir_t[:]),
                          reads=["fir_t"], writes=[f"fir_o{fb}"], tag="fscv")
                    em.dma("gpsimd", rr(ls_out)[oc, :, (c - 1) * TC:c * TC], fir_o[:, fb, :],
                           reads=[f"fir_o{fb}"], tag="lsst")
                    em.op("vector",
                          lambda e, oc=oc: e.tensor_scalar(
                              out=fir_t[:], in0=vw[:, oc, 0:TC], scalar1=fl_sb[:, oc, 0:1],
                              scalar2=None, op0=ALU.mult),
                          reads=[f"vw{oc}", "fl_sb"], writes=["fir_t"], tag="fl0")
                    for j in range(1, KF):
                        em.op("vector",
                              lambda e, oc=oc, j=j: e.scalar_tensor_tensor(
                                  out=fir_t[:], in0=vw[:, oc, j:j + TC],
                                  scalar=fl_sb[:, oc, j:j + 1], in1=fir_t[:], op0=ALU.mult, op1=ALU.add),
                              reads=[f"vw{oc}", "fir_t"], writes=["fir_t"], tag="flj")
                    em.op("vector",
                          lambda e, fb=fb: e.tensor_copy(out=fir_o[:, fb, :], in_=fir_t[:]),
                          reads=["fir_t"], writes=[f"fir_o{fb}"], tag="flcv")
                    em.dma("gpsimd", rr(ll_out)[oc, :, (c - 1) * TC:c * TC], fir_o[:, fb, :],
                           reads=[f"fir_o{fb}"], tag="llst")
        em.finish(block)
    return nc


def run_l1a(inputs, x2):
    def wT(name):
        return f32c(inputs[name]).T.copy()
    def packc(name):
        return f32c(inputs[name]).reshape(8, 128, 4).transpose(1, 0, 2).copy()
    fir_s = f32c(inputs["L1_attn_fir_short_w"]).reshape(1024, 5)
    fir_l = f32c(inputs["L1_attn_fir_long_w"]).reshape(1024, 64)
    in_maps = []
    for c in range(NCORES):
        b, sg = divmod(c, 4)
        t0 = sg * SEG
        x2e = np.zeros((D, 512 + SEG), BF)
        x2e[:, 512:] = x2[b, :, t0:t0 + SEG]
        if sg > 0:
            x2e[:, :512] = x2[b, :, t0 - 512:t0]
        in_maps.append(dict(
            x2e=x2e,
            qT=wT("L1_attn_q_proj_w"), kT=wT("L1_attn_k_proj_w"), vT=wT("L1_attn_v_proj_w"),
            bT=wT("L1_attn_b_proj_w"),
            qcw=packc("L1_attn_q_conv_w"), kcw=packc("L1_attn_k_conv_w"), vcw=packc("L1_attn_v_conv_w"),
            fs=fir_s.reshape(8, 128, 5).transpose(1, 0, 2).copy(),
            fl=fir_l.reshape(8, 128, 64).transpose(1, 0, 2).copy(),
        ))
    if "l1a" not in _BUILD_CACHE:
        _BUILD_CACHE["l1a"] = build_l1a()
    res = run_bass_kernel_spmd(_BUILD_CACHE["l1a"], in_maps, core_ids=list(range(NCORES)))
    out = {}
    for nm, width in (("q", D), ("k", D), ("v", D), ("ls", D), ("ll", D)):
        full = np.empty((B, width, L), BF)
        for c in range(NCORES):
            b, sg = divmod(c, 4)
            full[b, :, sg * SEG:(sg + 1) * SEG] = res.results[c][nm]
        out[nm] = full
    beta = np.empty((B, 4, L), np.float32)
    for c in range(NCORES):
        b, sg = divmod(c, 4)
        beta[b, :, sg * SEG:(sg + 1) * SEG] = res.results[c]["beta"]
    out["beta"] = beta
    return out


def build_l1b(nsc=None):
    nc = bass.Bass("TRN2", target_bir_lowering=False)
    DH = 256
    SC = 512
    NSC = nsc if nsc is not None else L // SC
    LL = NSC * SC
    NTT = 4

    q_in = nc.dram_tensor("q", [DH, L], BF16, kind="ExternalInput").ap()
    k_in = nc.dram_tensor("k", [DH, L], BF16, kind="ExternalInput").ap()
    v_in = nc.dram_tensor("v", [DH, L], BF16, kind="ExternalInput").ap()
    beta_in = nc.dram_tensor("beta", [128, L // 128], F32, kind="ExternalInput").ap()
    id_in = nc.dram_tensor("identb", [128, 128], BF16, kind="ExternalInput").ap()
    maskl_in = nc.dram_tensor("maskl", [128, 128], BF16, kind="ExternalInput").ap()
    masku_in = nc.dram_tensor("masku", [128, 128], BF16, kind="ExternalInput").ap()
    ones_in = nc.dram_tensor("ones", [128, 1], F32, kind="ExternalInput").ap()
    onesr_in = nc.dram_tensor("onesr", [1, 128], F32, kind="ExternalInput").ap()

    do_out = nc.dram_tensor("dout", [DH, L], BF16, kind="ExternalOutput").ap()
    do_tm = nc.dram_tensor("do_tm", [L, DH], BF16, kind="Internal").ap()

    es = ExitStack()
    T = lambda *a: es.enter_context(nc.sbuf_tensor(*a))
    P = lambda *a: es.enter_context(nc.psum_tensor(*a))

    id_sb = T("id_sb", [128, 128], BF16)
    maskl_sb = T("maskl_sb", [128, 128], BF16)
    masku_sb = T("masku_sb", [128, 128], BF16)
    ones_sb = T("ones_sb", [128, 1], F32)
    onesr_sb = T("onesr_sb", [1, 128], F32)
    beta_sb = T("beta_sb", [128, L // 128], F32)
    negb_sb = T("negb_sb", [128, L // 128], F32)

    qf = [T(f"qf_{i}", [128, 2, SC], BF16) for i in range(2)]
    kf = [T(f"kf_{i}", [128, 2, SC], BF16) for i in range(2)]
    vf = [T(f"vf_{i}", [128, 2, SC], BF16) for i in range(2)]
    qn = [T(f"qn_{i}", [128, 2, SC], BF16) for i in range(2)]
    kn = [T(f"kn_{i}", [128, 2, SC], BF16) for i in range(2)]
    sqs = T("sqs", [128, SC], F32)
    row1 = T("row1", [1, SC], F32)
    ktm = [T(f"ktm_{i}", [128, NTT, DH], BF16) for i in range(2)]
    vtm = [T(f"vtm_{i}", [128, NTT, DH], BF16) for i in range(2)]
    kbtm = [T(f"kbtm_{i}", [128, NTT, DH], BF16) for i in range(2)]
    A_sb = T("A_sb", [128, 128], BF16)
    Bt_sb = T("Bt_sb", [128, 128], BF16)
    A2_sb = T("A2_sb", [128, 128], BF16)
    B2_sb = T("B2_sb", [128, 128], BF16)
    A4_sb = T("A4_sb", [128, 128], BF16)
    B4_sb = T("B4_sb", [128, 128], BF16)
    A8_sb = T("A8_sb", [128, 128], BF16)
    B8_sb = T("B8_sb", [128, 128], BF16)
    A16_sb = T("A16_sb", [128, 128], BF16)
    B16_sb = T("B16_sb", [128, 128], BF16)
    A32_sb = T("A32_sb", [128, 128], BF16)
    B32_sb = T("B32_sb", [128, 128], BF16)
    A64_sb = T("A64_sb", [128, 128], BF16)
    N_sb = T("N_sb", [128, 128], BF16)
    N2_sb = T("N2_sb", [128, 128], BF16)
    utm = [T(f"utm_{i}", [128, NTT, DH], BF16) for i in range(2)]
    wfm = [T(f"wfm_{i}", [128, 2, NTT, 128], BF16) for i in range(2)]
    attnT = [T(f"attnT_{i}", [128, NTT, 128], BF16) for i in range(2)]
    S_bf = T("S_bf", [128, 2, DH], BF16)
    uadj = T("uadj", [128, DH], BF16)
    do_sb = T("do_sb", [128, DH], BF16)
    ttm = T("ttm", [128, DH], BF16)
    dofm = T("dofm", [128, 2, 128], BF16)

    ps_S = [P(f"ps_S{i}", [128, DH], F32) for i in range(2)]
    ps_pw = [P(f"ps_pw{i}", [128, 128], F32) for i in range(2)]
    ps_tr = P("ps_tr", [128, 128], BF16)
    ps_u = P("ps_u", [128, DH], F32)
    ps_o = P("ps_o", [128, DH], F32)
    ps_bc = P("ps_bc", [128, SC], F32)

    em = Em(nc, es)
    block = es.enter_context(nc.Block())
    with es:
        em.dma("sync", id_sb[:], id_in, writes=["id_sb"], sem="wl2")
        em.dma("sync", maskl_sb[:], maskl_in, writes=["maskl_sb"])
        em.dma("sync", masku_sb[:], masku_in, writes=["masku_sb"])
        em.dma("sync", ones_sb[:], ones_in, writes=["ones_sb"], sem="wl2")
        em.dma("sync", onesr_sb[:], onesr_in, writes=["onesr_sb"], sem="wl2")
        em.dma("sync", beta_sb[:], beta_in, writes=["beta_sb"])
        em.coalesce(["id_sb", "ones_sb", "onesr_sb"], "wl2")
        em.op("vector", lambda e: e.tensor_scalar_mul(out=negb_sb[:], in0=beta_sb[:], scalar1=-1.0),
              reads=["beta_sb"], writes=["negb_sb"], tag="negb")

        def prep(sc):
            p = sc % 2
            t0 = sc * SC
            for cc in range(2):
                em.dma("sync", qf[p][:, cc, :], q_in[cc * 128:(cc + 1) * 128, t0:t0 + SC],
                       writes=[f"qf{p}_{cc}"], tag="qld")
                em.dma("sync", kf[p][:, cc, :], k_in[cc * 128:(cc + 1) * 128, t0:t0 + SC],
                       writes=[f"kf{p}_{cc}"], tag="kld")
                em.dma("sync", vf[p][:, cc, :], v_in[cc * 128:(cc + 1) * 128, t0:t0 + SC],
                       writes=[f"vf{p}_{cc}"], tag="vld")
            for nm, src, dst in (("q", qf[p], qn[p]), ("k", kf[p], kn[p])):
                for cc in range(2):
                    em.op("scalar", lambda e, cc=cc, src=src: e.square(out=sqs[:], in_=src[:, cc, :]),
                          reads=[f"{nm}f{p}_{cc}"], writes=["sqs"], tag="l2sq")
                    em.op("tensor",
                          lambda e, cc=cc: e.matmul(ps_bc[0:1, :], ones_sb[:], sqs[:],
                                                    start=(cc == 0), stop=(cc == 1)),
                          reads=["sqs", "ones_sb"], writes=["ps_row", "ps_bcast"], tag="l2mm")
                em.op("vector", lambda e: e.tensor_copy(out=row1[:], in_=ps_bc[0:1, :]),
                      reads=["ps_row"], writes=["row1"], tag="l2cp")
                em.op("scalar", lambda e: e.activation(out=row1[:], in_=row1[:], func=AF.Ln),
                      reads=["row1"], writes=["row1"], tag="l2ln")
                em.op("scalar", lambda e: e.activation(out=row1[:], in_=row1[:], func=AF.Exp, scale=-0.5),
                      reads=["row1"], writes=["row1"], tag="l2ex")
                em.op("tensor", lambda e: e.matmul(ps_bc[:, :], onesr_sb[:], row1[:], start=True, stop=True),
                      reads=["row1", "onesr_sb"], writes=["ps_row", "ps_bcast"], tag="l2bc")
                for cc in range(2):
                    em.op("vector",
                          lambda e, cc=cc, src=src, dst=dst: e.tensor_mul(
                              out=dst[:, cc, :], in0=src[:, cc, :], in1=ps_bc[:, :]),
                          reads=[f"{nm}f{p}_{cc}", "ps_bcast"], writes=[f"{nm}n{p}_{cc}"], tag="l2ap")
            for tt in range(NTT):
                for cc in range(2):
                    em.op("tensor",
                          lambda e, tt=tt, cc=cc: e.transpose(
                              ps_tr[:, :], kn[p][:, cc, tt * 128:(tt + 1) * 128], id_sb[:]),
                          reads=[f"kn{p}_{cc}", "id_sb"], writes=["ps_tr"], tag="ktr")
                    em.op("scalar",
                          lambda e, tt=tt, cc=cc: e.copy(out=ktm[p][:, tt, cc * 128:(cc + 1) * 128],
                                                         in_=ps_tr[:, :]),
                          reads=["ps_tr"], writes=[f"ktm{p}_{tt}"], tag="ktev")
                    em.op("tensor",
                          lambda e, tt=tt, cc=cc: e.transpose(
                              ps_tr[:, :], vf[p][:, cc, tt * 128:(tt + 1) * 128], id_sb[:]),
                          reads=[f"vf{p}_{cc}", "id_sb"], writes=["ps_tr"], tag="vtr")
                    em.op("scalar",
                          lambda e, tt=tt, cc=cc: e.copy(out=vtm[p][:, tt, cc * 128:(cc + 1) * 128],
                                                         in_=ps_tr[:, :]),
                          reads=["ps_tr"], writes=[f"vtm{p}_{tt}"], tag="vtev")
                em.op("vector",
                      lambda e, tt=tt, sc=sc: e.tensor_scalar_mul(
                          out=kbtm[p][:, tt, :], in0=ktm[p][:, tt, :],
                          scalar1=beta_sb[:, sc * NTT + tt:sc * NTT + tt + 1]),
                      reads=[f"ktm{p}_{tt}", "beta_sb"], writes=[f"kbtm{p}_{tt}"], tag="kb")
                em.op("vector",
                      lambda e, tt=tt, sc=sc: e.tensor_scalar_mul(
                          out=vtm[p][:, tt, :], in0=vtm[p][:, tt, :],
                          scalar1=beta_sb[:, sc * NTT + tt:sc * NTT + tt + 1]),
                      reads=[f"vtm{p}_{tt}", "beta_sb"], writes=[f"vtm{p}_{tt}"], tag="vb")
        def prep_tile(sc, tt):
            p = sc % 2
            if True:
                if True:
                    pass
                for cc in range(2):
                    em.op("tensor",
                          lambda e, tt=tt, cc=cc: e.matmul(
                              ps_pw[0][:, :], kn[p][:, cc, tt * 128:(tt + 1) * 128],
                              kn[p][:, cc, tt * 128:(tt + 1) * 128],
                              start=(cc == 0), stop=(cc == 1)),
                          reads=[f"kn{p}_{cc}"], writes=["ps_pw0"], tag="gram")
                em.op("vector",
                      lambda e, tt=tt, sc=sc: e.scalar_tensor_tensor(
                          out=A_sb[:], in0=ps_pw[0][:, :],
                          scalar=negb_sb[:, sc * NTT + tt:sc * NTT + tt + 1],
                          in1=maskl_sb[:], op0=ALU.mult, op1=ALU.mult),
                      reads=["ps_pw0", "negb_sb", "maskl_sb"], writes=["A_sb"], tag="Amask")
                em.op("tensor", lambda e: e.transpose(ps_tr[:, :], A_sb[:], id_sb[:]),
                      reads=["A_sb", "id_sb"], writes=["ps_tr"], tag="Atr")
                em.op("scalar", lambda e: e.copy(out=Bt_sb[:], in_=ps_tr[:, :]),
                      reads=["ps_tr"], writes=["Bt_sb"], tag="Btev")
                for (lt, rh, dst, nm) in ((Bt_sb, A_sb, A2_sb, "A2"), (A_sb, Bt_sb, B2_sb, "B2"),
                                          (B2_sb, A2_sb, A4_sb, "A4"), (A2_sb, B2_sb, B4_sb, "B4"),
                                          (B4_sb, A4_sb, A8_sb, "A8"), (A4_sb, B4_sb, B8_sb, "B8"),
                                          (B8_sb, A8_sb, A16_sb, "A16"), (A8_sb, B8_sb, B16_sb, "B16"),
                                          (B16_sb, A16_sb, A32_sb, "A32"), (A16_sb, B16_sb, B32_sb, "B32"),
                                          (B32_sb, A32_sb, A64_sb, "A64")):
                    pw = 0 if nm in ("A2", "A4", "A8", "A16", "A32", "A64") else 1
                    em.op("tensor",
                          lambda e, lt=lt, rh=rh, pw=pw: e.matmul(ps_pw[pw][:, :], lt[:], rh[:],
                                                                  start=True, stop=True),
                          reads=["A_sb", "Bt_sb", "A2_sb", "B2_sb", "A4_sb", "B4_sb", "A8_sb", "B8_sb", "B16_sb", "A16_sb", "A32_sb", "B32_sb"],
                          writes=[f"ps_pw{pw}"], tag=f"pw{nm}")
                    em.op("scalar", lambda e, dst=dst, pw=pw: e.copy(out=dst[:], in_=ps_pw[pw][:, :]),
                          reads=[f"ps_pw{pw}"], writes=[f"{nm}_sb"], tag=f"ev{nm}")
                em.op("vector", lambda e: e.tensor_add(out=N_sb[:], in0=Bt_sb[:], in1=id_sb[:]),
                      reads=["Bt_sb", "id_sb"], writes=["N_sb"], tag="N1")
                for i, Ak in enumerate((A2_sb, A4_sb, A8_sb, A16_sb, A32_sb, A64_sb)):
                    nmA = ("A2_sb", "A4_sb", "A8_sb", "A16_sb", "A32_sb", "A64_sb")[i]
                    em.op("tensor",
                          lambda e, Ak=Ak: e.matmul(ps_pw[0][:, :], Ak[:], N_sb[:], start=True, stop=False),
                          reads=[nmA, "N_sb"], writes=["ps_pw0"], tag="Np")
                    em.op("tensor",
                          lambda e: e.matmul(ps_pw[0][:, :], id_sb[:], N_sb[:], start=False, stop=True),
                          reads=["id_sb", "N_sb"], writes=["ps_pw0"], tag="Ni")
                    em.op("scalar", lambda e: e.copy(out=N_sb[:], in_=ps_pw[0][:, :]),
                          reads=["ps_pw0"], writes=["N_sb"], tag="Nev")
                em.op("tensor",
                      lambda e, tt=tt: e.matmul(ps_u[:, :], N_sb[:], vtm[p][:, tt, :], start=True, stop=True),
                      reads=["N_sb", f"vtm{p}_{tt}"], writes=["ps_u"], tag="umm")
                em.op("scalar", lambda e, tt=tt: e.copy(out=utm[p][:, tt, :], in_=ps_u[:, :]),
                      reads=["ps_u"], writes=[f"utm{p}_{tt}"], tag="uev")
                for cc in range(2):
                    em.op("tensor",
                          lambda e, tt=tt, cc=cc: e.matmul(
                              ps_u[:, 0:128], kbtm[p][:, tt, cc * 128:(cc + 1) * 128], N_sb[:],
                              start=True, stop=True),
                          reads=[f"kbtm{p}_{tt}", "N_sb"], writes=["ps_u"], tag="wmm")
                    em.op("scalar", lambda e, tt=tt, cc=cc: e.copy(out=wfm[p][:, cc, tt, :], in_=ps_u[:, 0:128]),
                          reads=["ps_u"], writes=[f"wfm{p}_{tt}"], tag="wev")
                for cc in range(2):
                    em.op("tensor",
                          lambda e, tt=tt, cc=cc: e.matmul(
                              ps_pw[0][:, :], kn[p][:, cc, tt * 128:(tt + 1) * 128],
                              qn[p][:, cc, tt * 128:(tt + 1) * 128],
                              start=(cc == 0), stop=(cc == 1)),
                          reads=[f"kn{p}_{cc}", f"qn{p}_{cc}"], writes=["ps_pw0"], tag="atmm")
                em.op("vector",
                      lambda e, tt=tt: e.tensor_mul(out=attnT[p][:, tt, :], in0=ps_pw[0][:, :], in1=masku_sb[:]),
                      reads=["ps_pw0", "masku_sb"], writes=[f"attnT{p}_{tt}"], tag="atmask")

        def serial_chunk(sc, j):
            p = sc % 2
            if True:
                jj = sc * 4 + j
                tt = j
                first = (jj == 0)
                if not first:
                    for cc in range(2):
                        em.op("vector",
                              lambda e, cc=cc: e.tensor_copy(out=S_bf[:, cc, :], in_=ps_S[cc][:, :]),
                              reads=[f"ps_S{cc}"], writes=[f"S_bf{cc}"], tag="Scp")
                    for cc in range(2):
                        em.op("tensor",
                              lambda e, tt=tt, cc=cc: e.matmul(
                                  ps_o[:, :], wfm[p][:, cc, tt, :], S_bf[:, cc, :],
                                  start=(cc == 0), stop=(cc == 1)),
                              reads=[f"wfm{p}_{tt}", f"S_bf{cc}"], writes=["ps_o"], tag="wS")
                    em.op("vector",
                          lambda e, tt=tt: e.tensor_sub(
                              out=uadj[:, :], in0=utm[p][:, tt, :], in1=ps_o[:, :]),
                          reads=[f"utm{p}_{tt}", "ps_o"], writes=["uadj"], tag="uadj")
                else:
                    em.op("vector",
                          lambda e, tt=tt: e.tensor_copy(out=uadj[:, :], in_=utm[p][:, tt, :]),
                          reads=[f"utm{p}_{tt}"], writes=["uadj"], tag="uadj0")
                if not first:
                    for cc in range(2):
                        em.op("tensor",
                              lambda e, tt=tt, cc=cc: e.matmul(
                                  ps_o[:, :], qn[p][:, cc, tt * 128:(tt + 1) * 128], S_bf[:, cc, :],
                                  start=(cc == 0), stop=False),
                              reads=[f"qn{p}_{cc}", f"S_bf{cc}"], writes=(["ps_o"] if cc == 0 else []), tag="qS")
                    em.op("tensor",
                          lambda e, tt=tt: e.matmul(
                              ps_o[:, :], attnT[p][:, tt, :], uadj[:, :],
                              start=False, stop=True),
                          reads=[f"attnT{p}_{tt}", "uadj"], writes=["ps_o"], tag="attnterm")
                else:
                    em.op("tensor",
                          lambda e, tt=tt: e.matmul(
                              ps_o[:, :], attnT[p][:, tt, :], uadj[:, :],
                              start=True, stop=True),
                          reads=[f"attnT{p}_{tt}", "uadj"], writes=["ps_o"], tag="attnterm0")
                em.op("scalar", lambda e: e.copy(out=do_sb[:], in_=ps_o[:, :]),
                      reads=["ps_o"], writes=["do_sb"], tag="doev")
                em.dma("sync", do_tm[jj * 128:(jj + 1) * 128, :], do_sb[:], reads=["do_sb"], tag="dost")
                for cc in range(2):
                    em.op("tensor",
                          lambda e, tt=tt, cc=cc, first=first: e.matmul(
                              ps_S[cc][:, :], ktm[p][:, tt, cc * 128:(cc + 1) * 128], uadj[:, :],
                              start=first, stop=False, skip_group_check=True),
                          reads=[f"ktm{p}_{tt}", "uadj"], writes=[f"ps_S{cc}"], tag="Supd")

        def prep_all_tiles(sc):
            for tt in range(NTT):
                prep_tile(sc, tt)

        prep(0)
        prep_all_tiles(0)
        for sc in range(NSC):
            if sc + 1 < NSC:
                prep(sc + 1)
            for j in range(4):
                serial_chunk(sc, j)
                if sc + 1 < NSC:
                    prep_tile(sc + 1, j)
        all_dost = ("d:do_sb", em.cnt.get("d:do_sb", 0))
        for tt in range(LL // 128):
            em.dma("sync", ttm[:], do_tm[tt * 128:(tt + 1) * 128, :], writes=["ttm"],
                   deps=[all_dost], tag="dold")
            for cc in range(2):
                em.op("tensor",
                      lambda e, cc=cc: e.transpose(ps_tr[:, :], ttm[:, cc * 128:(cc + 1) * 128], id_sb[:]),
                      reads=["ttm", "id_sb"], writes=["ps_tr"], tag="dtr")
                em.op("scalar", lambda e, cc=cc: e.copy(out=dofm[:, cc, :], in_=ps_tr[:, :]),
                      reads=["ps_tr"], writes=[f"dofm{cc}"], tag="dofm")
                em.dma("scalar", do_out[cc * 128:(cc + 1) * 128, tt * 128:(tt + 1) * 128], dofm[:, cc, :],
                       reads=[f"dofm{cc}"], tag="dofst")
        em.finish(block)
    return nc


def run_l1b(l1a_out):
    maskl = np.tril(np.ones((128, 128), np.float32), -1)
    masku = np.triu(np.ones((128, 128), np.float32), 1)
    ident = np.eye(128, dtype=np.float32).astype(BF)
    in_maps = []
    for c in range(NCORES):
        b, h = divmod(c, 4)
        sl = slice(h * 256, (h + 1) * 256)
        in_maps.append(dict(
            q=bf16c(l1a_out["q"][b, sl]),
            k=bf16c(l1a_out["k"][b, sl]),
            v=bf16c(l1a_out["v"][b, sl]),
            beta=f32c(l1a_out["beta"][b, h].reshape(L // 128, 128).T),
            identb=ident,
            maskl=maskl.astype(BF), masku=masku.astype(BF),
            ones=np.ones((128, 1), np.float32),
            onesr=np.ones((1, 128), np.float32),
        ))
    if "l1b" not in _BUILD_CACHE:
        _BUILD_CACHE["l1b"] = build_l1b()
    res = run_bass_kernel_spmd(_BUILD_CACHE["l1b"], in_maps, core_ids=list(range(NCORES)))
    do = np.empty((B, D, L), BF)
    for c in range(NCORES):
        b, h = divmod(c, 4)
        do[b, h * 256:(h + 1) * 256] = res.results[c]["dout"]
    return do


def build_l1c():
    nc = bass.Bass("TRN2", target_bir_lowering=False)
    TC = 512
    NCH = SEG // TC

    x2_in = nc.dram_tensor("x2", [D, SEG], BF16, kind="ExternalInput").ap()
    ls_in = nc.dram_tensor("ls", [D, SEG], BF16, kind="ExternalInput").ap()
    ll_in = nc.dram_tensor("ll", [D, SEG], BF16, kind="ExternalInput").ap()
    do_in = nc.dram_tensor("do", [D, SEG], BF16, kind="ExternalInput").ap()
    v_in = nc.dram_tensor("v", [D, SEG], BF16, kind="ExternalInput").ap()
    w1aT_in = nc.dram_tensor("w1aT", [D, D], F32, kind="ExternalInput").ap()
    w1bT_in = nc.dram_tensor("w1bT", [16, D], F32, kind="ExternalInput").ap()
    fb1_in = nc.dram_tensor("fb1", [128, 8], F32, kind="ExternalInput").ap()
    w2T_in = nc.dram_tensor("w2T", [D, 4], F32, kind="ExternalInput").ap()
    fb2_in = nc.dram_tensor("fb2", [4, 1], F32, kind="ExternalInput").ap()
    lt_in = nc.dram_tensor("lt", [1, 1], F32, kind="ExternalInput").ap()
    rss_in = nc.dram_tensor("rss", [128, 1], F32, kind="ExternalInput").ap()
    rsl_in = nc.dram_tensor("rsl", [128, 1], F32, kind="ExternalInput").ap()
    onw_in = nc.dram_tensor("onw", [128, 2, 1], F32, kind="ExternalInput").ap()
    opT_in = nc.dram_tensor("opT", [D, D], F32, kind="ExternalInput").ap()
    id_in = nc.dram_tensor("identb", [128, 128], BF16, kind="ExternalInput").ap()
    ones_in = nc.dram_tensor("ones", [128, 1], F32, kind="ExternalInput").ap()
    onesb_in = nc.dram_tensor("onesb", [128, 1], BF16, kind="ExternalInput").ap()
    onesr_in = nc.dram_tensor("onesr", [1, 128], F32, kind="ExternalInput").ap()

    x3_out = nc.dram_tensor("x3", [D, SEG], BF16, kind="ExternalOutput").ap()

    es = ExitStack()
    T = lambda *a: es.enter_context(nc.sbuf_tensor(*a))
    P = lambda *a: es.enter_context(nc.psum_tensor(*a))

    w1a = T("w1a", [128, 8, D], BF16)
    w1b = T("w1b", [16, D], BF16)
    fb1 = T("fb1_s", [128, 8], F32)
    w2T = T("w2T_s", [128, 8, 4], BF16)
    fb2 = T("fb2_s", [4, 1], F32)
    opT = T("opT_s", [128, 8, D], BF16)
    id_sb = T("id_sb", [128, 128], BF16)
    ones_sb = T("ones_sb", [128, 1], F32)
    onesb_sb = T("onesb_sb", [128, 1], BF16)
    onesr_sb = T("onesr_sb", [1, 128], F32)
    rss_sb = T("rss_sb", [128, 1], F32)
    rsl_sb = T("rsl_sb", [128, 1], F32)
    onw_sb = T("onw_sb", [128, 2, 1], F32)
    lt_sb = T("lt_sb", [1, 1], F32)
    it_sb = T("it_sb", [1, 1], F32)
    itb_sb = T("itb_sb", [128, 1], F32)
    idf = T("idf", [128, 128], F32)

    comp = {n: T(f"c_{n}", [128, 8, TC], BF16) for n in ("x2", "ls", "ll", "do", "v")}
    sqf = T("sqf", [128, TC], F32)
    sqf2 = T("sqf2", [128, TC], F32)
    row1 = T("row1", [1, TC], F32)
    row2 = T("row2", [1, TC], F32)
    statb = T("statb", [16, 4, TC], BF16)
    shared = T("shared", [128, 8, TC], F32)
    tmpf = T("tmpf", [128, TC], F32)
    hdn2 = [T(f"hdn2_{i}", [128, TC], BF16) for i in range(2)]
    tmpg = [T(f"tmpg{i}", [128, TC], F32) for i in range(2)]
    lgh = T("lgh", [4, TC], F32)
    fwT = T("fwT", [128, 4], F32)
    mxc = T("mxc", [128, 1], F32)
    smc = T("smc", [128, 1], F32)
    fvt = T("fvt", [4, 128], F32)
    rcomp = T("rcomp", [1, 4, 4, TC], F32)
    o_sb = T("o_sb", [128, TC], F32)
    on_sb = T("on_sb", [128, 8, TC], BF16)
    s_sb = T("s_sb", [128, 8, TC], F32)
    row_m = T("row_m", [1, TC], F32)
    row_v = T("row_v", [1, TC], F32)
    row_t = T("row_t", [1, TC], F32)
    row_b = T("row_b", [1, TC], F32)
    row_a = T("row_a", [1, TC], F32)
    x3_sb = T("x3_sb", [128, TC], BF16)

    ps_r1 = P("ps_r1", [1, TC], F32)
    ps_r2 = P("ps_r2", [1, TC], F32)
    ps_b1 = P("ps_b1", [128, TC], F32)
    ps_b2 = P("ps_b2", [128, TC], F32)
    ps_lg = P("ps_lg", [4, TC], F32)
    ps_tr = P("ps_tr", [128, 128], F32)
    ps_mm = [P(f"ps_mm{i}", [128, TC], F32) for i in range(2)]

    em = Em(nc, es)
    block = es.enter_context(nc.Block())
    with es:
        em.dma("gpsimd", w1a[:], w1aT_in.rearrange("(k p) n -> p k n", p=128), writes=["w1a"], sem="wl")
        em.dma("gpsimd", w1b[:], w1bT_in, writes=["w1b"], sem="wl")
        em.dma("sync", fb1[:], fb1_in, writes=["fb1"], sem="wl2")
        em.dma("gpsimd", w2T[:], w2T_in.rearrange("(k p) n -> p k n", p=128), writes=["w2T"], sem="wl")
        em.dma("sync", fb2[:], fb2_in, writes=["fb2"], sem="wl2")
        em.dma("gpsimd", opT[:], opT_in.rearrange("(k p) n -> p k n", p=128), writes=["opT"], sem="wl")
        em.dma("sync", id_sb[:], id_in, writes=["id_sb"], sem="wl2")
        em.dma("sync", ones_sb[:], ones_in, writes=["ones_sb"], sem="wl2")
        em.dma("sync", onesb_sb[:], onesb_in, writes=["onesb_sb"], sem="wl2")
        em.dma("sync", onesr_sb[:], onesr_in, writes=["onesr_sb"], sem="wl2")
        em.dma("sync", rss_sb[:], rss_in, writes=["rss_sb"], sem="wl2")
        em.dma("sync", rsl_sb[:], rsl_in, writes=["rsl_sb"], sem="wl2")
        em.dma("sync", onw_sb[:], onw_in, writes=["onw_sb"], sem="wl2")
        em.dma("sync", lt_sb[:], lt_in, writes=["lt_sb"], sem="wl2")
        em.coalesce(["w1a", "w1b", "w2T", "opT"], "wl")
        em.coalesce(["fb1", "fb2", "id_sb", "ones_sb", "onesb_sb", "onesr_sb",
                     "rss_sb", "rsl_sb", "onw_sb", "lt_sb"], "wl2")
        em.op("scalar", lambda e: e.activation(out=it_sb[:], in_=lt_sb[:], func=AF.Abs),
              reads=["lt_sb"], writes=["it1"], tag="t1")
        em.op("scalar", lambda e: e.activation(out=it_sb[:], in_=it_sb[:], func=AF.Exp, scale=-1.0),
              reads=["it1"], writes=["it2"], tag="t2")
        em.op("scalar", lambda e: e.activation(out=it_sb[:], in_=it_sb[:], func=AF.Ln, bias=1.0),
              reads=["it2"], writes=["it3"], tag="t3")
        em.op("vector", lambda e: e.tensor_scalar_max(out=lt_sb[:], in0=lt_sb[:], scalar1=0.0),
              reads=["lt_sb"], writes=["lt_r", "lt_sb"], tag="t4")
        em.op("vector", lambda e: e.tensor_add(out=it_sb[:], in0=it_sb[:], in1=lt_sb[:]),
              reads=["it3", "lt_r"], writes=["it4"], tag="t5")
        em.op("vector", lambda e: e.tensor_scalar_add(out=it_sb[:], in0=it_sb[:], scalar1=1e-4),
              reads=["it4"], writes=["it5"], tag="t6")
        em.op("vector", lambda e: e.reciprocal(out=it_sb[:], in_=it_sb[:]),
              reads=["it5"], writes=["it6"], tag="t7")
        em.op("vector", lambda e: e.tensor_copy(out=idf[:], in_=id_sb[:]),
              reads=["id_sb"], writes=["idf"], tag="idf")
        em.op("tensor", lambda e: e.matmul(ps_b1[:, 0:1], onesr_sb[:], it_sb[:], start=True, stop=True),
              reads=["it6", "onesr_sb"], writes=["ps_b1"], tag="t8")
        em.op("vector", lambda e: e.tensor_copy(out=itb_sb[:], in_=ps_b1[:, 0:1]),
              reads=["ps_b1"], writes=["itb_sb"], tag="t9")

        names = ("ls", "ll", "do", "v")
        for c in range(NCH):
            t0 = c * TC
            for nm in ("x2", "ls", "ll", "do", "v"):
                src = {"x2": x2_in, "ls": ls_in, "ll": ll_in, "do": do_in, "v": v_in}[nm]
                for dc in range(8):
                    em.dma("sync", comp[nm][:, dc, :], rr(src)[dc, :, t0:t0 + TC],
                           writes=[f"{nm}_{dc}"], sem="comps", tag=nm + "ld")
            em.coalesce([f"{nm}_{dc}" for nm in ("x2", "ls", "ll", "do", "v") for dc in range(8)], "comps")
            for ti, nm in enumerate(names):
                for h in range(4):
                    for cc in (2 * h, 2 * h + 1):
                        em.op("tensor",
                              lambda e, nm=nm, cc=cc, h=h: e.matmul(
                                  ps_r1[:, :], onesb_sb[:], comp[nm][:, cc, :],
                                  start=(cc == 2 * h), stop=(cc == 2 * h + 1)),
                              reads=[f"{nm}_{cc}", "onesb_sb"], writes=["ps_r1"], tag="st_s")
                    for cc in (2 * h, 2 * h + 1):
                        em.op("scalar",
                              lambda e, nm=nm, cc=cc: e.square(out=sqf[:], in_=comp[nm][:, cc, :]),
                              reads=[f"{nm}_{cc}"], writes=["sqf"], tag="st_sq")
                        em.op("tensor",
                              lambda e, cc=cc, h=h: e.matmul(ps_r2[:, :], ones_sb[:], sqf[:],
                                                             start=(cc == 2 * h), stop=(cc == 2 * h + 1)),
                              reads=["sqf", "ones_sb"], writes=["ps_r2"], tag="st_sqm")
                    em.op("vector", lambda e: e.tensor_scalar_mul(out=row1[:], in0=ps_r1[:, :], scalar1=1.0 / 256),
                          reads=["ps_r1"], writes=["row1"], tag="st_m")
                    em.dma("gpsimd", statb[4 * ti:4 * ti + 1, h, :], row1[:],
                           reads=["row1"], writes=[f"stat{ti}0_{h}"], sem="st_r1", tag="strow0")
                    em.op("vector", lambda e: e.tensor_scalar_mul(out=row2[:], in0=ps_r2[:, :], scalar1=1.0 / 256),
                          reads=["ps_r2"], writes=["row2"], tag="st_msq")
                    em.op("vector", lambda e: e.tensor_mul(out=row_t[:], in0=row1[:], in1=row1[:]),
                          reads=["row1"], writes=["row_t"], tag="st_m2")
                    em.op("vector", lambda e: e.tensor_sub(out=row2[:], in0=row2[:], in1=row_t[:]),
                          reads=["row2", "row_t"], writes=["row2v", "row2"], tag="st_v")
                    em.dma("gpsimd", statb[4 * ti + 1:4 * ti + 2, h, :], row2[:],
                           reads=["row2v", "row2"], writes=[f"stat{ti}1_{h}"], sem="st_r2", tag="strow1")
                    em.op("scalar", lambda e: e.sqrt(out=row2[:], in_=ps_r2[:, :]),
                          reads=["ps_r2"], writes=["row2l", "row2"], tag="st_l2")
                    em.dma("gpsimd", statb[4 * ti + 3:4 * ti + 4, h, :], row2[:],
                           reads=["row2l", "row2"], writes=[f"stat{ti}3_{h}"], sem="st_r2", tag="strow3")
                    for cc in (2 * h, 2 * h + 1):
                        em.op("scalar",
                              lambda e, nm=nm, cc=cc: e.activation(out=sqf[:], in_=comp[nm][:, cc, :], func=AF.Abs),
                              reads=[f"{nm}_{cc}"], writes=["sqf"], tag="st_ab")
                        em.op("tensor",
                              lambda e, cc=cc, h=h: e.matmul(ps_r2[:, :], ones_sb[:], sqf[:],
                                                             start=(cc == 2 * h), stop=(cc == 2 * h + 1)),
                              reads=["sqf", "ones_sb"], writes=["ps_r2"], tag="st_abm")
                    em.op("vector", lambda e: e.tensor_scalar_mul(out=row2[:], in0=ps_r2[:, :], scalar1=1.0 / 256),
                          reads=["ps_r2"], writes=["row2a", "row2"], tag="st_am")
                    em.dma("gpsimd", statb[4 * ti + 2:4 * ti + 3, h, :], row2[:],
                           reads=["row2a", "row2"], writes=[f"stat{ti}2_{h}"], sem="st_r2", tag="strow2")
            em.coalesce([f"stat{ti}0_{h}" for ti in range(4) for h in range(4)], "st_r1")
            em.coalesce([f"stat{ti}{si}_{h}" for ti in range(4) for si in (1, 2, 3) for h in range(4)], "st_r2")
            for hc in range(8):
                pb = hc % 2
                for kc in range(8):
                    em.op("tensor",
                          lambda e, hc=hc, kc=kc, pb=pb: e.matmul(
                              ps_mm[pb][:, :], w1a[:, kc, hc * 128:(hc + 1) * 128], comp["x2"][:, kc, :],
                              start=(kc == 0), stop=(kc == 7)),
                          reads=["w1a", f"x2_{kc}"], writes=([f"ps_mm{pb}"] if kc in (0, 7) else []), tag="sh")
                em.op("vector",
                      lambda e, hc=hc, pb=pb: e.tensor_copy(out=shared[:, hc, :], in_=ps_mm[pb][:, :]),
                      reads=[f"ps_mm{pb}"], writes=[f"shared{hc}"], tag="shev")
            allstats = [f"stat{ti}{si}_{h}" for ti in range(4) for si in range(4) for h in range(4)]
            for h in range(4):
                for hc in range(8):
                    pb = hc % 2
                    em.op("tensor",
                          lambda e, h=h, hc=hc, pb=pb: e.matmul(
                              ps_mm[pb][:, :], w1b[:, hc * 128:(hc + 1) * 128], statb[:, h, :],
                              start=True, stop=True),
                          reads=allstats + ["w1b"], writes=[f"ps_mm{pb}"], tag="gb")
                    em.op("vector",
                          lambda e, hc=hc, pb=pb: e.tensor_add(out=tmpg[hc % 2][:], in0=ps_mm[pb][:, :],
                                                               in1=shared[:, hc, :]),
                          reads=[f"ps_mm{pb}", f"shared{hc}"], writes=[f"tmpg{hc % 2}"], tag="gadd")
                    em.op("scalar",
                          lambda e, hc=hc: e.activation(out=hdn2[hc % 2][:], in_=tmpg[hc % 2][:], func=AF.Gelu,
                                                        bias=fb1[:, hc:hc + 1]),
                          reads=[f"tmpg{hc % 2}", "fb1"], writes=[f"hdn2{hc % 2}"], tag="gelu")
                    em.op("tensor",
                          lambda e, hc=hc: e.matmul(ps_lg[:, :], w2T[:, hc, :], hdn2[hc % 2][:],
                                                    start=(hc == 0), stop=(hc == 7)),
                          reads=[f"hdn2{hc % 2}", "w2T"], writes=(["ps_lg"] if hc in (0, 7) else []), tag="lgm")
                em.op("scalar", lambda e: e.activation(out=lgh[:], in_=ps_lg[:, :], func=AF.Identity,
                                                       bias=fb2[:, 0:1]),
                      reads=["ps_lg", "fb2"], writes=["lgh"], tag="lgev")
                for tt in range(4):
                    em.op("tensor",
                          lambda e, tt=tt: e.transpose(ps_tr[:, 0:4], lgh[0:4, tt * 128:(tt + 1) * 128], idf[0:4, 0:4]),
                          reads=["lgh", "id_sb"], writes=["ps_tr"], tag="lgtr")
                    em.op("vector", lambda e: e.tensor_copy(out=fwT[:], in_=ps_tr[:, 0:4]),
                          reads=["ps_tr"], writes=["fwT"], tag="fwcp")
                    em.op("vector", lambda e: e.tensor_reduce(out=mxc[:], in_=fwT[:], axis=AX.X, op=ALU.max),
                          reads=["fwT"], writes=["mxc"], tag="fwmax")
                    em.op("vector", lambda e: e.tensor_scalar(out=fwT[:], in0=fwT[:], scalar1=mxc[:, 0:1],
                                                              scalar2=None, op0=ALU.subtract),
                          reads=["fwT", "mxc"], writes=["fwT"], tag="fwsub")
                    em.op("scalar", lambda e: e.activation(out=fwT[:], in_=fwT[:], func=AF.Exp,
                                                           scale=itb_sb[:, 0:1]),
                          reads=["fwT", "itb_sb"], writes=["fwT"], tag="fwexp")
                    em.op("vector", lambda e: e.tensor_reduce(out=smc[:], in_=fwT[:], axis=AX.X, op=ALU.add),
                          reads=["fwT"], writes=["smc"], tag="fwsum")
                    em.op("vector", lambda e: e.reciprocal(out=smc[:], in_=smc[:]),
                          reads=["smc"], writes=["smc"], tag="fwrec")
                    em.op("vector", lambda e: e.tensor_scalar(out=fwT[:], in0=fwT[:], scalar1=smc[:, 0:1],
                                                              scalar2=None, op0=ALU.mult),
                          reads=["fwT", "smc"], writes=["fwT"], tag="fwmul")
                    em.op("vector", lambda e: e.tensor_add(out=fwT[:, 0:1], in0=fwT[:, 0:1], in1=rss_sb[:]),
                          reads=["fwT", "rss_sb"], writes=["fwT"], tag="fwrss")
                    em.op("vector", lambda e: e.tensor_add(out=fwT[:, 1:2], in0=fwT[:, 1:2], in1=rsl_sb[:]),
                          reads=["fwT", "rsl_sb"], writes=["fwT"], tag="fwrsl")
                    em.op("vector", lambda e: e.tensor_copy(out=tmpf[:, 0:4], in_=fwT[:]),
                          reads=["fwT"], writes=["tmp4"], tag="fw2t")
                    em.op("tensor",
                          lambda e: e.transpose(ps_tr[0:4, 0:128], tmpf[:, 0:4], idf[:]),
                          reads=["tmp4", "idf"], writes=["ps_tr"], tag="fwtr")
                    em.op("scalar", lambda e: e.copy(out=fvt[:], in_=ps_tr[0:4, 0:128]),
                          reads=["ps_tr"], writes=["fvt"], tag="fvev")
                    for comp_i in range(4):
                        em.dma("scalar", rcomp[0:1, comp_i, h, tt * 128:(tt + 1) * 128],
                               fvt[comp_i:comp_i + 1, :], reads=["fvt"],
                               writes=[f"rc{comp_i}_{h}"], sem="rc", tag="rcst")
            em.coalesce([f"rc{ci}_{h}" for ci in range(4) for h in range(4)], "rc")
            for h in range(4):
                for cc in (2 * h, 2 * h + 1):
                    first = True
                    for comp_i, nm in enumerate(names):
                        em.op("tensor",
                              lambda e, comp_i=comp_i, h=h: e.matmul(
                                  ps_b2[:, :], onesr_sb[:], rcomp[0:1, comp_i, h, :],
                                  start=True, stop=True),
                              reads=[f"rc{comp_i}_{h}", "onesr_sb"], writes=["ps_b2"], tag="fbb")
                        if first:
                            em.op("vector",
                                  lambda e, nm=nm, cc=cc: e.tensor_mul(out=o_sb[:], in0=comp[nm][:, cc, :],
                                                                       in1=ps_b2[:, :]),
                                  reads=[f"{nm}_{cc}", "ps_b2"], writes=["o_sb"], tag="omul")
                            first = False
                        else:
                            em.op("vector",
                                  lambda e, nm=nm, cc=cc: e.tensor_mul(out=tmpf[:], in0=comp[nm][:, cc, :],
                                                                       in1=ps_b2[:, :]),
                                  reads=[f"{nm}_{cc}", "ps_b2"], writes=["tmpf"], tag="omul2")
                            em.op("vector",
                                  lambda e: e.tensor_add(out=o_sb[:], in0=o_sb[:], in1=tmpf[:]),
                                  reads=["o_sb", "tmpf"], writes=["o_sb"], tag="oacc")
                    em.op("scalar", lambda e, cc=cc: e.square(out=sqf[:], in_=o_sb[:]),
                          reads=["o_sb"], writes=["sqf"], tag="osq")
                    em.op("tensor",
                          lambda e, cc=cc, h=h: e.matmul(ps_r2[:, :], ones_sb[:], sqf[:],
                                                         start=(cc == 2 * h), stop=(cc == 2 * h + 1)),
                          reads=["sqf", "ones_sb"], writes=["ps_r2"], tag="osqm")
                    em.op("vector",
                          lambda e, cc=cc: e.tensor_copy(out=s_sb[:, cc, :], in_=o_sb[:]),
                          reads=["o_sb"], writes=[f"oh{cc}", f"s{cc}"], tag="okeep")
                em.op("vector", lambda e: e.tensor_scalar_mul(out=row1[:], in0=ps_r2[:, :], scalar1=1.0 / 256),
                      reads=["ps_r2"], writes=["row1"], tag="oms")
                em.op("vector", lambda e: e.tensor_scalar_add(out=row1[:], in0=row1[:], scalar1=1e-5),
                      reads=["row1"], writes=["row1"], tag="oeps")
                em.op("scalar", lambda e: e.activation(out=row1[:], in_=row1[:], func=AF.Ln),
                      reads=["row1"], writes=["row1"], tag="oln")
                em.op("scalar", lambda e: e.activation(out=row1[:], in_=row1[:], func=AF.Exp, scale=-0.5),
                      reads=["row1"], writes=["row1"], tag="oex")
                em.op("tensor", lambda e: e.matmul(ps_b2[:, :], onesr_sb[:], row1[:], start=True, stop=True),
                      reads=["row1", "onesr_sb"], writes=["ps_b2"], tag="obc")
                for cc in (2 * h, 2 * h + 1):
                    em.op("vector",
                          lambda e, cc=cc: e.scalar_tensor_tensor(
                              out=on_sb[:, cc, :], in0=s_sb[:, cc, :], scalar=onw_sb[:, cc % 2, 0:1],
                              in1=ps_b2[:, :], op0=ALU.mult, op1=ALU.mult),
                          reads=[f"oh{cc}", "onw_sb", "ps_b2"], writes=[f"on{cc}"], tag="onorm")
            for dc in range(8):
                pb = dc % 2
                for kc in range(8):
                    em.op("tensor",
                          lambda e, dc=dc, kc=kc, pb=pb: e.matmul(
                              ps_mm[pb][:, :], opT[:, kc, dc * 128:(dc + 1) * 128], on_sb[:, kc, :],
                              start=(kc == 0), stop=(kc == 7)),
                          reads=["opT", f"on{kc}"], writes=([f"ps_mm{pb}"] if kc in (0, 7) else []), tag="op")
                em.op("vector",
                      lambda e, dc=dc, pb=pb: e.tensor_add(out=s_sb[:, dc, :], in0=ps_mm[pb][:, :],
                                                           in1=comp["x2"][:, dc, :]),
                      reads=[f"ps_mm{pb}", f"x2_{dc}"], writes=[f"s{dc}"], tag="resadd")
            for dc in range(8):
                em.op("tensor",
                      lambda e, dc=dc: e.matmul(ps_r1[:, :], ones_sb[:], s_sb[:, dc, :],
                                                start=(dc == 0), stop=(dc == 7)),
                      reads=[f"s{dc}", "ones_sb"], writes=(["ps_r1"] if dc in (0, 7) else []), tag="lsum")
            for dc in range(8):
                em.op("scalar",
                      lambda e, dc=dc: e.square(out=sqf[:], in_=s_sb[:, dc, :]),
                      reads=[f"s{dc}"], writes=["sqf"], tag="lsq")
                em.op("tensor",
                      lambda e, dc=dc: e.matmul(ps_r2[:, :], ones_sb[:], sqf[:],
                                                start=(dc == 0), stop=(dc == 7)),
                      reads=["sqf", "ones_sb"], writes=(["ps_r2"] if dc in (0, 7) else []), tag="lsqm")
            em.op("vector", lambda e: e.tensor_scalar_mul(out=row_m[:], in0=ps_r1[:, :], scalar1=1.0 / D),
                  reads=["ps_r1"], writes=["row_m"], tag="lm")
            em.op("vector", lambda e: e.tensor_scalar_mul(out=row_v[:], in0=ps_r2[:, :], scalar1=1.0 / D),
                  reads=["ps_r2"], writes=["row_v"], tag="lv")
            em.op("vector", lambda e: e.tensor_mul(out=row_t[:], in0=row_m[:], in1=row_m[:]),
                  reads=["row_m"], writes=["row_t"], tag="lm2")
            em.op("vector", lambda e: e.tensor_sub(out=row_v[:], in0=row_v[:], in1=row_t[:]),
                  reads=["row_v", "row_t"], writes=["row_v"], tag="lvar")
            em.op("vector", lambda e: e.tensor_scalar_add(out=row_v[:], in0=row_v[:], scalar1=1e-5),
                  reads=["row_v"], writes=["row_v"], tag="leps")
            em.op("scalar", lambda e: e.activation(out=row_b[:], in_=row_v[:], func=AF.Ln),
                  reads=["row_v"], writes=["row_b"], tag="lln")
            em.op("scalar", lambda e: e.activation(out=row_b[:], in_=row_b[:], func=AF.Exp, scale=-0.5),
                  reads=["row_b"], writes=["row_b"], tag="lrstd")
            em.op("vector", lambda e: e.tensor_mul(out=row_a[:], in0=row_m[:], in1=row_b[:]),
                  reads=["row_m", "row_b"], writes=["row_a"], tag="la1")
            em.op("vector", lambda e: e.tensor_scalar_mul(out=row_a[:], in0=row_a[:], scalar1=-1.0),
                  reads=["row_a"], writes=["row_a"], tag="la2")
            em.op("tensor", lambda e: e.matmul(ps_b1[:, :], onesr_sb[:], row_b[:], start=True, stop=True),
                  reads=["row_b", "onesr_sb"], writes=["ps_b1"], tag="lbcb")
            em.op("tensor", lambda e: e.matmul(ps_b2[:, :], onesr_sb[:], row_a[:], start=True, stop=True),
                  reads=["row_a", "onesr_sb"], writes=["ps_b2"], tag="lbca")
            for dc in range(8):
                em.op("vector",
                      lambda e, dc=dc: e.tensor_mul(out=tmpf[:], in0=s_sb[:, dc, :], in1=ps_b1[:, :]),
                      reads=[f"s{dc}", "ps_b1"], writes=["tmpf"], tag="lapb")
                em.op("vector",
                      lambda e, dc=dc: e.tensor_add(out=x3_sb[:], in0=tmpf[:], in1=ps_b2[:, :]),
                      reads=["tmpf", "ps_b2"], writes=["x3_sb"], tag="lapa")
                em.dma("sync", rr(x3_out)[dc, :, t0:t0 + TC], x3_sb[:],
                       reads=["x3_sb"], tag="x3st")
        em.finish(block)
    return nc


def run_l1c(inputs, x2, l1a_out, do):
    fw1 = f32c(inputs["L1_attn_fuse_w1"])
    in_common = dict(
        w1aT=fw1[:, :D].T.copy(),
        w1bT=fw1[:, D:].T.copy(),
        fb1=f32c(inputs["L1_attn_fuse_b1"]).reshape(8, 128).T.copy(),
        w2T=f32c(inputs["L1_attn_fuse_w2"]).T.copy(),
        fb2=f32c(inputs["L1_attn_fuse_b2"]).reshape(4, 1),
        lt=f32c(inputs["L1_attn_logit_temp"]).reshape(1, 1),
        rss=np.broadcast_to(f32c(inputs["L1_attn_res_scale_short"]).reshape(1, 1), (128, 1)).copy(),
        rsl=np.broadcast_to(f32c(inputs["L1_attn_res_scale_long"]).reshape(1, 1), (128, 1)).copy(),
        onw=f32c(inputs["L1_attn_o_norm_w"]).reshape(2, 128, 1).transpose(1, 0, 2).copy(),
        opT=f32c(inputs["L1_attn_o_proj_w"]).T.copy(),
        identb=np.eye(128, dtype=np.float32).astype(BF),
        ones=np.ones((128, 1), np.float32),
        onesb=np.ones((128, 1), np.float32).astype(BF),
        onesr=np.ones((1, 128), np.float32),
    )
    in_maps = []
    for c in range(NCORES):
        b, sg = divmod(c, 4)
        t0 = sg * SEG
        m = dict(in_common)
        m.update(
            x2=bf16c(x2[b, :, t0:t0 + SEG]),
            ls=bf16c(l1a_out["ls"][b, :, t0:t0 + SEG]),
            ll=bf16c(l1a_out["ll"][b, :, t0:t0 + SEG]),
            do=bf16c(do[b, :, t0:t0 + SEG]),
            v=bf16c(l1a_out["v"][b, :, t0:t0 + SEG]),
        )
        in_maps.append(m)
    if "l1c" not in _BUILD_CACHE:
        _BUILD_CACHE["l1c"] = build_l1c()
    res = run_bass_kernel_spmd(_BUILD_CACHE["l1c"], in_maps, core_ids=list(range(NCORES)))
    x3 = np.empty((B, D, L), BF)
    for c in range(NCORES):
        b, sg = divmod(c, 4)
        x3[b, :, sg * SEG:(sg + 1) * SEG] = res.results[c]["x3"]
    return x3


def build_l1f():
    nc = bass.Bass("TRN2", target_bir_lowering=False)
    TC = 512
    NCH = SEG // TC

    x3_in = nc.dram_tensor("x3", [D, SEG], BF16, kind="ExternalInput").ap()
    xp_in = nc.dram_tensor("xp", [128, 8, 1], BF16, kind="ExternalInput").ap()
    wrT_in = nc.dram_tensor("wrT", [D, D], F32, kind="ExternalInput").ap()
    wkT_in = nc.dram_tensor("wkT", [D, D], F32, kind="ExternalInput").ap()
    wvT_in = nc.dram_tensor("wvT", [D, D], F32, kind="ExternalInput").ap()
    mur_in = nc.dram_tensor("mur", [128, 8, 1], F32, kind="ExternalInput").ap()
    muk_in = nc.dram_tensor("muk", [128, 8, 1], F32, kind="ExternalInput").ap()
    ones_in = nc.dram_tensor("ones", [128, 1], F32, kind="ExternalInput").ap()
    onesr_in = nc.dram_tensor("onesr", [1, 128], F32, kind="ExternalInput").ap()

    x4_out = nc.dram_tensor("x4", [D, SEG], F32, kind="ExternalOutput").ap()

    es = ExitStack()
    T = lambda *a: es.enter_context(nc.sbuf_tensor(*a))
    P = lambda *a: es.enter_context(nc.psum_tensor(*a))

    wr = T("wr_s", [128, 8, D], BF16)
    wk = T("wk_s", [128, 8, D], BF16)
    wv = T("wv_s", [128, 8, D], BF16)
    mur = T("mur_s", [128, 8, 1], F32)
    muk = T("muk_s", [128, 8, 1], F32)
    ones_sb = T("ones_sb", [128, 1], F32)
    onesr_sb = T("onesr_sb", [1, 128], F32)
    ext = T("ext", [128, 8, 1 + TC], BF16)
    dcol = T("dcol", [128, TC], BF16)
    xr_sb = T("xr_sb", [128, 8, TC], BF16)
    xk_sb = T("xk_sb", [128, 8, TC], BF16)
    r_sb = T("r_sb", [128, 8, TC], BF16)
    k2_sb = T("k2_sb", [128, 8, TC], BF16)
    ka = T("ka", [128, TC], F32)
    tmpf = T("tmpf", [128, TC], F32)
    sqf = T("sqf", [128, TC], F32)
    sqf2 = T("sqf2", [128, TC], F32)
    s_sb = T("s_sb", [128, 8, TC], F32)
    row_m = T("row_m", [1, TC], F32)
    row_v = T("row_v", [1, TC], F32)
    row_t = T("row_t", [1, TC], F32)
    row_b = T("row_b", [1, TC], F32)
    row_a = T("row_a", [1, TC], F32)
    x4_sb = T("x4_sb", [128, TC], F32)
    onesb_sb = T("onesb_sb", [128, 1], BF16)

    ps_mm = [P(f"ps_mm{i}", [128, TC], F32) for i in range(2)]
    ps_r1 = P("ps_r1", [1, TC], F32)
    ps_r2 = P("ps_r2", [1, TC], F32)
    ps_a = P("ps_a", [128, TC], F32)
    ps_b = P("ps_b", [128, TC], F32)

    em = Em(nc, es)
    block = es.enter_context(nc.Block())
    with es:
        em.dma("gpsimd", wr[:], wrT_in.rearrange("(k p) n -> p k n", p=128), writes=["wr"], sem="wl")
        em.dma("gpsimd", wk[:], wkT_in.rearrange("(k p) n -> p k n", p=128), writes=["wk"], sem="wl")
        em.dma("gpsimd", wv[:], wvT_in.rearrange("(k p) n -> p k n", p=128), writes=["wv"], sem="wl")
        em.dma("sync", mur[:], mur_in, writes=["mur"], sem="wl2")
        em.dma("sync", muk[:], muk_in, writes=["muk"], sem="wl2")
        em.dma("sync", ones_sb[:], ones_in, writes=["ones_sb"], sem="wl2")
        em.dma("sync", onesr_sb[:], onesr_in, writes=["onesr_sb"], sem="wl2")
        em.dma("sync", ext[:, :, 0:1], xp_in, writes=[f"extc{dc}" for dc in range(8)], sem="wl2",
               allow_slow_non_contiguous=True)
        em.coalesce(["wr", "wk", "wv"], "wl")
        em.coalesce(["mur", "muk", "ones_sb", "onesr_sb"] + [f"extc{dc}" for dc in range(8)], "wl2")
        em.op("vector", lambda e: e.memset(onesb_sb[:], 1.0), writes=["onesb_sb"], tag="ob")

        for c in range(NCH):
            t0 = c * TC
            for dc in range(8):
                em.dma("sync", ext[:, dc, 1:], rr(x3_in)[dc, :, t0:t0 + TC],
                       reads=[f"extc{dc}"], writes=[f"ext{dc}"], sem="xl", tag="xld")
            em.coalesce([f"ext{dc}" for dc in range(8)], "xl")
            for dc in range(8):
                em.op("vector",
                      lambda e, dc=dc: e.tensor_sub(out=dcol[:], in0=ext[:, dc, 1:], in1=ext[:, dc, 0:TC]),
                      reads=[f"ext{dc}", f"extc{dc}"], writes=["dcol"], tag="d")
                em.op("vector",
                      lambda e, dc=dc: e.scalar_tensor_tensor(
                          out=xr_sb[:, dc, :], in0=dcol[:], scalar=mur[:, dc, 0:1],
                          in1=ext[:, dc, 0:TC], op0=ALU.mult, op1=ALU.add),
                      reads=["dcol", "mur", f"ext{dc}"], writes=[f"xr{dc}"], tag="xr")
                em.op("vector",
                      lambda e, dc=dc: e.scalar_tensor_tensor(
                          out=xk_sb[:, dc, :], in0=dcol[:], scalar=muk[:, dc, 0:1],
                          in1=ext[:, dc, 0:TC], op0=ALU.mult, op1=ALU.add),
                      reads=["dcol", "muk", f"ext{dc}"], writes=[f"xk{dc}"], tag="xk")
                em.op("vector",
                      lambda e, dc=dc: e.tensor_copy(out=ext[:, dc, 0:1], in_=ext[:, dc, TC:TC + 1]),
                      reads=[f"ext{dc}", f"xr{dc}", f"xk{dc}"], writes=[f"extc{dc}"], tag="tail")
            for dc in range(8):
                pb = dc % 2
                for kc in range(8):
                    em.op("tensor",
                          lambda e, dc=dc, kc=kc, pb=pb: e.matmul(
                              ps_mm[pb][:, :], wr[:, kc, dc * 128:(dc + 1) * 128], xr_sb[:, kc, :],
                              start=(kc == 0), stop=(kc == 7)),
                          reads=["wr", f"xr{kc}"], writes=([f"ps_mm{pb}"] if kc in (0, 7) else []), tag="rmm")
                em.op("scalar",
                      lambda e, dc=dc, pb=pb: e.activation(out=r_sb[:, dc, :], in_=ps_mm[pb][:, :],
                                                           func=AF.Sigmoid),
                      reads=[f"ps_mm{pb}"], writes=[f"r{dc}"], tag="rsig")
            for dc in range(8):
                pb = dc % 2
                for kc in range(8):
                    em.op("tensor",
                          lambda e, dc=dc, kc=kc, pb=pb: e.matmul(
                              ps_mm[pb][:, :], wk[:, kc, dc * 128:(dc + 1) * 128], xk_sb[:, kc, :],
                              start=(kc == 0), stop=(kc == 7)),
                          reads=["wk", f"xk{kc}"], writes=([f"ps_mm{pb}"] if kc in (0, 7) else []), tag="kmm")
                em.op("scalar",
                      lambda e, dc=dc, pb=pb: e.activation(out=ka[:], in_=ps_mm[pb][:, :], func=AF.Relu),
                      reads=[f"ps_mm{pb}"], writes=["ka"], tag="krelu")
                em.op("scalar",
                      lambda e, dc=dc: e.square(out=k2_sb[:, dc, :], in_=ka[:]),
                      reads=["ka"], writes=[f"k2{dc}"], tag="ksq")
            for dc in range(8):
                pb = dc % 2
                for kc in range(8):
                    em.op("tensor",
                          lambda e, dc=dc, kc=kc, pb=pb: e.matmul(
                              ps_mm[pb][:, :], wv[:, kc, dc * 128:(dc + 1) * 128], k2_sb[:, kc, :],
                              start=(kc == 0), stop=(kc == 7)),
                          reads=["wv", f"k2{kc}"], writes=([f"ps_mm{pb}"] if kc in (0, 7) else []), tag="vmm")
                em.op("vector",
                      lambda e, dc=dc, pb=pb: e.tensor_mul(out=tmpf[:], in0=r_sb[:, dc, :], in1=ps_mm[pb][:, :]),
                      reads=[f"r{dc}", f"ps_mm{pb}"], writes=["tmpf"], tag="rkv")
                em.op("vector",
                      lambda e, dc=dc: e.tensor_add(out=s_sb[:, dc, :], in0=tmpf[:], in1=ext[:, dc, 1:]),
                      reads=["tmpf", f"ext{dc}"], writes=[f"s{dc}"], tag="res")
            for dc in range(8):
                em.op("tensor",
                      lambda e, dc=dc: e.matmul(ps_r1[:, :], ones_sb[:], s_sb[:, dc, :],
                                                start=(dc == 0), stop=(dc == 7)),
                      reads=[f"s{dc}", "ones_sb"], writes=(["ps_r1"] if dc in (0, 7) else []), tag="lsum")
            for dc in range(8):
                em.op("scalar",
                      lambda e, dc=dc: e.square(out=sqf[:], in_=s_sb[:, dc, :]),
                      reads=[f"s{dc}"], writes=["sqf"], tag="lsq")
                em.op("tensor",
                      lambda e, dc=dc: e.matmul(ps_r2[:, :], ones_sb[:], sqf[:],
                                                start=(dc == 0), stop=(dc == 7)),
                      reads=["sqf", "ones_sb"], writes=(["ps_r2"] if dc in (0, 7) else []), tag="lsqm")
            em.op("vector", lambda e: e.tensor_scalar_mul(out=row_m[:], in0=ps_r1[:, :], scalar1=1.0 / D),
                  reads=["ps_r1"], writes=["row_m"], tag="lm")
            em.op("vector", lambda e: e.tensor_scalar_mul(out=row_v[:], in0=ps_r2[:, :], scalar1=1.0 / D),
                  reads=["ps_r2"], writes=["row_v"], tag="lv")
            em.op("vector", lambda e: e.tensor_mul(out=row_t[:], in0=row_m[:], in1=row_m[:]),
                  reads=["row_m"], writes=["row_t"], tag="lm2")
            em.op("vector", lambda e: e.tensor_sub(out=row_v[:], in0=row_v[:], in1=row_t[:]),
                  reads=["row_v", "row_t"], writes=["row_v"], tag="lvar")
            em.op("vector", lambda e: e.tensor_scalar_add(out=row_v[:], in0=row_v[:], scalar1=1e-5),
                  reads=["row_v"], writes=["row_v"], tag="leps")
            em.op("scalar", lambda e: e.activation(out=row_b[:], in_=row_v[:], func=AF.Ln),
                  reads=["row_v"], writes=["row_b"], tag="lln")
            em.op("scalar", lambda e: e.activation(out=row_b[:], in_=row_b[:], func=AF.Exp, scale=-0.5),
                  reads=["row_b"], writes=["row_b"], tag="lrstd")
            em.op("vector", lambda e: e.tensor_mul(out=row_a[:], in0=row_m[:], in1=row_b[:]),
                  reads=["row_m", "row_b"], writes=["row_a"], tag="la1")
            em.op("vector", lambda e: e.tensor_scalar_mul(out=row_a[:], in0=row_a[:], scalar1=-1.0),
                  reads=["row_a"], writes=["row_a"], tag="la2")
            em.op("tensor", lambda e: e.matmul(ps_b[:, :], onesr_sb[:], row_b[:], start=True, stop=True),
                  reads=["row_b", "onesr_sb"], writes=["ps_b"], tag="lbcb")
            em.op("tensor", lambda e: e.matmul(ps_a[:, :], onesr_sb[:], row_a[:], start=True, stop=True),
                  reads=["row_a", "onesr_sb"], writes=["ps_a"], tag="lbca")
            for dc in range(8):
                em.op("vector",
                      lambda e, dc=dc: e.tensor_mul(out=x4_sb[:], in0=s_sb[:, dc, :], in1=ps_b[:, :]),
                      reads=[f"s{dc}", "ps_b"], writes=["x4_sb"], tag="lapb")
                em.op("vector",
                      lambda e, dc=dc: e.tensor_add(out=x4_sb[:], in0=x4_sb[:], in1=ps_a[:, :]),
                      reads=["x4_sb", "ps_a"], writes=["x4_sb"], tag="lapa")
                em.dma("sync", rr(x4_out)[dc, :, t0:t0 + TC], x4_sb[:],
                       reads=["x4_sb"], sem="x4s", tag="x4st")
        em.finish(block)
    return nc


def run_l1f(inputs, x3):
    in_common = dict(
        wrT=f32c(inputs["L1_ffn_W_r"]).T.copy(),
        wkT=f32c(inputs["L1_ffn_W_k"]).T.copy(),
        wvT=f32c(inputs["L1_ffn_W_v"]).T.copy(),
        mur=f32c(inputs["L1_ffn_mu_r"]).reshape(8, 128, 1).transpose(1, 0, 2).copy(),
        muk=f32c(inputs["L1_ffn_mu_k"]).reshape(8, 128, 1).transpose(1, 0, 2).copy(),
        ones=np.ones((128, 1), np.float32),
        onesr=np.ones((1, 128), np.float32),
    )
    in_maps = []
    for c in range(NCORES):
        b, sg = divmod(c, 4)
        t0 = sg * SEG
        xp = np.zeros((128, 8, 1), BF)
        if sg > 0:
            xp = np.ascontiguousarray(x3[b, :, t0 - 1].reshape(8, 128, 1).transpose(1, 0, 2))
        m = dict(in_common)
        m.update(x3=bf16c(x3[b, :, t0:t0 + SEG]), xp=xp)
        in_maps.append(m)
    if "l1f" not in _BUILD_CACHE:
        _BUILD_CACHE["l1f"] = build_l1f()
    res = run_bass_kernel_spmd(_BUILD_CACHE["l1f"], in_maps, core_ids=list(range(NCORES)))
    x4 = np.empty((B, D, L), np.float32)
    for c in range(NCORES):
        b, sg = divmod(c, 4)
        x4[b, :, sg * SEG:(sg + 1) * SEG] = res.results[c]["x4"]
    return x4


def kernel(**inputs):
    l0a = run_l0a(inputs)
    y = run_l0scan(l0a, inputs)
    x1 = run_l0b(inputs, l0a, y)
    x2 = run_l0f(inputs, x1)
    l1a = run_l1a(inputs, x2)
    do = run_l1b(l1a)
    x3 = run_l1c(inputs, x2, l1a, do)
    x4 = run_l1f(inputs, x3)
    return np.ascontiguousarray(x4.transpose(0, 2, 1))
